# version 02
# baseline (speedup 1.0000x reference)
import math
import numpy as np
import concourse.bass as bass
import concourse.mybir as mybir
from concourse.bass_utils import run_bass_kernel_spmd

F32 = mybir.dt.float32
BF16 = mybir.dt.bfloat16
AF = mybir.ActivationFunctionType
ALU = mybir.AluOpType
AX = mybir.AxisListType

D = 1024
SEQ = 2048
BPC = 2
NCORES = 8
H = 16
DH = 64
C_L0 = 28.0
C_L1 = 50.0

class Rec:
    __slots__ = ("meth", "a", "kw", "waits", "incs", "auto_inc", "auto_waits",
                 "hz_val", "_writes", "post")

    def __init__(self, meth, a, kw, waits, incs):
        self.meth = meth
        self.a = a
        self.kw = kw
        self.waits = list(waits)
        self.incs = list(incs)
        self.auto_inc = False
        self.auto_waits = []
        self.hz_val = 0
        self._writes = set()
        self.post = []


def _ap_name(x):
    try:
        return x.tensor.name
    except Exception:
        return None


class Em:
    ENGINES = ("sync", "gpsimd", "tensor", "vector", "scalar")
    AUTO = ("vector", "scalar")
    AUTO_DEPTH = 3

    def __init__(self, nc):
        self.nc = nc
        self.q = {e: [] for e in self.ENGINES}
        self.cnt = {}
        self.auto_sem = {e: self.sem(f"s_auto_{e}") for e in self.AUTO}

    def sem(self, name):
        h = self.nc.alloc_semaphore(name)
        self.cnt[h] = 0
        return h

    def val(self, s):
        return self.cnt[s]

    def emit(self, eng, meth, *a, waits=(), inc=None, incby=1, **kw):
        if inc is not None:
            self.cnt[inc] += incby
        rec = Rec(meth, a, kw, waits, [(inc, incby)] if inc is not None else [])
        if inc is not None:
            rec.post = [(inc, self.cnt[inc])]
        if eng in self.AUTO and meth != "memset":
            writes = set()
            reads = set()
            names = [_ap_name(x) for x in a]
            if kw.get("out") is not None:
                writes.add(_ap_name(kw["out"]))
                reads.update(n for n in (_ap_name(kw.get("in_")),) if n)
            else:
                if names:
                    writes.add(names[0])
                reads.update(n for n in names[1:] if n)
            for k2 in ("bias", "scale", "scalar1", "scalar2", "in_", "in_values",
                       "in_max", "in_to_replace"):
                n = _ap_name(kw.get(k2))
                if n:
                    reads.add(n)
            if kw.get("accum_out") is not None:
                writes.add(_ap_name(kw["accum_out"]))
            writes.discard(None); reads.discard(None)
            prev = [r for r in self.q[eng][-self.AUTO_DEPTH:] if isinstance(r, Rec)]
            for pr in prev:
                if pr._writes and (pr._writes & reads):
                    if pr.post:
                        rec.waits.append(pr.post[0])
                    else:
                        pr.auto_inc = True
                        rec.auto_waits.append(pr)
            rec._writes = writes
        self.q[eng].append(rec)
        return rec

    def wait(self, eng, sem, v=None):
        v = self.cnt[sem] if v is None else v
        r = Rec(None, None, None, [(sem, v)], [])
        r._writes = set()
        self.q[eng].append(r)

    def finalize(self):
        for e in self.AUTO:
            c = 0
            for r in self.q[e]:
                if isinstance(r, Rec) and r.auto_inc:
                    c += 1
                    r.hz_val = c
        with self.nc.allow_low_precision("bf16 outputs within validated end-to-end tolerance"), \
             self.nc.Block() as blk:
            for e in self.ENGINES:
                if self.q[e]:
                    def run_all(eng_obj, _e=e):
                        hsem = self.auto_sem.get(_e)
                        for r in self.q[_e]:
                            if not isinstance(r, Rec):
                                r(eng_obj)
                                continue
                            for (s, v) in r.waits:
                                eng_obj.wait_ge(s, v)
                            for pr in r.auto_waits:
                                eng_obj.wait_ge(hsem, pr.hz_val)
                            if r.meth is None:
                                continue
                            inst = getattr(eng_obj, r.meth)(*r.a, **r.kw)
                            for (s, by) in r.incs:
                                inst.then_inc(s, by)
                            if r.auto_inc:
                                inst.then_inc(hsem, 1)
                    getattr(blk, e)(run_all)


def new_nc():
    return bass.Bass("TRN2", target_bir_lowering=False, detect_race_conditions=False)


def build_ln(T, in_dtype=F32):
    NT = T // 128
    nc = new_nc()
    x = nc.dram_tensor("x", [T, D], in_dtype, kind="ExternalInput").ap()
    ident_in = nc.dram_tensor("ident", [128, 128], F32, kind="ExternalInput").ap()
    xn_out = nc.dram_tensor("xn", [T, D], BF16, kind="ExternalOutput").ap()
    xnfm_out = nc.dram_tensor("xn_fm", [D, T], BF16, kind="ExternalOutput").ap()
    stats_out = nc.dram_tensor("stats", [128, 2 * NT], F32, kind="ExternalOutput").ap()

    E = Em(nc)
    sb = lambda n, shp, dt=F32: nc.alloc_sbuf_tensor(n, shp, dt).ap()

    NSLOT = 4
    xslot = sb("xslot", [128, NSLOT * D], BF16)
    xslot32 = sb("xslot32", [128, 2 * D])
    xn_sb = sb("xn_sb", [128, NT * D], BF16)
    xnfm_sb = sb("xnfm_sb", [128, 8 * T], BF16)
    stats = sb("stats_sb", [128, 2 * NT])
    identb = sb("identb", [128, 128], BF16)
    acc = sb("acc", [128, 1])
    musq = sb("musq", [128, 1])
    sq = sb("sq", [128, 2])
    var = sb("var", [128, 2])
    lntmp = sb("lntmp", [128, 1])
    junk = sb("junk", [128, D])

    ps_tr = [nc.alloc_psum_tensor(f"ps_tr{i}", [128, 128], BF16).ap() for i in range(2)]

    s_x = E.sem("s_x"); s_id = E.sem("s_id"); s_ld32 = E.sem("s_ld32")
    s_sq = E.sem("s_sq"); s_var = E.sem("s_var"); s_rs = E.sem("s_rs")
    s_xn = E.sem("s_xn"); s_tr = E.sem("s_tr"); s_ev = E.sem("s_ev")
    s_st = E.sem("s_st")

    E.emit("gpsimd", "dma_start", identb[:], ident_in, inc=s_id, incby=16)

    for i in range(NT):
        slot = xslot[:, (i % NSLOT) * D:(i % NSLOT + 1) * D]
        slot32 = xslot32[:, (i % 2) * D:(i % 2 + 1) * D]
        w = [] if i < NSLOT else [(s_xn, i - NSLOT + 1)]
        E.emit("gpsimd", "dma_start", slot, x[i * 128:(i + 1) * 128, :],
               waits=w, inc=s_x, incby=16)
        w32 = [] if i < 2 else [(s_var, i - 1)]
        E.emit("gpsimd", "dma_start", slot32, x[i * 128:(i + 1) * 128, :],
               waits=w32, inc=s_x, incby=16)
        mu = stats[:, 2 * i:2 * i + 1]
        rstd = stats[:, 2 * i + 1:2 * i + 2]
        E.emit("vector", "reduce_sum", acc[:], slot32, AX.X,
               waits=[(s_x, 32 * (i + 1))])
        E.emit("vector", "tensor_scalar_mul", mu, acc[:], 1.0 / D)
        E.emit("vector", "tensor_tensor", musq[:], mu, mu, ALU.mult)
        E.emit("vector", "tensor_scalar_add", musq[:], musq[:], -1e-5)
        E.emit("vector", "tensor_tensor", junk[:], slot32, slot32, ALU.mult)
        E.emit("vector", "reduce_sum", sq[:, i % 2:i % 2 + 1], junk[:], AX.X,
               inc=s_sq)
        E.emit("vector", "scalar_tensor_tensor", var[:, i % 2:i % 2 + 1],
               sq[:, i % 2:i % 2 + 1], 1.0 / D, musq[:], ALU.mult, ALU.subtract,
               waits=[(s_sq, i + 1), (s_rs, max(0, i - 1))], inc=s_var)
        E.emit("scalar", "activation", lntmp[:], var[:, i % 2:i % 2 + 1], AF.Ln,
               waits=[(s_var, i + 1)])
        E.emit("scalar", "activation", rstd, lntmp[:], AF.Exp, scale=-0.5, inc=s_rs)
        E.emit("vector", "tensor_scalar", xn_sb[:, i * D:(i + 1) * D], slot,
               mu, rstd, ALU.subtract, ALU.mult,
               waits=[(s_rs, i + 1), (s_x, 32 * (i + 1))], inc=s_xn)
        for c in range(8):
            k = 8 * i + c
            w = [(s_xn, i + 1), (s_id, 16)] if c == 0 else []
            if k >= 2:
                w = list(w) + [(s_ev, k - 1)]
            E.emit("tensor", "transpose", ps_tr[k % 2][:],
                   xn_sb[:, i * D + c * 128: i * D + (c + 1) * 128], identb[:],
                   waits=w, inc=s_tr)
            E.emit("vector", "tensor_copy",
                   xnfm_sb[:, c * T + i * 128: c * T + (i + 1) * 128], ps_tr[k % 2][:],
                   waits=[(s_tr, k + 1)], inc=s_ev)
        E.emit("sync", "dma_start", xn_out[i * 128:(i + 1) * 128, :],
               xn_sb[:, i * D:(i + 1) * D], waits=[(s_xn, i + 1)], inc=s_st, incby=16)

    for c in range(8):
        E.emit("sync", "dma_start", xnfm_out[c * 128:(c + 1) * 128, :],
               xnfm_sb[:, c * T:(c + 1) * T], waits=[(s_ev, 8 * NT)], inc=s_st, incby=16)
    E.emit("sync", "dma_start", stats_out, stats[:], waits=[(s_rs, NT)], inc=s_st, incby=16)
    E.wait("sync", s_st)
    E.finalize()
    return nc


class Rot:
    def __init__(self, n):
        self.n = n
        self.inst = 0
        self.freed = {}

    def acquire(self):
        i = self.inst
        self.inst += 1
        w = self.freed.get(i - self.n)
        return i % self.n, ([w] if w else []), i

    def release(self, i, sem, cnt):
        self.freed[i] = (sem, cnt)


def build_l0_attn(T):
    S = T // BPC
    NG = S // 512
    NKT = S // 128
    nc = new_nc()
    xnfm_in = nc.dram_tensor("xn_fm", [D, T], BF16, kind="ExternalInput").ap()
    x_in = nc.dram_tensor("x", [T, D], F32, kind="ExternalInput").ap()
    wts = {n: nc.dram_tensor(n, [D, D], F32, kind="ExternalInput").ap()
           for n in ("WqT", "WkT", "WvT", "WoT")}
    perm_in = nc.dram_tensor("perm", [128, 128], F32, kind="ExternalInput").ap()
    selb_in = nc.dram_tensor("selb", [1, 256], F32, kind="ExternalInput").ap()
    dmask_in = nc.dram_tensor("dmask", [128, 128], F32, kind="ExternalInput").ap()
    cos_in = nc.dram_tensor("rope_cos", [128, S], F32, kind="ExternalInput").ap()
    ssin_in = nc.dram_tensor("rope_ssin", [128, S], F32, kind="ExternalInput").ap()
    tmp1_out = nc.dram_tensor("tmp1", [T, D], BF16, kind="ExternalOutput").ap()

    E = Em(nc)
    sb = lambda n, shp, dt=F32: nc.alloc_sbuf_tensor(n, shp, dt).ap()

    xnfm = sb("xnfm", [128, 8 * S], BF16)
    zt = xnfm
    qt = sb("qt", [128, 8 * S], BF16)
    kt = sb("kt", [128, 8 * S], BF16)
    VW = 65 * H
    vpad = sb("vpad", [128, NKT * VW], BF16)
    wb = [sb(f"wb{j}", [128, 8 * D], BF16) for j in range(2)]
    cosb = sb("cosb", [128, S], BF16)
    ssinb = sb("ssinb", [128, S], BF16)
    permb = sb("permb", [128, 128], BF16)
    selbb = sb("selbb", [1, 256], BF16)
    dmaskb = sb("dmaskb", [128, 128], BF16)
    expb = [sb(f"expb{j}", [128, 512], BF16) for j in range(2)]
    rab = [sb("rab0", [1, S], BF16), sb("rab1", [1, S], BF16)]
    rt1 = sb("rt1", [128, 512])
    rt2 = sb("rt2", [128, 512])
    negC = sb("negC", [128, 1])
    xres = sb("xres", [128, 4 * D], BF16)
    osb = sb("osb", [128, 4 * D], BF16)

    ps_a = [nc.alloc_psum_tensor(f"ps_a{j}", [128, 512], F32).ap() for j in range(2)]
    ps_b = [nc.alloc_psum_tensor(f"ps_b{j}", [128, 512], F32).ap() for j in range(2)]
    ps_c = nc.alloc_psum_tensor("ps_c", [128, 512], F32).ap()

    s_w = E.sem("s_w"); s_ld = E.sem("s_ld")
    s_p1 = E.sem("s_p1"); s_ev1 = E.sem("s_ev1")
    s_rpe = E.sem("s_rpe"); s_rv = E.sem("s_rv")
    s_sc = E.sem("s_sc"); s_exp = E.sem("s_exp"); s_pv = E.sem("s_pv")
    s_rec = E.sem("s_rec"); s_zev = E.sem("s_zev")
    s_nb = E.sem("s_nb"); s_nrm = E.sem("s_nrm")
    s_res = E.sem("s_res"); s_st = E.sem("s_st"); s_prep = E.sem("s_prep")

    rpa = Rot(2)
    rexp = Rot(2)
    rpb = Rot(2)
    rpc = Rot(1)
    rxr = Rot(4)
    ros = Rot(4)
    rwb = Rot(2)

    E.emit("gpsimd", "dma_start", permb[:], perm_in, inc=s_w, incby=16)
    E.emit("gpsimd", "dma_start", selbb[:], selb_in, inc=s_w, incby=16)
    E.emit("gpsimd", "dma_start", dmaskb[:], dmask_in, inc=s_w, incby=16)
    E.emit("gpsimd", "dma_start", cosb[:], cos_in, inc=s_w, incby=16)
    E.emit("gpsimd", "dma_start", ssinb[:], ssin_in, inc=s_w, incby=16)
    E.emit("vector", "memset", negC[:], -C_L0, inc=s_prep)
    CONSTS_W = E.val(s_w)

    def kloop(psum, lhsT_slices, rhs_slices, waits_first, inc):
        for p in range(8):
            w = waits_first if p == 0 else ()
            E.emit("tensor", "matmul", psum, lhsT_slices[p], rhs_slices[p],
                   start=(p == 0), stop=(p == 7), waits=w,
                   inc=(inc if p == 7 else None))

    wslice = lambda j, p, a, b: wb[j][:, 1024 * p + a: 1024 * p + b]

    def load_weight(name):
        j, waits, inst = rwb.acquire()
        E.emit("gpsimd", "dma_start", wb[j][:].rearrange("p (c n) -> p c n", c=8),
               wts[name].rearrange("(c p) n -> p c n", p=128),
               waits=waits, inc=s_w, incby=16)
        return j, inst, E.val(s_w)

    res_total = 0
    for b in range(BPC):
        xoff = b * S
        for p in range(8):
            E.emit("sync", "dma_start", xnfm[:, S * p:S * (p + 1)],
                   xnfm_in[128 * p:128 * (p + 1), xoff:xoff + S],
                   waits=[(s_res, res_total)], inc=s_ld, incby=16)
        ld_done = E.val(s_ld)

        jq, iq, wq_ld = load_weight("WqT")
        jk, ik, wk_ld = load_weight("WkT")

        def proj_fm(dst, wj, w_ready):
            for m in range(8):
                for n in range(NG):
                    j, bw, inst = rpa.acquire()
                    waits = [(s_ld, ld_done), (s_w, w_ready)] + bw
                    kloop(ps_a[j][:],
                          [wslice(wj, p, 128 * m, 128 * (m + 1)) for p in range(8)],
                          [xnfm[:, S * p + 512 * n: S * p + 512 * (n + 1)] for p in range(8)],
                          waits, s_p1)
                    E.emit("vector", "tensor_copy",
                           dst[:, S * m + 512 * n: S * m + 512 * (n + 1)], ps_a[j][:],
                           waits=[(s_p1, E.val(s_p1))], inc=s_ev1)
                    rpa.release(inst, s_ev1, E.val(s_ev1))
        proj_fm(qt, jq, wq_ld)
        p1_q_done = E.val(s_p1)
        rwb.release(iq, s_p1, p1_q_done)
        proj_fm(kt, jk, wk_ld)
        p1_k_done = E.val(s_p1)
        rwb.release(ik, s_p1, p1_k_done)

        jv, iv, wv_ld = load_weight("WvT")

        for i in range(NKT):
            for half in range(2):
                j, bw, inst = rpa.acquire()
                waits = [(s_w, wv_ld)] + bw
                kloop(ps_a[j][:],
                      [xnfm[:, S * p + 128 * i: S * p + 128 * (i + 1)] for p in range(8)],
                      [wslice(jv, p, 512 * half, 512 * (half + 1)) for p in range(8)],
                      waits, s_p1)
                dst = bass.AP(tensor=vpad.tensor, offset=i * VW + half * 520,
                              ap=[[NKT * VW, 128], [65, 8], [1, 64]])
                E.emit("vector", "tensor_copy", dst,
                       ps_a[j][:].rearrange("p (h e) -> p h e", h=8),
                       waits=[(s_p1, E.val(s_p1))], inc=s_ev1)
                rpa.release(inst, s_ev1, E.val(s_ev1))
            ones_ap = bass.AP(tensor=vpad.tensor, offset=i * VW + 64,
                              ap=[[NKT * VW, 128], [65, 16], [1, 1]])
            E.emit("vector", "memset", ones_ap, 1.0, inc=s_ev1)
        p1_done = E.val(s_ev1)
        rwb.release(iv, s_p1, E.val(s_p1))

        jo, io, wo_ld = load_weight("WoT")

        for X in (qt, kt):
            for m in range(8):
                for n in range(NG):
                    j2, bw, inst = rpb.acquire()
                    sl = slice(S * m + 512 * n, S * m + 512 * (n + 1))
                    csl = slice(512 * n, 512 * (n + 1))
                    E.emit("tensor", "matmul", ps_b[j2][:], permb[:], X[:, sl],
                           start=True, stop=True,
                           waits=[(s_ev1, p1_done), (s_w, CONSTS_W)] + bw, inc=s_rpe)
                    E.emit("vector", "tensor_tensor", rt1[:], X[:, sl], cosb[:, csl],
                           ALU.mult, waits=[(s_ev1, p1_done), (s_w, CONSTS_W)])
                    E.emit("vector", "tensor_tensor", rt2[:], ps_b[j2][:], ssinb[:, csl],
                           ALU.mult, waits=[(s_rpe, E.val(s_rpe))])
                    E.emit("vector", "tensor_tensor", X[:, sl], rt1[:], rt2[:],
                           ALU.add, inc=s_rv)
                    rpb.release(inst, s_rv, E.val(s_rv))
        rope_done = E.val(s_rv)

        for h in range(16):
            mch = h // 2
            roff = 64 * (h % 2)
            qhead = qt[roff:roff + 64, S * mch: S * (mch + 1)]
            khead = kt[roff:roff + 64, S * mch: S * (mch + 1)]
            for g in range(NG):
                co, bwo, inst_o = rpb.acquire()
                nkt_g = 4 * g + 4
                for ktl in range(nkt_g):
                    cs, bws, inst_s = rpa.acquire()
                    ce, bwe, inst_e = rexp.acquire()
                    w_sc = [(s_rv, rope_done), (s_ev1, p1_done)] + bws
                    E.emit("tensor", "matmul", ps_a[cs][:],
                           khead[:, 128 * ktl:128 * (ktl + 1)],
                           qhead[:, 512 * g:512 * (g + 1)],
                           start=True, stop=True, waits=w_sc, inc=s_sc)
                    w_e = [(s_sc, E.val(s_sc)), (s_prep, 1)] + bwe
                    E.emit("scalar", "activation", expb[ce][:], ps_a[cs][:], AF.Exp,
                           scale=0.125, bias=negC[:], waits=w_e, inc=s_exp)
                    if ktl >= 4 * g:
                        off = 128 * ktl - 512 * g
                        if off > 0:
                            E.emit("vector", "memset", expb[ce][:, 0:off], 0.0,
                                   waits=[(s_exp, E.val(s_exp))], inc=s_exp)
                        E.emit("vector", "tensor_tensor",
                               expb[ce][:, off:off + 128], expb[ce][:, off:off + 128],
                               dmaskb[:], ALU.mult,
                               waits=[(s_exp, E.val(s_exp)), (s_w, CONSTS_W)], inc=s_exp)
                    rpa.release(inst_s, s_exp, E.val(s_exp))
                    vap = bass.AP(tensor=vpad.tensor, offset=ktl * VW + 65 * h,
                                  ap=[[NKT * VW, 128], [1, 65]])
                    w_pv = [(s_exp, E.val(s_exp))] + (bwo if ktl == 0 else [])
                    E.emit("tensor", "matmul", ps_b[co][0:65, :], vap, expb[ce][:],
                           start=(ktl == 0), stop=(ktl == nkt_g - 1),
                           waits=w_pv, inc=s_pv)
                    rexp.release(inst_e, s_pv, E.val(s_pv))
                E.emit("vector", "reciprocal",
                       rab[h % 2][:, 512 * g:512 * (g + 1)],
                       ps_b[co][64:65, :], waits=[(s_pv, E.val(s_pv))], inc=s_rec)
                E.emit("vector", "tensor_copy",
                       zt[roff:roff + 64, S * mch + 512 * g: S * mch + 512 * (g + 1)],
                       ps_b[co][0:64, :], inc=s_zev)
                rpb.release(inst_o, s_zev, E.val(s_zev))
            if h % 2 == 1:
                for g4 in range(NG):
                    jc, bwc, inst_c = rpc.acquire()
                    E.emit("tensor", "matmul", ps_c[:], selbb[:, 0:128],
                           rab[0][:, 512 * g4:512 * (g4 + 1)],
                           start=True, stop=False,
                           waits=[(s_rec, E.val(s_rec)), (s_zev, E.val(s_zev)),
                                  (s_w, CONSTS_W)] + bwc)
                    E.emit("tensor", "matmul", ps_c[:], selbb[:, 128:256],
                           rab[1][:, 512 * g4:512 * (g4 + 1)],
                           start=False, stop=True, inc=s_nb)
                    E.emit("vector", "tensor_tensor",
                           zt[:, S * mch + 512 * g4: S * mch + 512 * (g4 + 1)],
                           zt[:, S * mch + 512 * g4: S * mch + 512 * (g4 + 1)],
                           ps_c[:], ALU.mult,
                           waits=[(s_nb, E.val(s_nb))], inc=s_nrm)
                    rpc.release(inst_c, s_nrm, E.val(s_nrm))
        nrm_done = E.val(s_nrm)

        for i in range(NKT):
            xr, bwx, inst_x = rxr.acquire()
            E.emit("gpsimd", "dma_start", xres[:, D * xr: D * (xr + 1)],
                   x_in[xoff + 128 * i: xoff + 128 * (i + 1), :],
                   waits=bwx, inc=s_w, incby=16)
            xr_ld = E.val(s_w)
            so, bwos, inst_os = ros.acquire()
            for half in range(2):
                j, bw, inst = rpa.acquire()
                waits = [(s_nrm, nrm_done), (s_w, wo_ld)] + bw + (bwos if half == 0 else [])
                kloop(ps_a[j][:],
                      [zt[:, S * p + 128 * i: S * p + 128 * (i + 1)] for p in range(8)],
                      [wslice(jo, p, 512 * half, 512 * (half + 1)) for p in range(8)],
                      waits, s_p1)
                E.emit("vector", "tensor_tensor",
                       osb[:, D * so + 512 * half: D * so + 512 * (half + 1)],
                       ps_a[j][:],
                       xres[:, D * xr + 512 * half: D * xr + 512 * (half + 1)],
                       ALU.add,
                       waits=[(s_p1, E.val(s_p1)), (s_w, xr_ld)], inc=s_res)
                rpa.release(inst, s_res, E.val(s_res))
            rxr.release(inst_x, s_res, E.val(s_res))
            E.emit("sync", "dma_start",
                   tmp1_out[xoff + 128 * i: xoff + 128 * (i + 1), :],
                   osb[:, D * so: D * (so + 1)],
                   waits=[(s_res, E.val(s_res))], inc=s_st, incby=16)
            ros.release(inst_os, s_st, E.val(s_st))
        res_total = E.val(s_res)
        rwb.release(io, s_p1, E.val(s_p1))

    E.wait("sync", s_st)
    E.finalize()
    return nc


def build_l0_ffn(T, debug=False):
    S = T // BPC
    NG = S // 512
    NTS = S // 128
    NSL = 64
    nc = new_nc()
    dt_in = lambda n, shp, dt=F32: nc.dram_tensor(n, shp, dt, kind="ExternalInput").ap()
    xraw_in = dt_in("xraw", [T, D], F32)
    stats_in = dt_in("stats", [128, 2 * (T // 128)], F32)
    xn_in = dt_in("xn", [T, D], BF16)
    tmp1_in = dt_in("tmp1", [T, D], BF16)
    sinit_in = dt_in("slot_init", [NSL, D])
    w1_in = dt_in("w1", [2 * D, 512]); b1_in = dt_in("b1", [512])
    w2_in = dt_in("w2", [512, NSL]); b2_in = dt_in("b2", [NSL])
    tau_in = dt_in("tau", [1])
    wih_in = dt_in("WihT", [D, 3 * D]); whh_in = dt_in("WhhT", [D, 3 * D])
    bih_in = dt_in("bih", [3 * D]); bhh_in = dt_in("bhh", [3 * D])
    wqs_in = dt_in("wqs", [D, 256]); bqs_in = dt_in("bqs", [256])
    wks_in = dt_in("wks", [D, 256]); bks_in = dt_in("bks", [256])
    wvs_in = dt_in("wvs", [D, D]); bvs_in = dt_in("bvs", [D])
    lng_in = dt_in("lng", [D]); lnb_in = dt_in("lnb", [D])
    wvp_in = dt_in("wvp", [D, D]); bvp_in = dt_in("bvp", [D])
    wo_in = dt_in("wo", [D, D]); bo_in = dt_in("bo", [D])
    ident_in = dt_in("ident", [128, 128])
    bdmask_in = dt_in("bdmask", [128, 128])
    x1_out = nc.dram_tensor("x1", [T, D], BF16, kind="ExternalOutput").ap()
    dbg = {}
    if debug:
        for nm, shp, dt in (("d_alphaT", [128, T], BF16), ("d_siT", [128, 1024], BF16),
                            ("d_rdcol", [128, 1], F32), ("d_gib", [128, 3 * D], BF16),
                            ("d_ghb", [128, 3 * D], BF16), ("d_snew", [128, D], F32),
                            ("d_svb", [128, D], BF16), ("d_esc", [128, 128], F32),
                            ("d_vb", [128, D], BF16), ("d_ycur", [128, D], F32),
                            ("d_midT", [128, 8 * S], BF16),
                            ("d_ctile", [128, 1], F32), ("d_m8", [128, 8], F32),
                            ("d_e8", [128, 8], F32), ("d_den8", [128, 1], F32),
                            ("d_rden8", [128, 1], F32), ("d_negm1c", [128, 1], F32),
                            ("d_al0", [128, 64], BF16), ("d_lgt0", [128, 64], F32),
                            ("d_attnb", [128, 128], BF16), ("d_dsc", [128, 1], F32),
                            ("d_rdsc", [128, 1], F32), ("d_var64", [128, 1], F32),
                            ("d_nm64", [128, 1], F32), ("d_ypre", [128, D], F32),
                            ("d_rs64", [128, 1], F32), ("d_mu64", [128, 1], F32),
                            ("d_snewb", [128, D], BF16), ("d_attnT", [128, 128], BF16)):
            dbg[nm] = nc.dram_tensor(nm, shp, dt, kind="ExternalOutput").ap()

    E = Em(nc)
    sb = lambda n, shp, dt=F32: nc.alloc_sbuf_tensor(n, shp, dt).ap()

    big0 = sb("big0", [128, 16 * D], BF16)
    wvpb = big0[:, 0:8 * D]
    wob = big0[:, 8 * D:16 * D]
    big1 = sb("big1", [128, 8 * D], BF16)
    big1f = big1[:].bitcast(F32)
    xnf32 = big1f[:, 0:1024]
    xnTf = big1f[:, 1024:2048]
    htile = big1f[:, 2048:2560]
    hTf = big1f[:, 2560:3072]
    wvsb = big1[:, 0:8 * D]
    big2 = sb("big2", [128, 16 * 512], BF16)
    wgih = big2[:, 0:8 * 512]
    wghh = big2[:, 8 * 512:16 * 512]
    midT = sb("midT", [128, 8 * S], BF16)
    w1f = midT[:].bitcast(F32)[:, 0:8 * 512]
    sinitf = w1f[0:NSL, 0:1024]

    wqb = sb("wqb", [128, 8 * 256], BF16)
    wkb = sb("wkb", [128, 8 * 256], BF16)
    identb = sb("identb", [128, 128], BF16)
    identf = sb("identf", [128, 128], F32)
    bdmb = sb("bdmb", [128, 128], BF16)
    sinit2 = sb("sinit2", [128, D], BF16)
    sinit = sinit2[0:NSL, :]
    sinitT2 = sb("sinitT2", [128, 8 * 128], BF16)
    smTf = sb("smTf", [128, 8])
    statsb = sb("statsb", [128, 2 * (T // 128)])
    xrawt = sb("xrawt", [128, D])
    w2f = sb("w2f", [128, 4 * NSL])
    b1f = sb("b1f", [1, 512]); b2f = sb("b2f", [1, NSL])
    onesf = sb("onesf", [1, 128]); onec64f = sb("onec64f", [NSL, 1])
    biashrow = sb("biashrow", [1, 512])
    onesb = sb("onesb", [1, 512], BF16)
    onec128 = sb("onec128", [128, 1], BF16)
    rows = {}
    for nm, ln in (("bih", 3 * D), ("bhh", 3 * D),
                   ("bqs", 256), ("bks", 256), ("bvs", D), ("lng", D), ("lnb", D),
                   ("bvp", D), ("bo", D)):
        rows[nm] = sb("row_" + nm, [1, ln], BF16)
    taur = sb("taur", [1, 1]); taua = sb("taua", [1, 1]); taub2 = sb("taub2", [1, 1])
    ctaub = sb("ctaub", [1, 1], BF16); ctile = sb("ctile", [128, 1])
    lgt = sb("lgt", [128, 2 * NSL])
    albuf = sb("albuf", [128, 2 * 128], BF16)
    alphaT = sb("alphaT", [128, BPC * S], BF16)
    siT = sb("siT", [128, 8 * 128], BF16)
    slin = sb("slin", [NSL, D], BF16)
    rdcol = sb("rdcol", [128, 1]); rdcol2 = sb("rdcol2", [128, 1])
    m8 = sb("m8", [128, 8]); e8 = sb("e8", [128, 8])
    negm1 = sb("negm1", [128, 1]); negm1c = sb("negm1c", [128, 1])
    mask64 = sb("mask64", [128, NSL]); e64 = sb("e64", [128, NSL])
    tmask = sb("tmask", [128, NSL])
    den8 = sb("den8", [128, 1]); rden8 = sb("rden8", [128, 1])
    xntile = sb("xntile", [128, 2 * D], BF16)
    ghb = sb("ghb", [128, 3 * D], BF16)
    gib = sb("gib", [128, 3 * D], BF16)
    gr = sb("gr", [128, D], BF16); gz = sb("gz", [128, D], BF16)
    gn = sb("gn", [128, D], BF16)
    gt1 = sb("gt1", [128, D], BF16); gt2 = sb("gt2", [128, D], BF16)
    snew = sb("snew", [128, D]); snewb = sb("snewb", [128, D], BF16)
    snewT = sb("snewT", [128, 8 * 128], BF16)
    qTb = sb("qTb", [128, 2 * 128], BF16); kTb = sb("kTb", [128, 2 * 128], BF16)
    vb = sb("vb", [128, D], BF16)
    nm64 = sb("nm64", [128, 1]); esc = sb("esc", [128, 128])
    dsc = sb("dsc", [128, 1]); rdsc = sb("rdsc", [128, 1])
    attnb = sb("attnb", [128, 128], BF16); attnTb = sb("attnTb", [128, 128], BF16)
    ycur = snew
    mu64 = sb("mu64", [128, 1]); musq64 = sb("musq64", [128, 1])
    sq64 = sb("sq64", [128, 1]); var64 = sb("var64", [128, 1])
    lnt64 = sb("lnt64", [128, 1]); rs64 = sb("rs64", [128, 1])
    junk64 = None
    ybb = snewb
    ybT = sb("ybT", [128, 8 * 128], BF16)
    svb = sb("svb", [128, D], BF16)
    t1tile = sb("t1tile", [128, 2 * D], BF16)
    x1sb = sb("x1sb", [128, 2 * D], BF16)

    ps1 = [nc.alloc_psum_tensor(f"f_ps{j}", [128, 512], F32).ap() for j in range(2)]
    ps_si = [nc.alloc_psum_tensor(f"f_pssi{h}", [NSL, 512], F32).ap() for h in range(2)]
    ps_d = nc.alloc_psum_tensor("f_psd", [128, 1], F32).ap()
    ps_tr = [nc.alloc_psum_tensor(f"f_pstr{j}", [128, 128], BF16).ap() for j in range(2)]
    ps_trf = nc.alloc_psum_tensor("f_pstrf", [128, 128], F32).ap()

    s_w = E.sem("s_w"); s_ld = E.sem("s_ld")
    s_pe = E.sem("s_pe"); s_ev = E.sem("s_ev"); s_act = E.sem("s_act")
    s_hz = E.sem("s_hz"); s_st = E.sem("s_st"); s_tr = E.sem("s_tr")

    r1 = Rot(2); rtr = Rot(2); rtrf = Rot(1)
    rlg = Rot(2); ral = Rot(2); rxn = Rot(2); rt1r = Rot(2); rx1 = Rot(2)

    def g_cast(dst, src, waits=()):
        E.emit("gpsimd", "dma_start", dst, src, waits=waits, inc=s_w, incby=16)
        return E.val(s_w)

    def ld_chunked(dst, src, c, waits=()):
        E.emit("gpsimd", "dma_start", dst.rearrange("p (c n) -> p c n", c=c),
               src.rearrange("(c p) n -> p c n", p=128), waits=waits, inc=s_w, incby=16)
        return E.val(s_w)

    def evac(dst, src_ps, wpe, inc=s_ev, extra=()):
        E.emit("vector", "tensor_copy", dst, src_ps,
               waits=[(s_pe, wpe)] + list(extra), inc=inc)

    w_id = g_cast(identb[:], ident_in)
    w_bd = g_cast(bdmb[:], bdmask_in)
    E.emit("sync", "dma_start", identf[:], ident_in, inc=s_ld, incby=16)
    w_idf = E.val(s_ld)
    w_sinit2a = g_cast(sinit2[0:NSL, :], sinit_in)
    w_sinit2b = g_cast(sinit2[NSL:128, :], sinit_in)
    w_sinit = w_sinit2b
    E.emit("sync", "dma_start", statsb[:], stats_in, inc=s_ld, incby=16)
    E.emit("sync", "dma_start", sinitf, sinit_in, inc=s_ld, incby=16)
    E.emit("sync", "dma_start", w2f[:].rearrange("p (c n) -> p c n", c=4),
           w2_in.rearrange("(c p) n -> p c n", p=128), inc=s_ld, incby=16)
    E.emit("sync", "dma_start", b1f[:], b1_in.unsqueeze(0), inc=s_ld, incby=16)
    E.emit("sync", "dma_start", b2f[:], b2_in.unsqueeze(0), inc=s_ld, incby=16)
    w_f32c = E.val(s_ld)
    for nm, src in (("bih", bih_in), ("bhh", bhh_in),
                    ("bqs", bqs_in), ("bks", bks_in), ("bvs", bvs_in), ("lng", lng_in),
                    ("lnb", lnb_in), ("bvp", bvp_in), ("bo", bo_in)):
        g_cast(rows[nm], src.unsqueeze(0))
    w_rows = E.val(s_w)
    w_wq = ld_chunked(wqb, wqs_in, 8)
    w_wk = ld_chunked(wkb, wks_in, 8)
    E.emit("sync", "dma_start", taur[:], tau_in.unsqueeze(0), inc=s_ld, incby=16)
    w_tau = E.val(s_ld)

    E.emit("vector", "memset", onesb[:], 1.0)
    E.emit("vector", "memset", onec128[:], 1.0)
    E.emit("vector", "memset", alphaT[:], 0.0)
    E.emit("vector", "memset", onesf[:], 1.0)
    E.emit("vector", "memset", onec64f[:], 1.0 / NSL, inc=s_ev)
    prep_ev = E.val(s_ev)

    for c in range(8):
        j, bw, it = rtr.acquire()
        E.emit("tensor", "transpose", ps_tr[j][:],
               sinit2[:, 128 * c:128 * (c + 1)], identb[:],
               waits=[(s_w, max(w_sinit2b, w_id))] + bw, inc=s_tr)
        evac(sinitT2[:, 128 * c:128 * (c + 1)], ps_tr[j][:], 0,
             extra=[(s_tr, E.val(s_tr))])
        rtr.release(it, s_ev, E.val(s_ev))
    for c in range(8):
        j, bw, it = r1.acquire()
        E.emit("tensor", "matmul", ps1[j][0:128, 0:1],
               sinitf[:, 128 * c:128 * (c + 1)], onec64f[:],
               start=True, stop=True,
               waits=[(s_ld, w_f32c), (s_ev, prep_ev)] + bw, inc=s_pe)
        evac(smTf[:, c:c + 1], ps1[j][0:128, 0:1], E.val(s_pe))
        r1.release(it, s_ev, E.val(s_ev))
    E.emit("sync", "dma_start", w1f.rearrange("p (c n) -> p c n", c=8),
           w1_in[D:2 * D, :].rearrange("(c p) n -> p c n", p=128),
           waits=[(s_ev, E.val(s_ev))], inc=s_ld, incby=16)
    w_w1b = E.val(s_ld)
    j, bw, it = r1.acquire()
    for c in range(8):
        E.emit("tensor", "matmul", ps1[j][0:1, :],
               smTf[:, c:c + 1], w1f[:, 512 * c:512 * (c + 1)],
               start=(c == 0), stop=False,
               waits=([(s_ld, w_w1b), (s_ev, E.val(s_ev))] + bw) if c == 0 else ())
    E.emit("tensor", "matmul", ps1[j][0:1, :], onesf[:, 0:1], b1f[:],
           start=False, stop=True, inc=s_pe)
    evac(biashrow[:], ps1[j][0:1, :], E.val(s_pe))
    r1.release(it, s_ev, E.val(s_ev))
    E.emit("sync", "dma_start", w1f.rearrange("p (c n) -> p c n", c=8),
           w1_in[0:D, :].rearrange("(c p) n -> p c n", p=128),
           waits=[(s_pe, E.val(s_pe))], inc=s_ld, incby=16)
    w_w1t = E.val(s_ld)
    E.emit("scalar", "activation", taua[:], taur[:], AF.Abs,
           waits=[(s_ld, w_tau)], inc=s_act)
    E.emit("vector", "tensor_scalar_add", taua[:], taua[:], 0.1,
           waits=[(s_act, E.val(s_act))], inc=s_hz)
    E.emit("vector", "reciprocal", taub2[:], taua[:],
           waits=[(s_hz, E.val(s_hz))])
    E.emit("vector", "tensor_copy", ctaub[:], taub2[:], inc=s_ev)
    j, bw, it = r1.acquire()
    E.emit("tensor", "matmul", ps1[j][0:128, 0:1], onesb[:, 0:128], ctaub[:],
           start=True, stop=True, waits=[(s_ev, E.val(s_ev))] + bw, inc=s_pe)
    evac(ctile[:], ps1[j][0:128, 0:1], E.val(s_pe))
    r1.release(it, s_ev, E.val(s_ev))

    pe_ht_done = 0
    for b in range(BPC):
        xoff = b * S
        for i in range(NTS):
            gidx = b * NTS + i
            E.emit("sync", "dma_start", xrawt[:],
                   xraw_in[xoff + 128 * i: xoff + 128 * (i + 1), :],
                   waits=[(s_ev, E.val(s_ev))], inc=s_ld, incby=16)
            E.emit("vector", "tensor_scalar", xnf32, xrawt[:],
                   statsb[:, 2 * gidx:2 * gidx + 1],
                   statsb[:, 2 * gidx + 1:2 * gidx + 2],
                   ALU.subtract, ALU.mult,
                   waits=[(s_ld, E.val(s_ld)), (s_ld, w_f32c)], inc=s_ev)
            xnf_ev = E.val(s_ev)
            for c in range(8):
                jtf, bwtf, itf = rtrf.acquire()
                E.emit("tensor", "transpose", ps_trf[:],
                       xnf32[:, 128 * c:128 * (c + 1)], identf[:],
                       waits=[(s_ev, xnf_ev), (s_ld, w_idf)] + bwtf, inc=s_tr)
                E.emit("vector", "tensor_copy", xnTf[:, 128 * c:128 * (c + 1)],
                       ps_trf[:], waits=[(s_tr, E.val(s_tr))], inc=s_ev)
                rtrf.release(itf, s_ev, E.val(s_ev))
            xnT_ev = E.val(s_ev)
            j, bw, it = r1.acquire()
            for c in range(8):
                E.emit("tensor", "matmul", ps1[j][:],
                       xnTf[:, 128 * c:128 * (c + 1)], w1f[:, 512 * c:512 * (c + 1)],
                       start=(c == 0), stop=False,
                       waits=([(s_ld, w_w1t), (s_ev, xnT_ev)] + bw) if c == 0 else ())
            E.emit("tensor", "matmul", ps1[j][:], onesf[:], biashrow[:],
                   start=False, stop=True, inc=s_pe)
            E.emit("scalar", "activation", htile, ps1[j][:], AF.Gelu,
                   waits=[(s_pe, E.val(s_pe))], inc=s_act)
            r1.release(it, s_act, E.val(s_act))
            for c in range(4):
                jtf, bwtf, itf = rtrf.acquire()
                E.emit("tensor", "transpose", ps_trf[:],
                       htile[:, 128 * c:128 * (c + 1)], identf[:],
                       waits=[(s_act, E.val(s_act))] + bwtf, inc=s_tr)
                E.emit("vector", "tensor_copy", hTf[:, 128 * c:128 * (c + 1)],
                       ps_trf[:], waits=[(s_tr, E.val(s_tr))], inc=s_ev)
                rtrf.release(itf, s_ev, E.val(s_ev))
            hT_ev = E.val(s_ev)
            jl, bwl, il = rlg.acquire()
            j, bw, it = r1.acquire()
            for c in range(4):
                E.emit("tensor", "matmul", ps1[j][:, 0:NSL],
                       hTf[:, 128 * c:128 * (c + 1)], w2f[:, NSL * c:NSL * (c + 1)],
                       start=(c == 0), stop=False,
                       waits=([(s_ld, w_f32c), (s_ev, hT_ev)] + bw + bwl)
                             if c == 0 else ())
            E.emit("tensor", "matmul", ps1[j][:, 0:NSL], onesf[:], b2f[:],
                   start=False, stop=True, inc=s_pe)
            E.emit("vector", "tensor_copy", lgt[:, NSL * jl:NSL * (jl + 1)],
                   ps1[j][:, 0:NSL], waits=[(s_pe, E.val(s_pe))], inc=s_ev)
            r1.release(it, s_ev, E.val(s_ev))
            lg_ready = E.val(s_ev)
            lgv = lgt[:, NSL * jl:NSL * (jl + 1)]
            jx, bwx, ix = rxn.acquire()
            E.emit("sync", "dma_start", xntile[:, D * jx:D * (jx + 1)],
                   xn_in[xoff + 128 * i: xoff + 128 * (i + 1), :],
                   waits=bwx, inc=s_ld, incby=16)
            ld_xn = E.val(s_ld)
            E.emit("vector", "max", out=m8[:], in_=lgv, waits=[(s_ev, lg_ready)])
            E.emit("vector", "tensor_reduce", negm1[:], lgv, AX.X, ALU.max, negate=True)
            E.emit("vector", "tensor_tensor", negm1c[:], negm1[:], ctile[:], ALU.mult)
            E.emit("vector", "tensor_scalar", mask64[:], lgv, m8[:, 7:8], None,
                   ALU.is_ge, inc=s_ev)
            E.emit("scalar", "activation", e8[:], m8[:], AF.Exp,
                   scale=ctile[:], bias=negm1c[:], waits=[(s_ev, E.val(s_ev))])
            E.emit("scalar", "activation", e64[:], lgv, AF.Exp,
                   scale=ctile[:], bias=negm1c[:], inc=s_act)
            ja, bwa, ia = ral.acquire()
            alv = albuf[:, 128 * ja + 64 * b: 128 * ja + 64 * (b + 1)]
            alz = albuf[:, 128 * ja + 64 * (1 - b): 128 * ja + 64 * (2 - b)]
            alfull = albuf[:, 128 * ja:128 * (ja + 1)]
            E.emit("vector", "memset", alz, 0.0, waits=bwa)
            E.emit("vector", "tensor_reduce", den8[:], e8[:], AX.X, ALU.add,
                   waits=[(s_act, E.val(s_act))], inc=s_hz)
            E.emit("vector", "reciprocal", rden8[:], den8[:],
                   waits=[(s_hz, E.val(s_hz))], inc=s_hz)
            E.emit("vector", "tensor_tensor", tmask[:], e64[:], mask64[:], ALU.mult)
            E.emit("vector", "tensor_scalar_mul", alv, tmask[:], rden8[:],
                   waits=[(s_hz, E.val(s_hz))], inc=s_ev)
            al_ready = E.val(s_ev)
            if debug and b == 0 and i == 0:
                for nm, sap in (("d_ctile", ctile[:]), ("d_m8", m8[:]), ("d_e8", e8[:]),
                                ("d_den8", den8[:]), ("d_rden8", rden8[:]),
                                ("d_negm1c", negm1c[:]), ("d_al0", alv),
                                ("d_lgt0", lgv)):
                    E.emit("sync", "dma_start", dbg[nm], sap,
                           waits=[(s_ev, al_ready)], inc=s_st, incby=16)
            jt, bwt, itr = rtr.acquire()
            E.emit("tensor", "transpose", ps_tr[jt][:], alfull, identb[:],
                   waits=[(s_ev, al_ready), (s_w, w_id)] + bwt, inc=s_tr)
            E.emit("vector", "tensor_copy",
                   alphaT[64 * b:64 * (b + 1), xoff + 128 * i: xoff + 128 * (i + 1)],
                   ps_tr[jt][64 * b:64 * (b + 1), :],
                   waits=[(s_tr, E.val(s_tr))], inc=s_ev)
            rtr.release(itr, s_ev, E.val(s_ev))
            for hh in range(2):
                E.emit("tensor", "matmul", ps_si[hh][:],
                       alv, xntile[:, D * jx + 512 * hh: D * jx + 512 * (hh + 1)],
                       start=(i == 0), stop=(i == NTS - 1),
                       waits=[(s_ev, al_ready), (s_ld, ld_xn)] if hh == 0 else ())
            E.emit("tensor", "matmul", ps_d[:], alfull, onec128[:],
                   start=(b == 0 and i == 0), stop=(b == BPC - 1 and i == NTS - 1),
                   inc=s_pe)
            ral.release(ia, s_pe, E.val(s_pe))
            rxn.release(ix, s_pe, E.val(s_pe))
        si_pe = E.val(s_pe)
        for hh in range(2):
            E.emit("vector", "tensor_copy", slin[:, 512 * hh:512 * (hh + 1)],
                   ps_si[hh][:], waits=[(s_pe, si_pe)], inc=s_ev)
        slin_ev = E.val(s_ev)
        for c in range(8):
            j, bwt, it = rtr.acquire()
            E.emit("tensor", "transpose", ps_tr[j][0:128, 0:NSL],
                   slin[:, 128 * c:128 * (c + 1)], identb[0:NSL, 0:NSL],
                   waits=[(s_ev, slin_ev), (s_w, w_id)] + bwt, inc=s_tr)
            E.emit("vector", "tensor_copy",
                   siT[:, 128 * c + 64 * b: 128 * c + 64 * (b + 1)],
                   ps_tr[j][0:128, 0:NSL], waits=[(s_tr, E.val(s_tr))], inc=s_ev)
            rtr.release(it, s_ev, E.val(s_ev))
    E.emit("vector", "tensor_scalar", rdcol2[:], ps_d[:],
           1e-8, None, ALU.add, waits=[(s_pe, E.val(s_pe))], inc=s_hz)
    E.emit("vector", "reciprocal", rdcol[:], rdcol2[:],
           waits=[(s_hz, E.val(s_hz))], inc=s_hz)
    rd_hz = E.val(s_hz)
    siT_ev = E.val(s_ev)

    pe_gs_done = pe_ht_done
    for gs in range(6):
        wihw = [(s_pe, pe_gs_done)]
        w_ih = ld_chunked(wgih, wih_in[:, 512 * gs:512 * (gs + 1)], 8, waits=wihw)
        w_hh = ld_chunked(wghh, whh_in[:, 512 * gs:512 * (gs + 1)], 8)
        j, bw, it = r1.acquire()
        for c in range(8):
            E.emit("tensor", "matmul", ps1[j][:],
                   sinitT2[:, 128 * c:128 * (c + 1)],
                   wghh[:, 512 * c:512 * (c + 1)],
                   start=(c == 0), stop=False,
                   waits=([(s_w, w_hh), (s_ev, siT_ev)] + bw) if c == 0 else ())
        E.emit("tensor", "matmul", ps1[j][:], onesb[:, 0:128],
               rows["bhh"][:, 512 * gs:512 * (gs + 1)], start=False, stop=True,
               inc=s_pe)
        evac(ghb[:, 512 * gs:512 * (gs + 1)], ps1[j][:], E.val(s_pe))
        r1.release(it, s_ev, E.val(s_ev))
        j, bw, it = r1.acquire()
        for c in range(8):
            E.emit("tensor", "matmul", ps1[j][:],
                   siT[:, 128 * c:128 * (c + 1)],
                   wgih[:, 512 * c:512 * (c + 1)],
                   start=(c == 0), stop=(c == 7),
                   waits=([(s_w, w_ih)] + bw) if c == 0 else (),
                   inc=(s_pe if c == 7 else None))
        E.emit("vector", "tensor_scalar_mul", gib[:, 512 * gs:512 * (gs + 1)],
               ps1[j][:], rdcol[:],
               waits=[(s_pe, E.val(s_pe)), (s_hz, rd_hz)], inc=s_ev)
        r1.release(it, s_ev, E.val(s_ev))
        j, bw, it = r1.acquire()
        E.emit("tensor", "matmul", ps1[j][:], onesb[:, 0:128],
               rows["bih"][:, 512 * gs:512 * (gs + 1)],
               start=True, stop=True, waits=[(s_w, w_rows)] + bw, inc=s_pe)
        E.emit("vector", "tensor_tensor", gib[:, 512 * gs:512 * (gs + 1)],
               gib[:, 512 * gs:512 * (gs + 1)], ps1[j][:], ALU.add,
               waits=[(s_pe, E.val(s_pe))], inc=s_ev)
        r1.release(it, s_ev, E.val(s_ev))
        pe_gs_done = E.val(s_pe)
    gi_done = E.val(s_ev)

    E.emit("vector", "tensor_tensor", gt1[:], gib[:, 0:D], ghb[:, 0:D], ALU.add,
           waits=[(s_ev, gi_done)], inc=s_ev)
    E.emit("vector", "tensor_tensor", gt2[:], gib[:, D:2 * D], ghb[:, D:2 * D],
           ALU.add, inc=s_ev)
    E.emit("scalar", "activation", gr[:], gt1[:], AF.Sigmoid,
           waits=[(s_ev, E.val(s_ev))])
    E.emit("scalar", "activation", gz[:], gt2[:], AF.Sigmoid, inc=s_act)
    E.emit("vector", "tensor_tensor", gt1[:], gr[:], ghb[:, 2 * D:3 * D], ALU.mult,
           waits=[(s_act, E.val(s_act))])
    E.emit("vector", "tensor_tensor", gt1[:], gt1[:], gib[:, 2 * D:3 * D], ALU.add,
           inc=s_ev)
    E.emit("scalar", "activation", gn[:], gt1[:], AF.Tanh,
           waits=[(s_ev, E.val(s_ev))], inc=s_act)
    E.emit("vector", "tensor_tensor", gt2[:], sinit2[:], gn[:], ALU.subtract,
           waits=[(s_act, E.val(s_act)), (s_w, w_sinit2b)])
    E.emit("vector", "tensor_tensor", gt2[:], gz[:], gt2[:], ALU.mult)
    E.emit("vector", "tensor_tensor", snew[:], gn[:], gt2[:], ALU.add)
    E.emit("vector", "tensor_copy", snewb[:], snew[:], inc=s_ev)
    snew_ev = E.val(s_ev)
    if debug:
        E.emit("sync", "dma_start", dbg["d_snewb"], snewb[:],
               waits=[(s_ev, snew_ev)], inc=s_st, incby=16)
    for c in range(8):
        j, bw, it = rtr.acquire()
        E.emit("tensor", "transpose", ps_tr[j][:],
               snewb[:, 128 * c:128 * (c + 1)], identb[:],
               waits=[(s_ev, snew_ev)] + bw, inc=s_tr)
        evac(snewT[:, 128 * c:128 * (c + 1)], ps_tr[j][:], 0,
             extra=[(s_tr, E.val(s_tr))])
        rtr.release(it, s_ev, E.val(s_ev))
    snewT_ev = E.val(s_ev)

    w_wvs = ld_chunked(wvsb, wvs_in, 8, waits=[(s_pe, si_pe)])
    w_wvp = ld_chunked(wvpb, wvp_in, 8, waits=[(s_pe, pe_ht_done)])
    w_wo = ld_chunked(wob, wo_in, 8)

    for (dst, wsrc, w_rdy, brow) in ((qTb, wqb, w_wq, "bqs"), (kTb, wkb, w_wk, "bks")):
        for qc in range(2):
            j, bw, it = r1.acquire()
            for c in range(8):
                E.emit("tensor", "matmul", ps1[j][0:128, 0:128],
                       wsrc[:, 256 * c + 128 * qc: 256 * c + 128 * (qc + 1)],
                       snewT[:, 128 * c:128 * (c + 1)],
                       start=(c == 0), stop=False,
                       waits=([(s_w, w_rdy), (s_ev, snewT_ev)] + bw) if c == 0 else ())
            E.emit("tensor", "matmul", ps1[j][0:128, 0:128],
                   rows[brow][:, 128 * qc:128 * (qc + 1)], onesb[:, 0:128],
                   start=False, stop=True, inc=s_pe)
            evac(dst[:, 128 * qc:128 * (qc + 1)], ps1[j][0:128, 0:128], E.val(s_pe))
            r1.release(it, s_ev, E.val(s_ev))
    for half in range(2):
        j, bw, it = r1.acquire()
        for c in range(8):
            E.emit("tensor", "matmul", ps1[j][:],
                   snewT[:, 128 * c:128 * (c + 1)],
                   wvsb[:, D * c + 512 * half: D * c + 512 * (half + 1)],
                   start=(c == 0), stop=False,
                   waits=([(s_w, w_wvs), (s_ev, snewT_ev)] + bw) if c == 0 else ())
        E.emit("tensor", "matmul", ps1[j][:], onesb[:, 0:128],
               rows["bvs"][:, 512 * half:512 * (half + 1)],
               start=False, stop=True, inc=s_pe)
        evac(vb[:, 512 * half:512 * (half + 1)], ps1[j][:], E.val(s_pe))
        r1.release(it, s_ev, E.val(s_ev))
    qkv_ev = E.val(s_ev)

    j, bw, it = r1.acquire()
    for qc in range(2):
        E.emit("tensor", "matmul", ps1[j][0:128, 0:128],
               qTb[:, 128 * qc:128 * (qc + 1)], kTb[:, 128 * qc:128 * (qc + 1)],
               start=(qc == 0), stop=(qc == 1),
               waits=([(s_ev, qkv_ev)] + bw) if qc == 0 else (),
               inc=(s_pe if qc == 1 else None))
    sc_pe = E.val(s_pe)
    E.emit("vector", "tensor_reduce", nm64[:], ps1[j][0:128, 0:128], AX.X, ALU.max,
           negate=True, waits=[(s_pe, sc_pe)])
    E.emit("vector", "tensor_scalar_mul", nm64[:], nm64[:], 256.0 ** -0.5, inc=s_ev)
    E.emit("scalar", "activation", esc[:], ps1[j][0:128, 0:128], AF.Exp,
           scale=256.0 ** -0.5, bias=nm64[:], waits=[(s_ev, E.val(s_ev))], inc=s_act)
    E.emit("vector", "tensor_tensor", esc[:], esc[:], bdmb[:], ALU.mult,
           waits=[(s_act, E.val(s_act)), (s_w, w_bd)])
    E.emit("vector", "tensor_reduce", dsc[:], esc[:], AX.X, ALU.add, inc=s_hz)
    E.emit("vector", "reciprocal", rdsc[:], dsc[:],
           waits=[(s_hz, E.val(s_hz))], inc=s_hz)
    E.emit("vector", "tensor_scalar_mul", attnb[:], esc[:], rdsc[:],
           waits=[(s_hz, E.val(s_hz))], inc=s_ev)
    r1.release(it, s_ev, E.val(s_ev))
    attn_ev = E.val(s_ev)
    jt, bwt, itr = rtr.acquire()
    E.emit("tensor", "transpose", ps_tr[jt][:], attnb[:], identb[:],
           waits=[(s_ev, attn_ev)] + bwt, inc=s_tr)
    E.emit("vector", "tensor_copy", attnTb[:], ps_tr[jt][:],
           waits=[(s_tr, E.val(s_tr))], inc=s_ev)
    if debug:
        E.emit("sync", "dma_start", dbg["d_attnT"], attnTb[:],
               waits=[(s_ev, E.val(s_ev))], inc=s_st, incby=16)
    rtr.release(itr, s_ev, E.val(s_ev))
    attnT_ev = E.val(s_ev)

    for half in range(2):
        j, bw, it = r1.acquire()
        E.emit("tensor", "matmul", ps1[j][:], attnTb[:],
               vb[:, 512 * half:512 * (half + 1)], start=True, stop=True,
               waits=[(s_ev, attnT_ev)] + bw, inc=s_pe)
        E.emit("vector", "tensor_tensor", ycur[:, 512 * half:512 * (half + 1)],
               snew[:, 512 * half:512 * (half + 1)], ps1[j][:], ALU.add,
               waits=[(s_pe, E.val(s_pe))], inc=s_ev)
        r1.release(it, s_ev, E.val(s_ev))
    y_ev = E.val(s_ev)
    ypre = big2[:, 0:2048].bitcast(F32) if debug else None
    if debug:
        E.emit("vector", "tensor_copy", ypre[:], ycur[:], waits=[(s_ev, y_ev)])
    E.emit("vector", "reduce_sum", mu64[:], ycur[:], AX.X, waits=[(s_ev, y_ev)])
    E.emit("vector", "tensor_scalar_mul", mu64[:], mu64[:], 1.0 / D)
    E.emit("vector", "tensor_tensor", musq64[:], mu64[:], mu64[:], ALU.mult)
    E.emit("vector", "tensor_scalar_add", musq64[:], musq64[:], -1e-5, inc=s_ev)
    E.emit("vector", "tensor_tensor", gt1[:], ycur[:], ycur[:], ALU.mult,
           waits=[(s_ev, E.val(s_ev))])
    E.emit("vector", "reduce_sum", sq64[:], gt1[:], AX.X)
    E.emit("vector", "scalar_tensor_tensor", var64[:], sq64[:], 1.0 / D, musq64[:],
           ALU.mult, ALU.subtract, inc=s_ev)
    E.emit("scalar", "activation", lnt64[:], var64[:], AF.Ln,
           waits=[(s_ev, E.val(s_ev))])
    E.emit("scalar", "activation", rs64[:], lnt64[:], AF.Exp, scale=-0.5, inc=s_act)
    E.emit("vector", "tensor_scalar", ycur[:], ycur[:], mu64[:], rs64[:],
           ALU.subtract, ALU.mult, waits=[(s_act, E.val(s_act))], inc=s_ev)
    for half in range(2):
        j, bw, it = r1.acquire()
        E.emit("tensor", "matmul", ps1[j][:], onesb[:, 0:128],
               rows["lng"][:, 512 * half:512 * (half + 1)], start=True, stop=True,
               waits=([(s_w, w_rows), (s_ev, E.val(s_ev))] + bw), inc=s_pe)
        E.emit("vector", "tensor_tensor", ycur[:, 512 * half:512 * (half + 1)],
               ycur[:, 512 * half:512 * (half + 1)], ps1[j][:], ALU.mult,
               waits=[(s_pe, E.val(s_pe))], inc=s_ev)
        r1.release(it, s_ev, E.val(s_ev))
        j, bw, it = r1.acquire()
        E.emit("tensor", "matmul", ps1[j][:], onesb[:, 0:128],
               rows["lnb"][:, 512 * half:512 * (half + 1)], start=True, stop=True,
               waits=([(s_w, w_rows)] + bw), inc=s_pe)
        E.emit("vector", "tensor_tensor", ycur[:, 512 * half:512 * (half + 1)],
               ycur[:, 512 * half:512 * (half + 1)], ps1[j][:], ALU.add,
               waits=[(s_pe, E.val(s_pe))], inc=s_ev)
        r1.release(it, s_ev, E.val(s_ev))
    E.emit("vector", "tensor_copy", ybb[:], ycur[:], inc=s_ev)
    yb_ev = E.val(s_ev)
    for c in range(8):
        j, bw, it = rtr.acquire()
        E.emit("tensor", "transpose", ps_tr[j][:],
               ybb[:, 128 * c:128 * (c + 1)], identb[:],
               waits=[(s_ev, yb_ev)] + bw, inc=s_tr)
        evac(ybT[:, 128 * c:128 * (c + 1)], ps_tr[j][:], 0,
             extra=[(s_tr, E.val(s_tr))])
        rtr.release(it, s_ev, E.val(s_ev))
    ybT_ev = E.val(s_ev)
    for half in range(2):
        j, bw, it = r1.acquire()
        for c in range(8):
            E.emit("tensor", "matmul", ps1[j][:],
                   ybT[:, 128 * c:128 * (c + 1)],
                   wvpb[:, D * c + 512 * half: D * c + 512 * (half + 1)],
                   start=(c == 0), stop=False,
                   waits=([(s_w, w_wvp), (s_ev, ybT_ev)] + bw) if c == 0 else ())
        E.emit("tensor", "matmul", ps1[j][:], onesb[:, 0:128],
               rows["bvp"][:, 512 * half:512 * (half + 1)],
               start=False, stop=True, inc=s_pe)
        evac(svb[:, 512 * half:512 * (half + 1)], ps1[j][:], E.val(s_pe))
        r1.release(it, s_ev, E.val(s_ev))
    sv_ev = E.val(s_ev)

    for b in range(BPC):
        for c in range(8):
            for g in range(NG):
                j, bw, it = r1.acquire()
                E.emit("tensor", "matmul", ps1[j][:],
                       svb[64 * b:64 * (b + 1), 128 * c:128 * (c + 1)],
                       alphaT[64 * b:64 * (b + 1), b * S + 512 * g: b * S + 512 * (g + 1)],
                       start=True, stop=True,
                       waits=[(s_ev, sv_ev)] + bw, inc=s_pe)
                evac(midT[:, S * c + 512 * g: S * c + 512 * (g + 1)],
                     ps1[j][:], E.val(s_pe))
                r1.release(it, s_ev, E.val(s_ev))
        mid_ev = E.val(s_ev)
        for i in range(NTS):
            jt1, bwt1, it1 = rt1r.acquire()
            E.emit("sync", "dma_start", t1tile[:, D * jt1:D * (jt1 + 1)],
                   tmp1_in[b * S + 128 * i: b * S + 128 * (i + 1), :],
                   waits=bwt1, inc=s_ld, incby=16)
            ld_t1 = E.val(s_ld)
            jx1, bwx1, ix1 = rx1.acquire()
            for half in range(2):
                j, bw, it = r1.acquire()
                for c in range(8):
                    E.emit("tensor", "matmul", ps1[j][:],
                           midT[:, S * c + 128 * i: S * c + 128 * (i + 1)],
                           wob[:, D * c + 512 * half: D * c + 512 * (half + 1)],
                           start=(c == 0), stop=False,
                           waits=([(s_w, w_wo), (s_ev, mid_ev)] + bw
                                  + (bwx1 if half == 0 else [])) if c == 0 else ())
                E.emit("tensor", "matmul", ps1[j][:], onesb[:, 0:128],
                       rows["bo"][:, 512 * half:512 * (half + 1)],
                       start=False, stop=True, inc=s_pe)
                E.emit("vector", "tensor_tensor",
                       x1sb[:, D * jx1 + 512 * half: D * jx1 + 512 * (half + 1)],
                       ps1[j][:],
                       t1tile[:, D * jt1 + 512 * half: D * jt1 + 512 * (half + 1)],
                       ALU.add, waits=[(s_pe, E.val(s_pe)), (s_ld, ld_t1)], inc=s_ev)
                r1.release(it, s_ev, E.val(s_ev))
            rt1r.release(it1, s_ev, E.val(s_ev))
            E.emit("sync", "dma_start", x1_out[b * S + 128 * i: b * S + 128 * (i + 1), :],
                   x1sb[:, D * jx1:D * (jx1 + 1)],
                   waits=[(s_ev, E.val(s_ev))], inc=s_st, incby=16)
            rx1.release(ix1, s_st, E.val(s_st))

    if debug:
        for nm, src_ap in (("d_attnb", attnb[:]), ("d_dsc", dsc[:]), ("d_rdsc", rdsc[:]),
                           ("d_var64", var64[:]), ("d_nm64", nm64[:]), ("d_ypre", ypre[:]),
                           ("d_rs64", rs64[:]), ("d_mu64", mu64[:]),
                           ("d_alphaT", alphaT[:]), ("d_siT", siT[:]), ("d_rdcol", rdcol[:]),
                           ("d_gib", gib[:]), ("d_ghb", ghb[:]), ("d_snew", snew[:]),
                           ("d_svb", svb[:]), ("d_esc", esc[:]), ("d_vb", vb[:]), ("d_ycur", ycur[:]),
                           ("d_midT", midT[:])):
            E.emit("sync", "dma_start", dbg[nm], src_ap, waits=[(s_ev, E.val(s_ev))],
                   inc=s_st, incby=16)
    E.wait("sync", s_st)
    E.finalize()
    return nc


def build_l1_attn(T, debug=False):
    S = T // BPC
    NB = S // 128
    W = 16
    nc = new_nc()
    dt_in = lambda n, shp, dt=F32: nc.dram_tensor(n, shp, dt, kind="ExternalInput").ap()
    xnfm_in = dt_in("xn_fm", [D, T], BF16)
    xn_in = dt_in("xn", [T, D], BF16)
    wts = {n: dt_in(n, [D, D]) for n in ("WqT", "WkT", "WvT", "WkpT", "WoT")}
    gamma_in = dt_in("gamma", [D]); beta_in = dt_in("beta", [D])
    bandm_in = dt_in("bandm", [128, 128])
    topm_in = dt_in("topm", [15, 128])
    mask0_in = dt_in("mask0", [128, W * 128])
    attn1_out = nc.dram_tensor("attn1", [T, D], BF16, kind="ExternalOutput").ap()
    dbg = {}
    if debug:
        for nm, shp, dt in (("d_zt", [128, 8 * (T // BPC)], BF16),
                            ("d_expb", [128, 2 * 2048], BF16),
                            ("d_exptop", [15, 2 * 2048], BF16),
                            ("d_tmpc", [128, 2048], BF16),
                            ("d_ztt", [128, 128], F32),
                            ("d_ra", [1, 128], BF16), ("d_rb", [1, 128], BF16),
                            ("d_xb", [128, 2048], BF16)):
            dbg[nm] = nc.dram_tensor(nm, shp, dt, kind="ExternalOutput").ap()

    E = Em(nc)
    sb = lambda n, shp, dt=F32: nc.alloc_sbuf_tensor(n, shp, dt).ap()

    HH = 4
    SP = S + 32
    qt = sb("qt", [128, HH * S], BF16)
    kpad = sb("kpad", [128, HH * SP], BF16)
    kppad = sb("kppad", [128, HH * SP], BF16)
    vtpad = sb("vtpad", [128, HH * SP], BF16)
    vw = sb("vw", [128, 17 * 512], BF16)
    zt = sb("zt", [128, 8 * S], BF16)
    xnfm = sb("xnfm_l1", [128, 8 * S], BF16)
    xb = sb("xb", [128, W * 128], BF16)
    expb = sb("expb", [128, 2 * W * 128], BF16)
    exptop = sb("exptop", [15, 2 * W * 128], BF16)
    tmpc = sb("tmpc", [128, W * 128], BF16)
    ztt = sb("ztt", [128, 128])
    rbc_sb = sb("rbc_sb", [128, 128])
    wbf = [sb(f"l1wb{j}", [128, 8 * 512], BF16) for j in range(2)]
    bandb = sb("bandb", [128, 128], BF16)
    topb = sb("topb", [15, 128], BF16)
    mask0b = sb("mask0b", [128, W * 128], BF16)
    negC5 = sb("negC5", [128, 1])
    onesc = sb("onesc_l1", [128, 1], BF16)
    selrow = sb("selrow", [1, 256], BF16)
    ra = sb("ra_l1", [1, 128], BF16); rbr = sb("rb_l1", [1, 128], BF16)
    gbc = sb("gbc", [128, D], BF16); bbc = sb("bbc", [128, D], BF16)
    grow = sb("grow", [1, D], BF16); brow = sb("brow", [1, D], BF16)
    onesr = sb("onesr_l1", [1, 128], BF16)
    xn1t = sb("xn1t", [128, 2 * D], BF16)
    otile = tmpc[:].bitcast(F32)
    oslot = sb("oslot", [128, D], BF16)
    vtmp = sb("vtmp", [128, 512], BF16)
    mu_ = sb("mu_l1", [128, 1]); musq_ = sb("musq_l1", [128, 1])
    sq_ = sb("sq_l1", [128, 1]); var_ = sb("var_l1", [128, 1])
    lnt_ = sb("lnt_l1", [128, 1]); rs_ = sb("rs_l1", [128, 1])
    junk_ = xb[:].bitcast(F32)

    PS0 = nc.alloc_psum_tensor("l1_ps0", [128, 2048], F32).ap()
    PS1 = nc.alloc_psum_tensor("l1_ps1", [128, 2048], F32).ap()

    s_w = E.sem("s_w"); s_ld = E.sem("s_ld")
    s_pe = E.sem("s_pe"); s_ev = E.sem("s_ev"); s_act = E.sem("s_act")
    s_hz = E.sem("s_hz"); s_st = E.sem("s_st")

    rxn1 = Rot(2); ros1 = Rot(1)

    def g_cast(dst, src, waits=()):
        E.emit("gpsimd", "dma_start", dst, src, waits=waits, inc=s_w, incby=16)
        return E.val(s_w)

    w_bm = g_cast(bandb[:], bandm_in)
    w_tm = g_cast(topb[:], topm_in)
    w_m0 = g_cast(mask0b[:], mask0_in)
    w_g = g_cast(grow[:], gamma_in.unsqueeze(0))
    w_b = g_cast(brow[:], beta_in.unsqueeze(0))
    E.emit("vector", "memset", negC5[:], -C_L1)
    E.emit("vector", "memset", onesc[:], 1.0)
    E.emit("vector", "memset", onesr[:], 1.0)
    E.emit("vector", "memset", selrow[:, 0:64], 1.0)
    E.emit("vector", "memset", selrow[:, 64:192], 0.0)
    E.emit("vector", "memset", selrow[:, 192:256], 1.0)
    for buf in (kpad, kppad, vtpad):
        for cch in range(HH):
            E.emit("vector", "memset", buf[:, SP * cch: SP * cch + 15], 0.0)
            E.emit("vector", "memset", buf[:, SP * cch + 15 + S: SP * (cch + 1)], 0.0)
    E.emit("vector", "memset", vw[:, 16 * 512:17 * 512], 0.0, inc=s_ev)
    prep_ev = E.val(s_ev)
    for half in range(2):
        E.emit("tensor", "matmul", PS0[:, 512 * half:512 * (half + 1)],
               onesr[:], grow[:, 512 * half:512 * (half + 1)],
               start=True, stop=True,
               waits=[(s_w, w_g), (s_ev, prep_ev)], inc=s_pe)
        E.emit("tensor", "matmul", PS1[:, 512 * half:512 * (half + 1)],
               onesr[:], brow[:, 512 * half:512 * (half + 1)],
               start=True, stop=True, waits=[(s_w, w_b)], inc=s_pe)
        E.emit("vector", "tensor_copy", gbc[:, 512 * half:512 * (half + 1)],
               PS0[:, 512 * half:512 * (half + 1)], waits=[(s_pe, E.val(s_pe))])
        E.emit("vector", "tensor_copy", bbc[:, 512 * half:512 * (half + 1)],
               PS1[:, 512 * half:512 * (half + 1)], inc=s_ev)
    gb_ev = E.val(s_ev)

    PSP = [PS0[:, 512 * q:512 * (q + 1)] for q in range(4)]
    rps = Rot(4)
    qrowstride = qt.ap[0][0]
    kprowstride = kppad.ap[0][0]
    vtrowstride = vtpad.ap[0][0]
    vwrowstride = vw.ap[0][0]
    tmrowstride = tmpc.ap[0][0]

    st_count = 0
    for b in range(BPC):
        xoff = b * S
        for p in range(8):
            E.emit("sync", "dma_start", xnfm[:, S * p:S * (p + 1)],
                   xnfm_in[128 * p:128 * (p + 1), xoff:xoff + S],
                   waits=[(s_pe, E.val(s_pe))], inc=s_ld, incby=16)
        ld_x = E.val(s_ld)
        for half in range(2):
            coff = 4 * half

            def ldw(name, j):
                E.emit("gpsimd", "dma_start",
                       wbf[j][:].rearrange("p (c n) -> p c n", c=8),
                       wts[name][:, 512 * half:512 * (half + 1)]
                           .rearrange("(c p) n -> p c n", p=128),
                       waits=[(s_pe, E.val(s_pe))], inc=s_w, incby=16)
                return E.val(s_w)

            def proj_fm(dst, wj, w_rdy, dst_off, dst_stride):
                for m in range(HH):
                    for g in range(S // 512):
                        jj, bwp, itp = rps.acquire()
                        ps = PSP[jj]
                        for c in range(8):
                            E.emit("tensor", "matmul", ps,
                                   wbf[wj][:, 512 * c + 128 * m: 512 * c + 128 * (m + 1)],
                                   xnfm[:, S * c + 512 * g: S * c + 512 * (g + 1)],
                                   start=(c == 0), stop=(c == 7),
                                   waits=([(s_ld, ld_x), (s_w, w_rdy),
                                           (s_ev, E.val(s_ev))] + bwp) if c == 0 else (),
                                   inc=(s_pe if c == 7 else None))
                        E.emit("vector", "tensor_copy",
                               dst[:, dst_stride * m + dst_off + 512 * g:
                                   dst_stride * m + dst_off + 512 * (g + 1)],
                               ps, waits=[(s_pe, E.val(s_pe))], inc=s_ev)
                        rps.release(itp, s_ev, E.val(s_ev))

            wq_ld = ldw("WqT", 0)
            proj_fm(qt, 0, wq_ld, 0, S)
            wk_ld = ldw("WkT", 1)
            proj_fm(kpad, 1, wk_ld, 15, SP)
            wkp_ld = ldw("WkpT", 0)
            proj_fm(kppad, 0, wkp_ld, 15, SP)
            wv_ld = ldw("WvT", 1)
            proj_fm(vtpad, 1, wv_ld, 15, SP)
            for i in range(NB + 1):
                jj, bwp, itp = rps.acquire()
                ps = PSP[jj]
                lo = 128 * i - 15
                if i == 0:
                    m_sz, src_lo = 113, 0
                elif i == NB:
                    m_sz, src_lo = 15, S - 15
                else:
                    m_sz, src_lo = 128, lo
                for c in range(8):
                    E.emit("tensor", "matmul", ps[0:m_sz, :],
                           xnfm[:, S * c + src_lo: S * c + src_lo + m_sz],
                           wbf[1][:, 512 * c:512 * (c + 1)],
                           start=(c == 0), stop=(c == 7),
                           waits=([(s_w, wv_ld), (s_ev, E.val(s_ev))] + bwp) if c == 0 else (),
                           inc=(s_pe if c == 7 else None))
                if i == 0:
                    E.emit("vector", "tensor_copy", vtmp[0:113, :], ps[0:113, :],
                           waits=[(s_pe, E.val(s_pe))], inc=s_ev)
                    E.emit("gpsimd", "dma_start", vw[15:128, 0:512], vtmp[0:113, :],
                           waits=[(s_ev, E.val(s_ev))], inc=s_w, incby=16)
                    E.emit("vector", "memset", vw[0:15, 0:512], 0.0, inc=s_ev)
                else:
                    E.emit("vector", "tensor_copy", vw[0:m_sz, 512 * i:512 * i + 512],
                           ps[0:m_sz, :], waits=[(s_pe, E.val(s_pe))], inc=s_ev)
                rps.release(itp, s_ev, E.val(s_ev))
            vw_dma = E.val(s_w)
            proj_ev = E.val(s_ev)

            for m in range(HH):
                gch = coff + m
                for blk in range(NB):
                    s0 = 128 * blk
                    q_b = bass.AP(tensor=qt.tensor, offset=qt.offset + S * m + s0,
                                  ap=[[qrowstride, 128], [0, W], [1, 128]])
                    kp_b = bass.AP(tensor=kppad.tensor,
                                   offset=kppad.offset + SP * m + s0,
                                   ap=[[kprowstride, 128], [1, W], [1, 128]])
                    E.emit("vector", "tensor_tensor",
                           xb[:, 0:2048].rearrange("p (k t) -> p k t", k=W),
                           q_b, kp_b, ALU.mult,
                           waits=[(s_ev, proj_ev), (s_pe, E.val(s_pe))], inc=s_ev)
                    xb_ev = E.val(s_ev)
                    for hh2 in range(2):
                        roff = 64 * hh2
                        for q4 in range(4):
                            aw = ()
                            if q4 == 0:
                                aw = ([(s_ev, xb_ev)] if hh2 == 0
                                      else [(s_act, E.val(s_act))])
                            E.emit("tensor", "matmul", PS0[:, 512 * q4:512 * (q4 + 1)],
                                   kpad[roff:roff + 64, SP * m + s0: SP * m + s0 + 128],
                                   xb[roff:roff + 64, 512 * q4:512 * (q4 + 1)],
                                   start=True, stop=True,
                                   waits=aw)
                            E.emit("tensor", "matmul", PS1[0:15, 512 * q4:512 * (q4 + 1)],
                                   kpad[roff:roff + 64, SP * m + s0 + 128: SP * m + s0 + 143],
                                   xb[roff:roff + 64, 512 * q4:512 * (q4 + 1)],
                                   start=True, stop=True,
                                   inc=(s_pe if q4 == 3 else None))
                        E.emit("scalar", "activation", expb[:, 2048 * hh2:2048 * (hh2 + 1)],
                               PS0[:], AF.Exp, scale=0.125, bias=negC5[:],
                               waits=[(s_pe, E.val(s_pe)), (s_ev, prep_ev)])
                        E.emit("scalar", "activation",
                               exptop[:, 2048 * hh2:2048 * (hh2 + 1)],
                               PS1[0:15, :], AF.Exp, scale=0.125, bias=negC5[0:15, :],
                               inc=s_act)
                        mv = expb[:, 2048 * hh2:2048 * (hh2 + 1)]
                        if blk == 0:
                            E.emit("vector", "tensor_tensor", mv, mv, mask0b[:],
                                   ALU.mult, waits=[(s_act, E.val(s_act)), (s_w, w_m0)])
                        else:
                            E.emit("vector", "tensor_tensor",
                                   mv.rearrange("p (k t) -> p k t", k=W),
                                   mv.rearrange("p (k t) -> p k t", k=W),
                                   bandb[:].unsqueeze(1).to_broadcast([128, W, 128]),
                                   ALU.mult, waits=[(s_act, E.val(s_act)), (s_w, w_bm)])
                        tv = exptop[:, 2048 * hh2:2048 * (hh2 + 1)]
                        E.emit("vector", "tensor_tensor",
                               tv.rearrange("p (k t) -> p k t", k=W),
                               tv.rearrange("p (k t) -> p k t", k=W),
                               topb[:].unsqueeze(1).to_broadcast([15, W, 128]),
                               ALU.mult, waits=[(s_w, w_tm)], inc=s_ev)
                    mexp_ev = E.val(s_ev)
                    for hh2 in range(2):
                        dcol = 512 * hh2
                        for k in range(W):
                            E.emit("tensor", "matmul", PS0[0:1, dcol:dcol + 128],
                                   onesc[:], expb[:, 2048 * hh2 + 128 * k:
                                                  2048 * hh2 + 128 * (k + 1)],
                                   start=(k == 0), stop=False,
                                   waits=[(s_ev, mexp_ev)] if k == 0 and hh2 == 0 else ())
                            E.emit("tensor", "matmul", PS0[0:1, dcol:dcol + 128],
                                   onesc[0:15, :], exptop[0:15, 2048 * hh2 + 128 * k:
                                                          2048 * hh2 + 128 * (k + 1)],
                                   start=False, stop=(k == W - 1),
                                   inc=(s_pe if k == W - 1 else None))
                    d_pe = E.val(s_pe)
                    E.emit("vector", "reciprocal", ra[:], PS0[0:1, 0:128],
                           waits=[(s_pe, d_pe)])
                    E.emit("vector", "reciprocal", rbr[:], PS0[0:1, 512:640], inc=s_ev)
                    E.emit("tensor", "matmul", PS0[:, 1024:1152], selrow[:, 0:128],
                           ra[:], start=True, stop=False,
                           waits=[(s_ev, E.val(s_ev))])
                    E.emit("tensor", "matmul", PS0[:, 1024:1152], selrow[:, 128:256],
                           rbr[:], start=False, stop=True, inc=s_pe)
                    E.emit("vector", "tensor_copy", rbc_sb[:], PS0[:, 1024:1152],
                           waits=[(s_pe, E.val(s_pe))], inc=s_ev)
                    dr_ev = E.val(s_ev)
                    for hh2 in range(2):
                        roff = 64 * hh2
                        WPS = PS0 if hh2 == 0 else PS1
                        vcol = 64 * (2 * m + hh2)
                        vap_m = bass.AP(tensor=vw.tensor,
                                        offset=vw.offset + 512 * blk + vcol,
                                        ap=[[vwrowstride, 128], [1, 64]])
                        vap_t = bass.AP(tensor=vw.tensor,
                                        offset=vw.offset + 512 * (blk + 1) + vcol,
                                        ap=[[vwrowstride, 15], [1, 64]])
                        for q4 in range(4):
                            E.emit("tensor", "matmul",
                                   WPS[roff:roff + 64, 512 * q4:512 * (q4 + 1)],
                                   vap_m, expb[:, 2048 * hh2 + 512 * q4:
                                               2048 * hh2 + 512 * (q4 + 1)],
                                   start=True, stop=False,
                                   waits=[(s_ev, dr_ev), (s_w, vw_dma)]
                                         if q4 == 0 and hh2 == 0 else ())
                            E.emit("tensor", "matmul",
                                   WPS[roff:roff + 64, 512 * q4:512 * (q4 + 1)],
                                   vap_t, exptop[0:15, 2048 * hh2 + 512 * q4:
                                                 2048 * hh2 + 512 * (q4 + 1)],
                                   start=False, stop=True,
                                   inc=(s_pe if q4 == 3 and hh2 == 1 else None))
                    wd_pe = E.val(s_pe)
                    for hh2 in range(2):
                        roff = 64 * hh2
                        WPS = PS0 if hh2 == 0 else PS1
                        vt_b = bass.AP(tensor=vtpad.tensor,
                                       offset=vtpad.offset + vtrowstride * roff
                                              + SP * m + s0,
                                       ap=[[vtrowstride, 64], [1, W], [1, 128]])
                        E.emit("vector", "tensor_tensor",
                               tmpc[roff:roff + 64, :].rearrange("p (k t) -> p k t", k=W),
                               WPS[roff:roff + 64, :].rearrange("p (k t) -> p k t", k=W),
                               vt_b, ALU.mult,
                               waits=[(s_pe, wd_pe)] if hh2 == 0 else ())
                    red_in = bass.AP(tensor=tmpc.tensor, offset=tmpc.offset,
                                     ap=[[tmrowstride, 128], [1, 128], [128, W]])
                    E.emit("vector", "tensor_reduce", ztt[:].unsqueeze(2),
                           red_in, AX.X, ALU.add)
                    E.emit("vector", "tensor_tensor",
                           zt[:, S * gch + s0: S * gch + s0 + 128],
                           ztt[:], rbc_sb[:], ALU.mult, inc=s_ev)
        z_done = E.val(s_ev)
        if debug and b == 0:
            for nm, src_ap in (("d_zt", zt[:]), ("d_expb", expb[:]),
                               ("d_exptop", exptop[:]), ("d_tmpc", tmpc[:]),
                               ("d_ztt", ztt[:]), ("d_ra", ra[:]), ("d_rb", rbr[:]),
                               ("d_xb", xb[:, 0:2048])):
                E.emit("sync", "dma_start", dbg[nm], src_ap,
                       waits=[(s_ev, z_done)], inc=s_st, incby=16)
        wo_ld = []
        for j in range(2):
            E.emit("gpsimd", "dma_start",
                   wbf[j][:].rearrange("p (c n) -> p c n", c=8),
                   wts["WoT"][:, 512 * j:512 * (j + 1)].rearrange("(c p) n -> p c n", p=128),
                   waits=[(s_pe, E.val(s_pe))], inc=s_w, incby=16)
            wo_ld.append(E.val(s_w))
        for i in range(NB):
            jx1, bwx1, ix1 = rxn1.acquire()
            E.emit("sync", "dma_start", xn1t[:, D * jx1:D * (jx1 + 1)],
                   xn_in[xoff + 128 * i: xoff + 128 * (i + 1), :],
                   waits=bwx1, inc=s_ld, incby=16)
            ld_t = E.val(s_ld)
            for half2 in range(2):
                jj, bwp, itp = rps.acquire()
                ps = PSP[jj]
                for c in range(8):
                    E.emit("tensor", "matmul", ps,
                           zt[:, S * c + 128 * i: S * c + 128 * (i + 1)],
                           wbf[half2][:, 512 * c:512 * (c + 1)],
                           start=(c == 0), stop=(c == 7),
                           waits=([(s_ev, z_done), (s_w, wo_ld[half2])] + bwp)
                                 if c == 0 else (),
                           inc=(s_pe if c == 7 else None))
                E.emit("vector", "tensor_tensor",
                       otile[:, 512 * half2:512 * (half2 + 1)], ps,
                       xn1t[:, D * jx1 + 512 * half2: D * jx1 + 512 * (half2 + 1)],
                       ALU.add, waits=[(s_pe, E.val(s_pe)), (s_ld, ld_t)], inc=s_ev)
                rps.release(itp, s_ev, E.val(s_ev))
            rxn1.release(ix1, s_ev, E.val(s_ev))
            E.emit("vector", "reduce_sum", mu_[:], otile[:], AX.X)
            E.emit("vector", "tensor_scalar_mul", mu_[:], mu_[:], 1.0 / D)
            E.emit("vector", "tensor_tensor", musq_[:], mu_[:], mu_[:], ALU.mult)
            E.emit("vector", "tensor_scalar_add", musq_[:], musq_[:], -1e-5, inc=s_ev)
            E.emit("vector", "tensor_tensor", junk_[:], otile[:], otile[:], ALU.mult,
                   waits=[(s_ev, E.val(s_ev))])
            E.emit("vector", "reduce_sum", sq_[:], junk_[:], AX.X)
            E.emit("vector", "scalar_tensor_tensor", var_[:], sq_[:], 1.0 / D,
                   musq_[:], ALU.mult, ALU.subtract, inc=s_ev)
            E.emit("scalar", "activation", lnt_[:], var_[:], AF.Ln,
                   waits=[(s_ev, E.val(s_ev))])
            E.emit("scalar", "activation", rs_[:], lnt_[:], AF.Exp, scale=-0.5,
                   inc=s_act)
            jo, bwo, io = ros1.acquire()
            ov = oslot[:, D * jo:D * (jo + 1)]
            E.emit("vector", "tensor_scalar", otile[:], otile[:], mu_[:], rs_[:],
                   ALU.subtract, ALU.mult, waits=[(s_act, E.val(s_act))] + bwo)
            E.emit("vector", "tensor_tensor", otile[:], otile[:], gbc[:], ALU.mult,
                   waits=[(s_ev, gb_ev)])
            E.emit("vector", "tensor_tensor", ov, otile[:], bbc[:], ALU.add, inc=s_ev)
            E.emit("sync", "dma_start",
                   attn1_out[xoff + 128 * i: xoff + 128 * (i + 1), :], ov,
                   waits=[(s_ev, E.val(s_ev))], inc=s_st, incby=16)
            ros1.release(io, s_st, E.val(s_st))

    E.wait("sync", s_st)
    E.finalize()
    return nc


def build_l1_ffn(T):
    S = T // BPC
    NG = S // 512
    NTS = S // 128
    E8 = 8
    F = 2816
    NFC = F // 128
    nc = new_nc()
    dt_in = lambda n, shp, dt=F32: nc.dram_tensor(n, shp, dt, kind="ExternalInput").ap()
    xnfm_in = dt_in("xn_fm", [D, T], BF16)
    x1_in = dt_in("x1", [T, D], BF16)
    attn1_in = dt_in("attn1", [T, D], BF16)
    router_in = dt_in("router", [D, E8])
    w1_in = dt_in("w1e", [E8, D, F]); w3_in = dt_in("w3e", [E8, D, F])
    w2_in = dt_in("w2e", [E8, F, D])
    ws1_in = dt_in("ws1", [D, F]); ws3_in = dt_in("ws3", [D, F])
    ws2_in = dt_in("ws2", [F, D])
    ident_in = dt_in("ident", [128, 128])
    out_out = nc.dram_tensor("out", [T, D], F32, kind="ExternalOutput").ap()

    E = Em(nc)
    sb = lambda n, shp, dt=F32: nc.alloc_sbuf_tensor(n, shp, dt).ap()

    xnfm = sb("xnfm_f1", [128, 8 * S], BF16)
    xg = sb("xg_f1", [128, 8 * S], BF16)
    mid = sb("mid_f1", [128, NFC * S], BF16)
    acc = sb("acc_f1", [128, NTS * D], BF16)
    w13 = sb("w13_f1", [128, 2 * 8 * 128], BF16)
    w2s = sb("w2s_f1", [128, 2 * 512], BF16)
    routerb = sb("router_f1", [128, 8 * E8], BF16)
    identb = sb("ident_f1", [128, 128], BF16)
    scs = sb("scs_f1", [128, E8])
    mx1 = sb("mx1_f1", [128, 1])
    msk8 = sb("msk8_f1", [128, E8])
    sig8 = sb("sig8_f1", [128, E8])
    gates = sb("gates_f1", [128, E8], BF16)
    gatesall = sb("gatesall_f1", [128, E8 * (SEQ // 128)], BF16)
    growsm = sb("growsm_f1", [1, 512], BF16)
    onesrow = sb("onesrow_f1", [1, 512], BF16)
    gbcs = sb("gbcs_f1", [128, 512], BF16)
    silub = sb("silu_f1", [128, 512], BF16)
    t2tile = sb("t2_f1", [128, 2 * D], BF16)
    outf = sb("outf_f1", [128, 2 * D])

    ps_u = nc.alloc_psum_tensor("f1_psu", [128, 512], F32).ap()
    ps_v = nc.alloc_psum_tensor("f1_psv", [128, 512], F32).ap()
    ps_g = nc.alloc_psum_tensor("f1_psg", [128, 512], F32).ap()
    ps_w2 = [nc.alloc_psum_tensor(f"f1_psw{j}", [128, 512], F32).ap() for j in range(4)]
    ps_tr = nc.alloc_psum_tensor("f1_pstr", [128, 128], BF16).ap()

    s_w = E.sem("s_w"); s_ld = E.sem("s_ld")
    s_pe = E.sem("s_pe"); s_ev = E.sem("s_ev"); s_act = E.sem("s_act")
    s_st = E.sem("s_st"); s_hz = E.sem("s_hz")

    rt2 = Rot(2); rout = Rot(2)

    def g_cast(dst, src, waits=()):
        E.emit("gpsimd", "dma_start", dst, src, waits=waits, inc=s_w, incby=16)
        return E.val(s_w)

    w_id = g_cast(identb[:], ident_in)
    E.emit("vector", "memset", onesrow[:], 1.0)
    E.emit("gpsimd", "dma_start", routerb[:].rearrange("p (c n) -> p c n", c=8),
           router_in.rearrange("(c p) n -> p c n", p=128), inc=s_w, incby=16)
    w_rt = E.val(s_w)

    prev_store_ev = 0
    for b in range(BPC):
        xoff = b * S
        for p in range(8):
            E.emit("sync", "dma_start", xnfm[:, S * p:S * (p + 1)],
                   xnfm_in[128 * p:128 * (p + 1), xoff:xoff + S],
                   waits=[(s_pe, E.val(s_pe))], inc=s_ld, incby=16)
        ld_x = E.val(s_ld)
        for i in range(NTS):
            jt, bwt, it = rt2.acquire()
            E.emit("sync", "dma_start", t2tile[:, D * jt:D * (jt + 1)],
                   x1_in[xoff + 128 * i: xoff + 128 * (i + 1), :],
                   waits=bwt, inc=s_ld, incby=16)
            E.emit("gpsimd", "dma_start", acc[:, D * i:D * (i + 1)],
                   attn1_in[xoff + 128 * i: xoff + 128 * (i + 1), :],
                   waits=[(s_ev, prev_store_ev)], inc=s_w, incby=16)
            E.emit("vector", "tensor_tensor", acc[:, D * i:D * (i + 1)],
                   acc[:, D * i:D * (i + 1)], t2tile[:, D * jt:D * (jt + 1)], ALU.add,
                   waits=[(s_ld, E.val(s_ld)), (s_w, E.val(s_w))], inc=s_ev)
            rt2.release(it, s_ev, E.val(s_ev))
        for i in range(NTS):
            for c in range(8):
                E.emit("tensor", "matmul", ps_g[:, 0:E8],
                       xnfm[:, S * c + 128 * i: S * c + 128 * (i + 1)],
                       routerb[:, E8 * c:E8 * (c + 1)],
                       start=(c == 0), stop=(c == 7),
                       waits=[(s_ld, ld_x), (s_w, w_rt),
                              (s_ev, E.val(s_ev))] if c == 0 else (),
                       inc=(s_pe if c == 7 else None))
            E.emit("vector", "tensor_copy", scs[:], ps_g[:, 0:E8],
                   waits=[(s_pe, E.val(s_pe))])
            E.emit("vector", "tensor_reduce", mx1[:], scs[:], AX.X, ALU.max, inc=s_hz)
            E.emit("scalar", "activation", sig8[:], scs[:], AF.Sigmoid,
                   waits=[(s_pe, E.val(s_pe))], inc=s_act)
            E.emit("vector", "tensor_scalar", msk8[:], scs[:], mx1[:], None, ALU.is_ge,
                   waits=[(s_act, E.val(s_act)), (s_hz, E.val(s_hz))])
            E.emit("vector", "tensor_tensor",
                   gatesall[:, E8 * i:E8 * (i + 1)], sig8[:], msk8[:], ALU.mult,
                   inc=s_ev)
        gates_ev = E.val(s_ev)

        for se in range(9):
            routed = se < E8
            if routed:
                for g4 in range(NG):
                    for sub in range(4):
                        ti = 4 * g4 + sub
                        E.emit("tensor", "matmul", ps_w2[sub][0:1, 0:128],
                               gatesall[:, E8 * ti + se:E8 * ti + se + 1], identb[:],
                               start=True, stop=True,
                               waits=[(s_ev, E.val(s_ev)), (s_pe, E.val(s_pe)),
                                      (s_w, w_id)] if sub == 0 else (),
                               inc=(s_pe if sub == 3 else None))
                    E.emit("vector", "tensor_copy", growsm[:, 0:128],
                           ps_w2[0][0:1, 0:128], waits=[(s_pe, E.val(s_pe))])
                    E.emit("vector", "tensor_copy", growsm[:, 128:256],
                           ps_w2[1][0:1, 0:128])
                    E.emit("vector", "tensor_copy", growsm[:, 256:384],
                           ps_w2[2][0:1, 0:128])
                    E.emit("vector", "tensor_copy", growsm[:, 384:512],
                           ps_w2[3][0:1, 0:128], inc=s_ev)
                    E.emit("tensor", "matmul", ps_g[:],
                           onesrow[:, 0:128], growsm[:],
                           start=True, stop=True,
                           waits=[(s_ev, E.val(s_ev))], inc=s_pe)
                    E.emit("vector", "tensor_copy", gbcs[:], ps_g[:],
                           waits=[(s_pe, E.val(s_pe))])
                    for c in range(8):
                        E.emit("vector", "tensor_tensor",
                               xg[:, S * c + 512 * g4: S * c + 512 * (g4 + 1)],
                               xnfm[:, S * c + 512 * g4: S * c + 512 * (g4 + 1)],
                               gbcs[:], ALU.mult,
                               inc=(s_ev if c == 7 else None))
                xg_ev = E.val(s_ev)
                src = xg
            else:
                xg_ev = E.val(s_ev)
                src = xnfm
            for fc in range(NFC):
                E.emit("gpsimd", "dma_start",
                       w13[:, 0:8 * 128].rearrange("p (c n) -> p c n", c=8),
                       (w1_in[se] if routed else ws1_in)[:, 128 * fc:128 * (fc + 1)]
                           .rearrange("(c p) n -> p c n", p=128),
                       waits=[(s_pe, E.val(s_pe))], inc=s_w, incby=16)
                E.emit("gpsimd", "dma_start",
                       w13[:, 8 * 128:16 * 128].rearrange("p (c n) -> p c n", c=8),
                       (w3_in[se] if routed else ws3_in)[:, 128 * fc:128 * (fc + 1)]
                           .rearrange("(c p) n -> p c n", p=128),
                       inc=s_w, incby=16)
                w13_ld = E.val(s_w)
                for g4 in range(NG):
                    for c in range(8):
                        E.emit("tensor", "matmul", ps_u[:],
                               w13[:, 128 * c:128 * (c + 1)],
                               src[:, S * c + 512 * g4: S * c + 512 * (g4 + 1)],
                               start=(c == 0), stop=(c == 7),
                               waits=[(s_w, w13_ld), (s_ev, xg_ev),
                                      (s_act, E.val(s_act)),
                                      (s_ev, E.val(s_ev))] if c == 0 else (),
                               inc=(s_pe if c == 7 else None))
                    E.emit("scalar", "activation", silub[:], ps_u[:], AF.Silu,
                           waits=[(s_pe, E.val(s_pe))], inc=s_act)
                    for c in range(8):
                        E.emit("tensor", "matmul", ps_v[:],
                               w13[:, 8 * 128 + 128 * c:8 * 128 + 128 * (c + 1)],
                               src[:, S * c + 512 * g4: S * c + 512 * (g4 + 1)],
                               start=(c == 0), stop=(c == 7),
                               waits=[(s_ev, E.val(s_ev))] if c == 0 else (),
                               inc=(s_pe if c == 7 else None))
                    E.emit("vector", "tensor_tensor",
                           mid[:, S * fc + 512 * g4: S * fc + 512 * (g4 + 1)],
                           ps_v[:], silub[:], ALU.mult,
                           waits=[(s_pe, E.val(s_pe)), (s_act, E.val(s_act))],
                           inc=s_ev)
            mid_ev = E.val(s_ev)
            for half in range(2):
                for ctg in range(NTS // 4):
                    fc_pe = {}
                    for fc in range(NFC):
                        wv2 = [(s_pe, fc_pe[fc - 2])] if fc >= 2 else [(s_pe, E.val(s_pe))]
                        E.emit("gpsimd", "dma_start",
                               w2s[:, 512 * (fc % 2):512 * (fc % 2 + 1)],
                               (w2_in[se] if routed else ws2_in)
                                   [128 * fc:128 * (fc + 1),
                                    512 * half:512 * (half + 1)],
                               waits=wv2, inc=s_w, incby=16)
                        w2_ld = E.val(s_w)
                        for q in range(4):
                            ct = 4 * ctg + q
                            E.emit("tensor", "matmul", ps_w2[q][:],
                                   mid[:, S * fc + 128 * ct: S * fc + 128 * (ct + 1)],
                                   w2s[:, 512 * (fc % 2):512 * (fc % 2 + 1)],
                                   start=(fc == 0), stop=(fc == NFC - 1),
                                   waits=[(s_w, w2_ld), (s_ev, mid_ev),
                                          (s_ev, E.val(s_ev))]
                                         if q == 0 and fc == 0 else
                                         ([(s_w, w2_ld)] if q == 0 else ()),
                                   inc=(s_pe if q == 3 else None))
                        fc_pe[fc] = E.val(s_pe)
                    w2_pe = E.val(s_pe)
                    for q in range(4):
                        ct = 4 * ctg + q
                        av = acc[:, D * ct + 512 * half: D * ct + 512 * (half + 1)]
                        E.emit("vector", "tensor_tensor", av, av, ps_w2[q][:],
                               ALU.add, waits=[(s_pe, w2_pe)], inc=s_ev)
        fin_ev = E.val(s_ev)
        for i in range(NTS):
            jo, bwo, io = rout.acquire()
            E.emit("vector", "tensor_copy", outf[:, D * jo:D * (jo + 1)],
                   acc[:, D * i:D * (i + 1)], waits=[(s_ev, fin_ev)] + bwo, inc=s_ev)
            E.emit("sync", "dma_start", out_out[xoff + 128 * i: xoff + 128 * (i + 1), :],
                   outf[:, D * jo:D * (jo + 1)], waits=[(s_ev, E.val(s_ev))],
                   inc=s_st, incby=16)
            rout.release(io, s_st, E.val(s_st))
        prev_store_ev = E.val(s_ev)

    E.wait("sync", s_st)
    E.finalize()
    return nc


def host_consts():
    c = {}
    c["ident"] = np.eye(128, dtype=np.float32)
    perm = np.zeros((128, 128), np.float32)
    for m in range(128):
        base = (m // 64) * 64
        r = m % 64
        rho = base + ((r + 32) % 64)
        perm[rho, m] = 1.0
    c["perm"] = perm
    selb = np.zeros((1, 256), np.float32)
    selb[0, 0:64] = 1.0
    selb[0, 128 + 64:256] = 1.0
    c["selb"] = selb
    dm = np.triu(np.ones((128, 128), np.float32))
    c["dmask"] = dm
    t = np.arange(SEQ, dtype=np.float64)
    inv_freq = 1.0 / (10000.0 ** (np.arange(0, DH, 2).astype(np.float64) / DH))
    r = np.arange(128)
    d_in_head = r % 64
    freq_idx = d_in_head % 32
    ang = np.outer(inv_freq[freq_idx], t)
    c["rope_cos"] = np.cos(ang).astype(np.float32)
    sgn = np.where(d_in_head < 32, -1.0, 1.0)[:, None]
    c["rope_ssin"] = (np.sin(ang) * sgn).astype(np.float32)
    bd = np.zeros((128, 128), np.float32)
    bd[:64, :64] = 1.0; bd[64:, 64:] = 1.0
    c["bdmask"] = bd
    ml, tl = np.arange(128)[:, None], np.arange(128)[None, :]
    c["bandm"] = ((ml >= tl) & (ml <= tl + 15)).astype(np.float32)
    rr = np.arange(15)[:, None]
    c["topm"] = (tl >= 113 + rr).astype(np.float32)
    kk = np.arange(16)[:, None, None]
    m0 = ((ml[None] >= tl[None]) & (ml[None] <= tl[None] + 15)
          & (ml[None] >= 15) & ((tl[None] + kk) >= 15)).astype(np.float32)
    c["mask0"] = np.ascontiguousarray(m0.transpose(1, 0, 2).reshape(128, 16 * 128))
    return c


def kernel(**inputs):
    T = BPC * SEQ
    C = lambda a: np.ascontiguousarray(np.asarray(a, dtype=np.float32))
    CT = lambda a: np.ascontiguousarray(np.asarray(a, dtype=np.float32).T)
    consts = host_consts()
    x_full = C(inputs["L0_attn_x"]).reshape(16, SEQ, D)
    xs = [np.ascontiguousarray(x_full[2 * i:2 * i + 2].reshape(T, D))
          for i in range(NCORES)]
    run = lambda nc, maps: run_bass_kernel_spmd(nc, maps, core_ids=list(range(NCORES))).results

    nc = build_ln(T)
    r1 = run(nc, [{"x": xs[i], "ident": consts["ident"]} for i in range(NCORES)])

    w0 = {"WqT": CT(inputs["L0_attn_Wq"]), "WkT": CT(inputs["L0_attn_Wk"]),
          "WvT": CT(inputs["L0_attn_Wv"]), "WoT": CT(inputs["L0_attn_Wo"])}
    nc = build_l0_attn(T)
    r2 = run(nc, [dict(xn_fm=r1[i]["xn_fm"], x=xs[i], perm=consts["perm"],
                       selb=consts["selb"], dmask=consts["dmask"],
                       rope_cos=consts["rope_cos"], rope_ssin=consts["rope_ssin"],
                       **w0) for i in range(NCORES)])

    wf = {"slot_init": C(inputs["L0_ffn_slot_init"]), "w1": C(inputs["L0_ffn_w1"]),
          "b1": C(inputs["L0_ffn_b1"]), "w2": C(inputs["L0_ffn_w2"]),
          "b2": C(inputs["L0_ffn_b2"]), "tau": C(inputs["L0_ffn_tau"]),
          "WihT": CT(inputs["L0_ffn_Wih"]), "WhhT": CT(inputs["L0_ffn_Whh"]),
          "bih": C(inputs["L0_ffn_bih"]), "bhh": C(inputs["L0_ffn_bhh"]),
          "wqs": C(inputs["L0_ffn_wq"]), "bqs": C(inputs["L0_ffn_bq"]),
          "wks": C(inputs["L0_ffn_wk"]), "bks": C(inputs["L0_ffn_bk"]),
          "wvs": C(inputs["L0_ffn_wv"]), "bvs": C(inputs["L0_ffn_bv"]),
          "lng": C(inputs["L0_ffn_ln_g"]), "lnb": C(inputs["L0_ffn_ln_b"]),
          "wvp": C(inputs["L0_ffn_wvp"]), "bvp": C(inputs["L0_ffn_bvp"]),
          "wo": C(inputs["L0_ffn_wo"]), "bo": C(inputs["L0_ffn_bo"]),
          "ident": consts["ident"], "bdmask": consts["bdmask"]}
    nc = build_l0_ffn(T)
    r3 = run(nc, [dict(xraw=xs[i], stats=r1[i]["stats"], xn=r1[i]["xn"],
                       tmp1=r2[i]["tmp1"], **wf) for i in range(NCORES)])

    nc = build_ln(T, in_dtype=BF16)
    r4 = run(nc, [{"x": r3[i]["x1"], "ident": consts["ident"]} for i in range(NCORES)])

    w1a = {"WqT": CT(inputs["L1_attn_Wq"]), "WkT": CT(inputs["L1_attn_Wk"]),
           "WvT": CT(inputs["L1_attn_Wv"]), "WkpT": CT(inputs["L1_attn_Wkp"]),
           "WoT": CT(inputs["L1_attn_Wo"])}
    nc = build_l1_attn(T)
    r5 = run(nc, [dict(xn_fm=r4[i]["xn_fm"], xn=r4[i]["xn"],
                       gamma=C(inputs["L1_attn_gamma"]), beta=C(inputs["L1_attn_beta"]),
                       bandm=consts["bandm"], topm=consts["topm"],
                       mask0=consts["mask0"], **w1a) for i in range(NCORES)])

    wf1 = {"router": C(inputs["L1_ffn_router_DE"]), "w1e": C(inputs["L1_ffn_w1_eDF"]),
           "w3e": C(inputs["L1_ffn_w3_eDF"]), "w2e": C(inputs["L1_ffn_w2_eFD"]),
           "ws1": C(inputs["L1_ffn_ws1_DF"]), "ws3": C(inputs["L1_ffn_ws3_DF"]),
           "ws2": C(inputs["L1_ffn_ws2_FD"]), "ident": consts["ident"]}
    nc = build_l1_ffn(T)
    r6 = run(nc, [dict(xn_fm=r4[i]["xn_fm"], x1=r3[i]["x1"], attn1=r5[i]["attn1"], **wf1)
                  for i in range(NCORES)])

    out = np.empty((16, SEQ, D), dtype=np.float32)
    for i in range(NCORES):
        out[2 * i:2 * i + 2] = np.asarray(r6[i]["out"], np.float32).reshape(2, SEQ, D)
    return out


# version 03
# speedup vs baseline: 1.1614x; 1.1614x over previous
import math
import numpy as np
import concourse.bass as bass
import concourse.mybir as mybir
from concourse.bass_utils import run_bass_kernel_spmd

F32 = mybir.dt.float32
BF16 = mybir.dt.bfloat16
AF = mybir.ActivationFunctionType
ALU = mybir.AluOpType
AX = mybir.AxisListType

D = 1024
SEQ = 2048
BPC = 2
NCORES = 8
H = 16
DH = 64
C_L0 = 28.0
C_L1 = 50.0

class Rec:
    __slots__ = ("meth", "a", "kw", "waits", "incs", "auto_inc", "auto_waits",
                 "hz_val", "_writes", "post")

    def __init__(self, meth, a, kw, waits, incs):
        self.meth = meth
        self.a = a
        self.kw = kw
        self.waits = list(waits)
        self.incs = list(incs)
        self.auto_inc = False
        self.auto_waits = []
        self.hz_val = 0
        self._writes = set()
        self.post = []


def _ap_name(x):
    try:
        return x.tensor.name
    except Exception:
        return None


class Em:
    ENGINES = ("sync", "gpsimd", "tensor", "vector", "scalar")
    AUTO = ("vector", "scalar")
    AUTO_DEPTH = 3

    def __init__(self, nc):
        self.nc = nc
        self.q = {e: [] for e in self.ENGINES}
        self.cnt = {}
        self.auto_sem = {e: self.sem(f"s_auto_{e}") for e in self.AUTO}

    def sem(self, name):
        if not hasattr(self, "_sem_by_name"):
            self._sem_by_name = {}
        if name in self._sem_by_name:
            return self._sem_by_name[name]
        h = self.nc.alloc_semaphore(name)
        self.cnt[h] = 0
        self._sem_by_name[name] = h
        return h

    def barrier(self):
        sbar = self.sem("s_barrier")
        for e in self.ENGINES:
            r = Rec("nop", (), {}, [], [(sbar, 1)])
            self.cnt[sbar] += 1
            r.post = [(sbar, self.cnt[sbar])]
            self.q[e].append(r)
        tgt = self.cnt[sbar]
        for e in self.ENGINES:
            self.wait(e, sbar, tgt)

    def val(self, s):
        return self.cnt[s]

    def emit(self, eng, meth, *a, waits=(), inc=None, incby=1, **kw):
        if inc is not None:
            self.cnt[inc] += incby
        rec = Rec(meth, a, kw, waits, [(inc, incby)] if inc is not None else [])
        if inc is not None:
            rec.post = [(inc, self.cnt[inc])]
        if eng in self.AUTO and meth != "memset":
            writes = set()
            reads = set()
            names = [_ap_name(x) for x in a]
            if kw.get("out") is not None:
                writes.add(_ap_name(kw["out"]))
                reads.update(n for n in (_ap_name(kw.get("in_")),) if n)
            else:
                if names:
                    writes.add(names[0])
                reads.update(n for n in names[1:] if n)
            for k2 in ("bias", "scale", "scalar1", "scalar2", "in_", "in_values",
                       "in_max", "in_to_replace"):
                n = _ap_name(kw.get(k2))
                if n:
                    reads.add(n)
            if kw.get("accum_out") is not None:
                writes.add(_ap_name(kw["accum_out"]))
            writes.discard(None); reads.discard(None)
            prev = [r for r in self.q[eng][-self.AUTO_DEPTH:] if isinstance(r, Rec)]
            for pr in prev:
                if pr._writes and (pr._writes & reads):
                    if pr.post:
                        rec.waits.append(pr.post[0])
                    else:
                        pr.auto_inc = True
                        rec.auto_waits.append(pr)
            rec._writes = writes
        self.q[eng].append(rec)
        return rec

    def wait(self, eng, sem, v=None):
        v = self.cnt[sem] if v is None else v
        r = Rec(None, None, None, [(sem, v)], [])
        r._writes = set()
        self.q[eng].append(r)

    def finalize(self):
        for e in self.AUTO:
            c = 0
            for r in self.q[e]:
                if isinstance(r, Rec) and r.auto_inc:
                    c += 1
                    r.hz_val = c
        with self.nc.allow_low_precision("bf16 outputs within validated end-to-end tolerance"), \
             self.nc.Block() as blk:
            for e in self.ENGINES:
                if self.q[e]:
                    def run_all(eng_obj, _e=e):
                        hsem = self.auto_sem.get(_e)
                        for r in self.q[_e]:
                            if not isinstance(r, Rec):
                                r(eng_obj)
                                continue
                            for (s, v) in r.waits:
                                eng_obj.wait_ge(s, v)
                            for pr in r.auto_waits:
                                eng_obj.wait_ge(hsem, pr.hz_val)
                            if r.meth is None:
                                continue
                            inst = getattr(eng_obj, r.meth)(*r.a, **r.kw)
                            for (s, by) in r.incs:
                                inst.then_inc(s, by)
                            if r.auto_inc:
                                inst.then_inc(hsem, 1)
                    getattr(blk, e)(run_all)


def new_nc():
    return bass.Bass("TRN2", target_bir_lowering=False, detect_race_conditions=False)


def build_ln(T, in_dtype=F32):
    NT = T // 128
    nc = new_nc()
    ext = {
        "x": nc.dram_tensor("x", [T, D], in_dtype, kind="ExternalInput").ap(),
        "ident": nc.dram_tensor("ident", [128, 128], F32, kind="ExternalInput").ap(),
        "xn": nc.dram_tensor("xn", [T, D], BF16, kind="ExternalOutput").ap(),
        "xn_fm": nc.dram_tensor("xn_fm", [D, T], BF16, kind="ExternalOutput").ap(),
        "stats": nc.dram_tensor("stats", [128, 2 * NT], F32, kind="ExternalOutput").ap(),
    }
    E = Em(nc)
    emit_ln(nc, E, T, ext)
    E.finalize()
    return nc


def emit_ln(nc, E, T, ext, ns="", salloc=None, palloc=None):
    NT = T // 128
    x = ext["x"]
    ident_in = ext["ident"]
    xn_out = ext["xn"]
    xnfm_out = ext["xn_fm"]
    stats_out = ext["stats"]
    sbA = salloc or (lambda n, shp, dt=F32: nc.alloc_sbuf_tensor(n, shp, dt).ap())
    paA = palloc or (lambda n, shp, dt=F32: nc.alloc_psum_tensor(n, shp, dt).ap())
    sb = lambda n, shp, dt=F32: sbA(ns + n, shp, dt)

    NSLOT = 4
    xslot = sb("xslot", [128, NSLOT * D], BF16)
    xslot32 = sb("xslot32", [128, 2 * D])
    xn_sb = sb("xn_sb", [128, NT * D], BF16)
    xnfm_sb = sb("xnfm_sb", [128, 8 * T], BF16)
    stats = sb("stats_sb", [128, 2 * NT])
    identb = sb("identb", [128, 128], BF16)
    acc = sb("acc", [128, 1])
    musq = sb("musq", [128, 1])
    sq = sb("sq", [128, 2])
    var = sb("var", [128, 2])
    lntmp = sb("lntmp", [128, 1])
    junk = sb("junk", [128, D])

    ps_tr = [paA(ns + f"ps_tr{i}", [128, 128], BF16) for i in range(2)]

    s_x = E.sem(ns + "s_x"); s_id = E.sem(ns + "s_id"); s_ld32 = E.sem(ns + "s_ld32")
    s_sq = E.sem(ns + "s_sq"); s_var = E.sem(ns + "s_var"); s_rs = E.sem(ns + "s_rs")
    s_xn = E.sem(ns + "s_xn"); s_tr = E.sem(ns + "s_tr"); s_ev = E.sem(ns + "s_ev")
    s_st = E.sem(ns + "s_st")

    E.emit("gpsimd", "dma_start", identb[:], ident_in, inc=s_id, incby=16)

    for i in range(NT):
        slot = xslot[:, (i % NSLOT) * D:(i % NSLOT + 1) * D]
        slot32 = xslot32[:, (i % 2) * D:(i % 2 + 1) * D]
        w = [] if i < NSLOT else [(s_xn, i - NSLOT + 1)]
        E.emit("gpsimd", "dma_start", slot, x[i * 128:(i + 1) * 128, :],
               waits=w, inc=s_x, incby=16)
        w32 = [] if i < 2 else [(s_var, i - 1)]
        E.emit("gpsimd", "dma_start", slot32, x[i * 128:(i + 1) * 128, :],
               waits=w32, inc=s_x, incby=16)
        mu = stats[:, 2 * i:2 * i + 1]
        rstd = stats[:, 2 * i + 1:2 * i + 2]
        E.emit("vector", "reduce_sum", acc[:], slot32, AX.X,
               waits=[(s_x, 32 * (i + 1))])
        E.emit("vector", "tensor_scalar_mul", mu, acc[:], 1.0 / D)
        E.emit("vector", "tensor_tensor", musq[:], mu, mu, ALU.mult)
        E.emit("vector", "tensor_scalar_add", musq[:], musq[:], -1e-5)
        E.emit("vector", "tensor_tensor", junk[:], slot32, slot32, ALU.mult)
        E.emit("vector", "reduce_sum", sq[:, i % 2:i % 2 + 1], junk[:], AX.X,
               inc=s_sq)
        E.emit("vector", "scalar_tensor_tensor", var[:, i % 2:i % 2 + 1],
               sq[:, i % 2:i % 2 + 1], 1.0 / D, musq[:], ALU.mult, ALU.subtract,
               waits=[(s_sq, i + 1), (s_rs, max(0, i - 1))], inc=s_var)
        E.emit("scalar", "activation", lntmp[:], var[:, i % 2:i % 2 + 1], AF.Ln,
               waits=[(s_var, i + 1)])
        E.emit("scalar", "activation", rstd, lntmp[:], AF.Exp, scale=-0.5, inc=s_rs)
        E.emit("vector", "tensor_scalar", xn_sb[:, i * D:(i + 1) * D], slot,
               mu, rstd, ALU.subtract, ALU.mult,
               waits=[(s_rs, i + 1), (s_x, 32 * (i + 1))], inc=s_xn)
        for c in range(8):
            k = 8 * i + c
            w = [(s_xn, i + 1), (s_id, 16)] if c == 0 else []
            if k >= 2:
                w = list(w) + [(s_ev, k - 1)]
            E.emit("tensor", "transpose", ps_tr[k % 2][:],
                   xn_sb[:, i * D + c * 128: i * D + (c + 1) * 128], identb[:],
                   waits=w, inc=s_tr)
            E.emit("vector", "tensor_copy",
                   xnfm_sb[:, c * T + i * 128: c * T + (i + 1) * 128], ps_tr[k % 2][:],
                   waits=[(s_tr, k + 1)], inc=s_ev)
        E.emit("sync", "dma_start", xn_out[i * 128:(i + 1) * 128, :],
               xn_sb[:, i * D:(i + 1) * D], waits=[(s_xn, i + 1)], inc=s_st, incby=16)

    for c in range(8):
        E.emit("sync", "dma_start", xnfm_out[c * 128:(c + 1) * 128, :],
               xnfm_sb[:, c * T:(c + 1) * T], waits=[(s_ev, 8 * NT)], inc=s_st, incby=16)
    E.emit("sync", "dma_start", stats_out, stats[:], waits=[(s_rs, NT)], inc=s_st, incby=16)
    E.wait("sync", s_st)


class Rot:
    def __init__(self, n):
        self.n = n
        self.inst = 0
        self.freed = {}

    def acquire(self):
        i = self.inst
        self.inst += 1
        w = self.freed.get(i - self.n)
        return i % self.n, ([w] if w else []), i

    def release(self, i, sem, cnt):
        self.freed[i] = (sem, cnt)


def build_l0_attn(T):
    S = T // BPC
    nc = new_nc()
    ext = {"xn_fm": nc.dram_tensor("xn_fm", [D, T], BF16, kind="ExternalInput").ap(),
           "x": nc.dram_tensor("x", [T, D], F32, kind="ExternalInput").ap(),
           "perm": nc.dram_tensor("perm", [128, 128], F32, kind="ExternalInput").ap(),
           "selb": nc.dram_tensor("selb", [1, 256], F32, kind="ExternalInput").ap(),
           "dmask": nc.dram_tensor("dmask", [128, 128], F32, kind="ExternalInput").ap(),
           "rope_cos": nc.dram_tensor("rope_cos", [128, S], F32, kind="ExternalInput").ap(),
           "rope_ssin": nc.dram_tensor("rope_ssin", [128, S], F32, kind="ExternalInput").ap(),
           "tmp1": nc.dram_tensor("tmp1", [T, D], BF16, kind="ExternalOutput").ap()}
    for n in ("WqT", "WkT", "WvT", "WoT"):
        ext["l0a_" + n] = nc.dram_tensor(n, [D, D], F32, kind="ExternalInput").ap()
    E = Em(nc)
    emit_l0_attn(nc, E, T, ext)
    E.finalize()
    return nc


def emit_l0_attn(nc, E, T, ext, ns="", salloc=None, palloc=None):
    S = T // BPC
    NG = S // 512
    NKT = S // 128
    xnfm_in = ext["xn_fm"]
    x_in = ext["x"]
    wts = {n: ext["l0a_" + n] for n in ("WqT", "WkT", "WvT", "WoT")}
    perm_in = ext["perm"]; selb_in = ext["selb"]; dmask_in = ext["dmask"]
    cos_in = ext["rope_cos"]; ssin_in = ext["rope_ssin"]
    tmp1_out = ext["tmp1"]
    sbA = salloc or (lambda n, shp, dt=F32: nc.alloc_sbuf_tensor(n, shp, dt).ap())
    paA = palloc or (lambda n, shp, dt=F32: nc.alloc_psum_tensor(n, shp, dt).ap())
    sb = lambda n, shp, dt=F32: sbA(ns + n, shp, dt)

    xnfm = sb("xnfm", [128, 8 * S], BF16)
    zt = xnfm
    qt = sb("qt", [128, 8 * S], BF16)
    kt = sb("kt", [128, 8 * S], BF16)
    VW = 65 * H
    vpad = sb("vpad", [128, NKT * VW], BF16)
    wb = [sb(f"wb{j}", [128, 8 * D], BF16) for j in range(2)]
    cosb = sb("cosb", [128, S], BF16)
    ssinb = sb("ssinb", [128, S], BF16)
    permb = sb("permb", [128, 128], BF16)
    selbb = sb("selbb", [1, 256], BF16)
    dmaskb = sb("dmaskb", [128, 128], BF16)
    expb = [sb(f"expb{j}", [128, 512], BF16) for j in range(2)]
    rab = [sb("rab0", [1, S], BF16), sb("rab1", [1, S], BF16)]
    rt1 = sb("rt1", [128, 512])
    rt2 = sb("rt2", [128, 512])
    negC = sb("negC", [128, 1])
    xres = sb("xres", [128, 4 * D], BF16)
    osb = sb("osb", [128, 4 * D], BF16)

    ps_a = [paA(ns + f"ps_a{j}", [128, 512], F32) for j in range(2)]
    ps_b = [paA(ns + f"ps_b{j}", [128, 512], F32) for j in range(2)]
    ps_c = paA(ns + "ps_c", [128, 512], F32)

    _s = lambda n: E.sem(ns + n)
    s_w = _s("s_w"); s_ld = _s("s_ld")
    s_p1 = _s("s_p1"); s_ev1 = _s("s_ev1")
    s_rpe = _s("s_rpe"); s_rv = _s("s_rv")
    s_sc = _s("s_sc"); s_exp = _s("s_exp"); s_pv = _s("s_pv")
    s_rec = _s("s_rec"); s_zev = _s("s_zev")
    s_nb = _s("s_nb"); s_nrm = _s("s_nrm")
    s_res = _s("s_res"); s_st = _s("s_st"); s_prep = _s("s_prep")

    rpa = Rot(2)
    rexp = Rot(2)
    rpb = Rot(2)
    rpc = Rot(1)
    rxr = Rot(4)
    ros = Rot(4)
    rwb = Rot(2)

    E.emit("gpsimd", "dma_start", permb[:], perm_in, inc=s_w, incby=16)
    E.emit("gpsimd", "dma_start", selbb[:], selb_in, inc=s_w, incby=16)
    E.emit("gpsimd", "dma_start", dmaskb[:], dmask_in, inc=s_w, incby=16)
    E.emit("gpsimd", "dma_start", cosb[:], cos_in, inc=s_w, incby=16)
    E.emit("gpsimd", "dma_start", ssinb[:], ssin_in, inc=s_w, incby=16)
    E.emit("vector", "memset", negC[:], -C_L0, inc=s_prep)
    CONSTS_W = E.val(s_w)

    def kloop(psum, lhsT_slices, rhs_slices, waits_first, inc):
        for p in range(8):
            w = waits_first if p == 0 else ()
            E.emit("tensor", "matmul", psum, lhsT_slices[p], rhs_slices[p],
                   start=(p == 0), stop=(p == 7), waits=w,
                   inc=(inc if p == 7 else None))

    wslice = lambda j, p, a, b: wb[j][:, 1024 * p + a: 1024 * p + b]

    def load_weight(name):
        j, waits, inst = rwb.acquire()
        E.emit("gpsimd", "dma_start", wb[j][:].rearrange("p (c n) -> p c n", c=8),
               wts[name].rearrange("(c p) n -> p c n", p=128),
               waits=waits, inc=s_w, incby=16)
        return j, inst, E.val(s_w)

    res_total = 0
    for b in range(BPC):
        xoff = b * S
        for p in range(8):
            E.emit("sync", "dma_start", xnfm[:, S * p:S * (p + 1)],
                   xnfm_in[128 * p:128 * (p + 1), xoff:xoff + S],
                   waits=[(s_res, res_total)], inc=s_ld, incby=16)
        ld_done = E.val(s_ld)

        jq, iq, wq_ld = load_weight("WqT")
        jk, ik, wk_ld = load_weight("WkT")

        def proj_fm(dst, wj, w_ready):
            for m in range(8):
                for n in range(NG):
                    j, bw, inst = rpa.acquire()
                    waits = [(s_ld, ld_done), (s_w, w_ready)] + bw
                    kloop(ps_a[j][:],
                          [wslice(wj, p, 128 * m, 128 * (m + 1)) for p in range(8)],
                          [xnfm[:, S * p + 512 * n: S * p + 512 * (n + 1)] for p in range(8)],
                          waits, s_p1)
                    E.emit("vector", "tensor_copy",
                           dst[:, S * m + 512 * n: S * m + 512 * (n + 1)], ps_a[j][:],
                           waits=[(s_p1, E.val(s_p1))], inc=s_ev1)
                    rpa.release(inst, s_ev1, E.val(s_ev1))
        proj_fm(qt, jq, wq_ld)
        p1_q_done = E.val(s_p1)
        rwb.release(iq, s_p1, p1_q_done)
        proj_fm(kt, jk, wk_ld)
        p1_k_done = E.val(s_p1)
        rwb.release(ik, s_p1, p1_k_done)

        jv, iv, wv_ld = load_weight("WvT")

        for i in range(NKT):
            for half in range(2):
                j, bw, inst = rpa.acquire()
                waits = [(s_w, wv_ld)] + bw
                kloop(ps_a[j][:],
                      [xnfm[:, S * p + 128 * i: S * p + 128 * (i + 1)] for p in range(8)],
                      [wslice(jv, p, 512 * half, 512 * (half + 1)) for p in range(8)],
                      waits, s_p1)
                dst = bass.AP(tensor=vpad.tensor, offset=i * VW + half * 520,
                              ap=[[NKT * VW, 128], [65, 8], [1, 64]])
                E.emit("vector", "tensor_copy", dst,
                       ps_a[j][:].rearrange("p (h e) -> p h e", h=8),
                       waits=[(s_p1, E.val(s_p1))], inc=s_ev1)
                rpa.release(inst, s_ev1, E.val(s_ev1))
            ones_ap = bass.AP(tensor=vpad.tensor, offset=i * VW + 64,
                              ap=[[NKT * VW, 128], [65, 16], [1, 1]])
            E.emit("vector", "memset", ones_ap, 1.0, inc=s_ev1)
        p1_done = E.val(s_ev1)
        rwb.release(iv, s_p1, E.val(s_p1))

        jo, io, wo_ld = load_weight("WoT")

        for X in (qt, kt):
            for m in range(8):
                for n in range(NG):
                    j2, bw, inst = rpb.acquire()
                    sl = slice(S * m + 512 * n, S * m + 512 * (n + 1))
                    csl = slice(512 * n, 512 * (n + 1))
                    E.emit("tensor", "matmul", ps_b[j2][:], permb[:], X[:, sl],
                           start=True, stop=True,
                           waits=[(s_ev1, p1_done), (s_w, CONSTS_W)] + bw, inc=s_rpe)
                    E.emit("vector", "tensor_tensor", rt1[:], X[:, sl], cosb[:, csl],
                           ALU.mult, waits=[(s_ev1, p1_done), (s_w, CONSTS_W)])
                    E.emit("vector", "tensor_tensor", rt2[:], ps_b[j2][:], ssinb[:, csl],
                           ALU.mult, waits=[(s_rpe, E.val(s_rpe))])
                    E.emit("vector", "tensor_tensor", X[:, sl], rt1[:], rt2[:],
                           ALU.add, inc=s_rv)
                    rpb.release(inst, s_rv, E.val(s_rv))
        rope_done = E.val(s_rv)

        for h in range(16):
            mch = h // 2
            roff = 64 * (h % 2)
            qhead = qt[roff:roff + 64, S * mch: S * (mch + 1)]
            khead = kt[roff:roff + 64, S * mch: S * (mch + 1)]
            for g in range(NG):
                co, bwo, inst_o = rpb.acquire()
                nkt_g = 4 * g + 4
                for ktl in range(nkt_g):
                    cs, bws, inst_s = rpa.acquire()
                    ce, bwe, inst_e = rexp.acquire()
                    w_sc = [(s_rv, rope_done), (s_ev1, p1_done)] + bws
                    E.emit("tensor", "matmul", ps_a[cs][:],
                           khead[:, 128 * ktl:128 * (ktl + 1)],
                           qhead[:, 512 * g:512 * (g + 1)],
                           start=True, stop=True, waits=w_sc, inc=s_sc)
                    w_e = [(s_sc, E.val(s_sc)), (s_prep, 1)] + bwe
                    E.emit("scalar", "activation", expb[ce][:], ps_a[cs][:], AF.Exp,
                           scale=0.125, bias=negC[:], waits=w_e, inc=s_exp)
                    if ktl >= 4 * g:
                        off = 128 * ktl - 512 * g
                        if off > 0:
                            E.emit("vector", "memset", expb[ce][:, 0:off], 0.0,
                                   waits=[(s_exp, E.val(s_exp))], inc=s_exp)
                        E.emit("vector", "tensor_tensor",
                               expb[ce][:, off:off + 128], expb[ce][:, off:off + 128],
                               dmaskb[:], ALU.mult,
                               waits=[(s_exp, E.val(s_exp)), (s_w, CONSTS_W)], inc=s_exp)
                    rpa.release(inst_s, s_exp, E.val(s_exp))
                    vap = bass.AP(tensor=vpad.tensor, offset=ktl * VW + 65 * h,
                                  ap=[[NKT * VW, 128], [1, 65]])
                    w_pv = [(s_exp, E.val(s_exp))] + (bwo if ktl == 0 else [])
                    E.emit("tensor", "matmul", ps_b[co][0:65, :], vap, expb[ce][:],
                           start=(ktl == 0), stop=(ktl == nkt_g - 1),
                           waits=w_pv, inc=s_pv)
                    rexp.release(inst_e, s_pv, E.val(s_pv))
                E.emit("vector", "reciprocal",
                       rab[h % 2][:, 512 * g:512 * (g + 1)],
                       ps_b[co][64:65, :], waits=[(s_pv, E.val(s_pv))], inc=s_rec)
                E.emit("vector", "tensor_copy",
                       zt[roff:roff + 64, S * mch + 512 * g: S * mch + 512 * (g + 1)],
                       ps_b[co][0:64, :], inc=s_zev)
                rpb.release(inst_o, s_zev, E.val(s_zev))
            if h % 2 == 1:
                for g4 in range(NG):
                    jc, bwc, inst_c = rpc.acquire()
                    E.emit("tensor", "matmul", ps_c[:], selbb[:, 0:128],
                           rab[0][:, 512 * g4:512 * (g4 + 1)],
                           start=True, stop=False,
                           waits=[(s_rec, E.val(s_rec)), (s_zev, E.val(s_zev)),
                                  (s_w, CONSTS_W)] + bwc)
                    E.emit("tensor", "matmul", ps_c[:], selbb[:, 128:256],
                           rab[1][:, 512 * g4:512 * (g4 + 1)],
                           start=False, stop=True, inc=s_nb)
                    E.emit("vector", "tensor_tensor",
                           zt[:, S * mch + 512 * g4: S * mch + 512 * (g4 + 1)],
                           zt[:, S * mch + 512 * g4: S * mch + 512 * (g4 + 1)],
                           ps_c[:], ALU.mult,
                           waits=[(s_nb, E.val(s_nb))], inc=s_nrm)
                    rpc.release(inst_c, s_nrm, E.val(s_nrm))
        nrm_done = E.val(s_nrm)

        for i in range(NKT):
            xr, bwx, inst_x = rxr.acquire()
            E.emit("gpsimd", "dma_start", xres[:, D * xr: D * (xr + 1)],
                   x_in[xoff + 128 * i: xoff + 128 * (i + 1), :],
                   waits=bwx, inc=s_w, incby=16)
            xr_ld = E.val(s_w)
            so, bwos, inst_os = ros.acquire()
            for half in range(2):
                j, bw, inst = rpa.acquire()
                waits = [(s_nrm, nrm_done), (s_w, wo_ld)] + bw + (bwos if half == 0 else [])
                kloop(ps_a[j][:],
                      [zt[:, S * p + 128 * i: S * p + 128 * (i + 1)] for p in range(8)],
                      [wslice(jo, p, 512 * half, 512 * (half + 1)) for p in range(8)],
                      waits, s_p1)
                E.emit("vector", "tensor_tensor",
                       osb[:, D * so + 512 * half: D * so + 512 * (half + 1)],
                       ps_a[j][:],
                       xres[:, D * xr + 512 * half: D * xr + 512 * (half + 1)],
                       ALU.add,
                       waits=[(s_p1, E.val(s_p1)), (s_w, xr_ld)], inc=s_res)
                rpa.release(inst, s_res, E.val(s_res))
            rxr.release(inst_x, s_res, E.val(s_res))
            E.emit("sync", "dma_start",
                   tmp1_out[xoff + 128 * i: xoff + 128 * (i + 1), :],
                   osb[:, D * so: D * (so + 1)],
                   waits=[(s_res, E.val(s_res))], inc=s_st, incby=16)
            ros.release(inst_os, s_st, E.val(s_st))
        res_total = E.val(s_res)
        rwb.release(io, s_p1, E.val(s_p1))

    E.wait("sync", s_st)


def build_l0_ffn(T, debug=False):
    S = T // BPC
    NSL = 64
    nc = new_nc()
    ext = {
        "xraw": nc.dram_tensor("xraw", [T, D], F32, kind="ExternalInput").ap(),
        "stats": nc.dram_tensor("stats", [128, 2 * (T // 128)], F32, kind="ExternalInput").ap(),
        "xn": nc.dram_tensor("xn", [T, D], BF16, kind="ExternalInput").ap(),
        "tmp1": nc.dram_tensor("tmp1", [T, D], BF16, kind="ExternalInput").ap(),
        "slot_init": nc.dram_tensor("slot_init", [NSL, D], F32, kind="ExternalInput").ap(),
        "w1": nc.dram_tensor("w1", [2 * D, 512], F32, kind="ExternalInput").ap(),
        "b1": nc.dram_tensor("b1", [512], F32, kind="ExternalInput").ap(),
        "w2": nc.dram_tensor("w2", [512, NSL], F32, kind="ExternalInput").ap(),
        "b2": nc.dram_tensor("b2", [NSL], F32, kind="ExternalInput").ap(),
        "tau": nc.dram_tensor("tau", [1], F32, kind="ExternalInput").ap(),
        "WihT": nc.dram_tensor("WihT", [D, 3 * D], F32, kind="ExternalInput").ap(),
        "WhhT": nc.dram_tensor("WhhT", [D, 3 * D], F32, kind="ExternalInput").ap(),
        "bih": nc.dram_tensor("bih", [3 * D], F32, kind="ExternalInput").ap(),
        "bhh": nc.dram_tensor("bhh", [3 * D], F32, kind="ExternalInput").ap(),
        "wqs": nc.dram_tensor("wqs", [D, 256], F32, kind="ExternalInput").ap(),
        "bqs": nc.dram_tensor("bqs", [256], F32, kind="ExternalInput").ap(),
        "wks": nc.dram_tensor("wks", [D, 256], F32, kind="ExternalInput").ap(),
        "bks": nc.dram_tensor("bks", [256], F32, kind="ExternalInput").ap(),
        "wvs": nc.dram_tensor("wvs", [D, D], F32, kind="ExternalInput").ap(),
        "bvs": nc.dram_tensor("bvs", [D], F32, kind="ExternalInput").ap(),
        "lng": nc.dram_tensor("lng", [D], F32, kind="ExternalInput").ap(),
        "lnb": nc.dram_tensor("lnb", [D], F32, kind="ExternalInput").ap(),
        "wvp": nc.dram_tensor("wvp", [D, D], F32, kind="ExternalInput").ap(),
        "bvp": nc.dram_tensor("bvp", [D], F32, kind="ExternalInput").ap(),
        "wo": nc.dram_tensor("wo", [D, D], F32, kind="ExternalInput").ap(),
        "bo": nc.dram_tensor("bo", [D], F32, kind="ExternalInput").ap(),
        "ident": nc.dram_tensor("ident", [128, 128], F32, kind="ExternalInput").ap(),
        "bdmask": nc.dram_tensor("bdmask", [128, 128], F32, kind="ExternalInput").ap(),
        "x1": nc.dram_tensor("x1", [T, D], BF16, kind="ExternalOutput").ap(),
    }
    dbg = {}
    if debug:
        for nm, shp, dt in (("d_alphaT", [128, T], BF16), ("d_siT", [128, 1024], BF16),
                            ("d_rdcol", [128, 1], F32), ("d_gib", [128, 3 * D], BF16),
                            ("d_ghb", [128, 3 * D], BF16), ("d_snew", [128, D], F32),
                            ("d_svb", [128, D], BF16), ("d_esc", [128, 128], F32),
                            ("d_vb", [128, D], BF16), ("d_ycur", [128, D], F32),
                            ("d_midT", [128, 8 * S], BF16),
                            ("d_ctile", [128, 1], F32), ("d_m8", [128, 8], F32),
                            ("d_e8", [128, 8], F32), ("d_den8", [128, 1], F32),
                            ("d_rden8", [128, 1], F32), ("d_negm1c", [128, 1], F32),
                            ("d_al0", [128, 64], BF16), ("d_lgt0", [128, 64], F32),
                            ("d_attnb", [128, 128], BF16), ("d_dsc", [128, 1], F32),
                            ("d_rdsc", [128, 1], F32), ("d_var64", [128, 1], F32),
                            ("d_nm64", [128, 1], F32), ("d_ypre", [128, D], F32),
                            ("d_rs64", [128, 1], F32), ("d_mu64", [128, 1], F32),
                            ("d_snewb", [128, D], BF16), ("d_attnT", [128, 128], BF16)):
            dbg[nm] = nc.dram_tensor(nm, shp, dt, kind="ExternalOutput").ap()
    E = Em(nc)
    emit_l0_ffn(nc, E, T, ext, debug=debug, dbg=dbg)
    E.finalize()
    return nc


def emit_l0_ffn(nc, E, T, ext, ns="", salloc=None, palloc=None, debug=False, dbg=None):
    S = T // BPC
    NG = S // 512
    NTS = S // 128
    NSL = 64
    xraw_in = ext["xraw"]; stats_in = ext["stats"]; xn_in = ext["xn"]
    tmp1_in = ext["tmp1"]; sinit_in = ext["slot_init"]
    w1_in = ext["w1"]; b1_in = ext["b1"]; w2_in = ext["w2"]; b2_in = ext["b2"]
    tau_in = ext["tau"]; wih_in = ext["WihT"]; whh_in = ext["WhhT"]
    bih_in = ext["bih"]; bhh_in = ext["bhh"]
    wqs_in = ext["wqs"]; bqs_in = ext["bqs"]; wks_in = ext["wks"]; bks_in = ext["bks"]
    wvs_in = ext["wvs"]; bvs_in = ext["bvs"]; lng_in = ext["lng"]; lnb_in = ext["lnb"]
    wvp_in = ext["wvp"]; bvp_in = ext["bvp"]; wo_in = ext["wo"]; bo_in = ext["bo"]
    ident_in = ext["ident"]; bdmask_in = ext["bdmask"]
    x1_out = ext["x1"]
    sbA = salloc or (lambda n, shp, dt=F32: nc.alloc_sbuf_tensor(n, shp, dt).ap())
    paA = palloc or (lambda n, shp, dt=F32: nc.alloc_psum_tensor(n, shp, dt).ap())
    sb = lambda n, shp, dt=F32: sbA(ns + n, shp, dt)

    big0 = sb("big0", [128, 16 * D], BF16)
    wvpb = big0[:, 0:8 * D]
    wob = big0[:, 8 * D:16 * D]
    big1 = sb("big1", [128, 8 * D], BF16)
    big1f = big1[:].bitcast(F32)
    xnf32 = big1f[:, 0:1024]
    xnTf = big1f[:, 1024:2048]
    htile = big1f[:, 2048:2560]
    hTf = big1f[:, 2560:3072]
    wvsb = big1[:, 0:8 * D]
    big2 = sb("big2", [128, 16 * 512], BF16)
    wgih = big2[:, 0:8 * 512]
    wghh = big2[:, 8 * 512:16 * 512]
    midT = sb("midT", [128, 8 * S], BF16)
    w1f = midT[:].bitcast(F32)[:, 0:8 * 512]
    sinitf = w1f[0:NSL, 0:1024]

    wqb = sb("wqb", [128, 8 * 256], BF16)
    wkb = sb("wkb", [128, 8 * 256], BF16)
    identb = sb("identb", [128, 128], BF16)
    identf = sb("identf", [128, 128], F32)
    bdmb = sb("bdmb", [128, 128], BF16)
    sinit2 = sb("sinit2", [128, D], BF16)
    sinit = sinit2[0:NSL, :]
    sinitT2 = sb("sinitT2", [128, 8 * 128], BF16)
    smTf = sb("smTf", [128, 8])
    statsb = sb("statsb", [128, 2 * (T // 128)])
    xrawt = sb("xrawt", [128, D])
    w2f = sb("w2f", [128, 4 * NSL])
    b1f = sb("b1f", [1, 512]); b2f = sb("b2f", [1, NSL])
    onesf = sb("onesf", [1, 128]); onec64f = sb("onec64f", [NSL, 1])
    biashrow = sb("biashrow", [1, 512])
    onesb = sb("onesb", [1, 512], BF16)
    onec128 = sb("onec128", [128, 1], BF16)
    rows = {}
    for nm, ln in (("bih", 3 * D), ("bhh", 3 * D),
                   ("bqs", 256), ("bks", 256), ("bvs", D), ("lng", D), ("lnb", D),
                   ("bvp", D), ("bo", D)):
        rows[nm] = sb("row_" + nm, [1, ln], BF16)
    taur = sb("taur", [1, 1]); taua = sb("taua", [1, 1]); taub2 = sb("taub2", [1, 1])
    ctaub = sb("ctaub", [1, 1], BF16); ctile = sb("ctile", [128, 1])
    lgt = sb("lgt", [128, 2 * NSL])
    albuf = sb("albuf", [128, 2 * 128], BF16)
    alphaT = sb("alphaT", [128, BPC * S], BF16)
    siT = sb("siT", [128, 8 * 128], BF16)
    slin = sb("slin", [NSL, D], BF16)
    rdcol = sb("rdcol", [128, 1]); rdcol2 = sb("rdcol2", [128, 1])
    m8 = sb("m8", [128, 8]); e8 = sb("e8", [128, 8])
    negm1 = sb("negm1", [128, 1]); negm1c = sb("negm1c", [128, 1])
    mask64 = sb("mask64", [128, NSL]); e64 = sb("e64", [128, NSL])
    tmask = sb("tmask", [128, NSL])
    den8 = sb("den8", [128, 1]); rden8 = sb("rden8", [128, 1])
    xntile = sb("xntile", [128, 2 * D], BF16)
    ghb = sb("ghb", [128, 3 * D], BF16)
    gib = sb("gib", [128, 3 * D], BF16)
    gr = sb("gr", [128, D], BF16); gz = sb("gz", [128, D], BF16)
    gn = sb("gn", [128, D], BF16)
    gt1 = sb("gt1", [128, D], BF16); gt2 = sb("gt2", [128, D], BF16)
    snew = sb("snew", [128, D]); snewb = sb("snewb", [128, D], BF16)
    snewT = sb("snewT", [128, 8 * 128], BF16)
    qTb = sb("qTb", [128, 2 * 128], BF16); kTb = sb("kTb", [128, 2 * 128], BF16)
    vb = sb("vb", [128, D], BF16)
    nm64 = sb("nm64", [128, 1]); esc = sb("esc", [128, 128])
    dsc = sb("dsc", [128, 1]); rdsc = sb("rdsc", [128, 1])
    attnb = sb("attnb", [128, 128], BF16); attnTb = sb("attnTb", [128, 128], BF16)
    ycur = snew
    mu64 = sb("mu64", [128, 1]); musq64 = sb("musq64", [128, 1])
    sq64 = sb("sq64", [128, 1]); var64 = sb("var64", [128, 1])
    lnt64 = sb("lnt64", [128, 1]); rs64 = sb("rs64", [128, 1])
    junk64 = None
    ybb = snewb
    ybT = sb("ybT", [128, 8 * 128], BF16)
    svb = sb("svb", [128, D], BF16)
    t1tile = sb("t1tile", [128, 2 * D], BF16)
    x1sb = sb("x1sb", [128, 2 * D], BF16)

    ps1 = [paA(ns + f"f_ps{j}", [128, 512], F32) for j in range(2)]
    ps_si = [paA(ns + f"f_pssi{h}", [NSL, 512], F32) for h in range(2)]
    ps_d = paA(ns + "f_psd", [128, 1], F32)
    ps_tr = [paA(ns + f"f_pstr{j}", [128, 128], BF16) for j in range(2)]
    ps_trf = paA(ns + "f_pstrf", [128, 128], F32)

    _s = lambda n: E.sem(ns + n)
    s_w = _s("s_w"); s_ld = _s("s_ld")
    s_pe = _s("s_pe"); s_ev = _s("s_ev"); s_act = _s("s_act")
    s_hz = _s("s_hz"); s_st = _s("s_st"); s_tr = _s("s_tr")

    r1 = Rot(2); rtr = Rot(2); rtrf = Rot(1)
    rlg = Rot(2); ral = Rot(2); rxn = Rot(2); rt1r = Rot(2); rx1 = Rot(2)

    def g_cast(dst, src, waits=()):
        E.emit("gpsimd", "dma_start", dst, src, waits=waits, inc=s_w, incby=16)
        return E.val(s_w)

    def ld_chunked(dst, src, c, waits=()):
        E.emit("gpsimd", "dma_start", dst.rearrange("p (c n) -> p c n", c=c),
               src.rearrange("(c p) n -> p c n", p=128), waits=waits, inc=s_w, incby=16)
        return E.val(s_w)

    def evac(dst, src_ps, wpe, inc=s_ev, extra=()):
        E.emit("vector", "tensor_copy", dst, src_ps,
               waits=[(s_pe, wpe)] + list(extra), inc=inc)

    w_id = g_cast(identb[:], ident_in)
    w_bd = g_cast(bdmb[:], bdmask_in)
    E.emit("sync", "dma_start", identf[:], ident_in, inc=s_ld, incby=16)
    w_idf = E.val(s_ld)
    w_sinit2a = g_cast(sinit2[0:NSL, :], sinit_in)
    w_sinit2b = g_cast(sinit2[NSL:128, :], sinit_in)
    w_sinit = w_sinit2b
    E.emit("sync", "dma_start", statsb[:], stats_in, inc=s_ld, incby=16)
    E.emit("sync", "dma_start", sinitf, sinit_in, inc=s_ld, incby=16)
    E.emit("sync", "dma_start", w2f[:].rearrange("p (c n) -> p c n", c=4),
           w2_in.rearrange("(c p) n -> p c n", p=128), inc=s_ld, incby=16)
    E.emit("sync", "dma_start", b1f[:], b1_in.unsqueeze(0), inc=s_ld, incby=16)
    E.emit("sync", "dma_start", b2f[:], b2_in.unsqueeze(0), inc=s_ld, incby=16)
    w_f32c = E.val(s_ld)
    for nm, src in (("bih", bih_in), ("bhh", bhh_in),
                    ("bqs", bqs_in), ("bks", bks_in), ("bvs", bvs_in), ("lng", lng_in),
                    ("lnb", lnb_in), ("bvp", bvp_in), ("bo", bo_in)):
        g_cast(rows[nm], src.unsqueeze(0))
    w_rows = E.val(s_w)
    w_wq = ld_chunked(wqb, wqs_in, 8)
    w_wk = ld_chunked(wkb, wks_in, 8)
    E.emit("sync", "dma_start", taur[:], tau_in.unsqueeze(0), inc=s_ld, incby=16)
    w_tau = E.val(s_ld)

    E.emit("vector", "memset", onesb[:], 1.0)
    E.emit("vector", "memset", onec128[:], 1.0)
    E.emit("vector", "memset", alphaT[:], 0.0)
    E.emit("vector", "memset", onesf[:], 1.0)
    E.emit("vector", "memset", onec64f[:], 1.0 / NSL, inc=s_ev)
    prep_ev = E.val(s_ev)

    for c in range(8):
        j, bw, it = rtr.acquire()
        E.emit("tensor", "transpose", ps_tr[j][:],
               sinit2[:, 128 * c:128 * (c + 1)], identb[:],
               waits=[(s_w, max(w_sinit2b, w_id))] + bw, inc=s_tr)
        evac(sinitT2[:, 128 * c:128 * (c + 1)], ps_tr[j][:], 0,
             extra=[(s_tr, E.val(s_tr))])
        rtr.release(it, s_ev, E.val(s_ev))
    for c in range(8):
        j, bw, it = r1.acquire()
        E.emit("tensor", "matmul", ps1[j][0:128, 0:1],
               sinitf[:, 128 * c:128 * (c + 1)], onec64f[:],
               start=True, stop=True,
               waits=[(s_ld, w_f32c), (s_ev, prep_ev)] + bw, inc=s_pe)
        evac(smTf[:, c:c + 1], ps1[j][0:128, 0:1], E.val(s_pe))
        r1.release(it, s_ev, E.val(s_ev))
    E.emit("sync", "dma_start", w1f.rearrange("p (c n) -> p c n", c=8),
           w1_in[D:2 * D, :].rearrange("(c p) n -> p c n", p=128),
           waits=[(s_ev, E.val(s_ev))], inc=s_ld, incby=16)
    w_w1b = E.val(s_ld)
    j, bw, it = r1.acquire()
    for c in range(8):
        E.emit("tensor", "matmul", ps1[j][0:1, :],
               smTf[:, c:c + 1], w1f[:, 512 * c:512 * (c + 1)],
               start=(c == 0), stop=False,
               waits=([(s_ld, w_w1b), (s_ev, E.val(s_ev))] + bw) if c == 0 else ())
    E.emit("tensor", "matmul", ps1[j][0:1, :], onesf[:, 0:1], b1f[:],
           start=False, stop=True, inc=s_pe)
    evac(biashrow[:], ps1[j][0:1, :], E.val(s_pe))
    r1.release(it, s_ev, E.val(s_ev))
    E.emit("sync", "dma_start", w1f.rearrange("p (c n) -> p c n", c=8),
           w1_in[0:D, :].rearrange("(c p) n -> p c n", p=128),
           waits=[(s_pe, E.val(s_pe))], inc=s_ld, incby=16)
    w_w1t = E.val(s_ld)
    E.emit("scalar", "activation", taua[:], taur[:], AF.Abs,
           waits=[(s_ld, w_tau)], inc=s_act)
    E.emit("vector", "tensor_scalar_add", taua[:], taua[:], 0.1,
           waits=[(s_act, E.val(s_act))], inc=s_hz)
    E.emit("vector", "reciprocal", taub2[:], taua[:],
           waits=[(s_hz, E.val(s_hz))])
    E.emit("vector", "tensor_copy", ctaub[:], taub2[:], inc=s_ev)
    j, bw, it = r1.acquire()
    E.emit("tensor", "matmul", ps1[j][0:128, 0:1], onesb[:, 0:128], ctaub[:],
           start=True, stop=True, waits=[(s_ev, E.val(s_ev))] + bw, inc=s_pe)
    evac(ctile[:], ps1[j][0:128, 0:1], E.val(s_pe))
    r1.release(it, s_ev, E.val(s_ev))

    pe_ht_done = 0
    for b in range(BPC):
        xoff = b * S
        for i in range(NTS):
            gidx = b * NTS + i
            E.emit("sync", "dma_start", xrawt[:],
                   xraw_in[xoff + 128 * i: xoff + 128 * (i + 1), :],
                   waits=[(s_ev, E.val(s_ev))], inc=s_ld, incby=16)
            E.emit("vector", "tensor_scalar", xnf32, xrawt[:],
                   statsb[:, 2 * gidx:2 * gidx + 1],
                   statsb[:, 2 * gidx + 1:2 * gidx + 2],
                   ALU.subtract, ALU.mult,
                   waits=[(s_ld, E.val(s_ld)), (s_ld, w_f32c)], inc=s_ev)
            xnf_ev = E.val(s_ev)
            for c in range(8):
                jtf, bwtf, itf = rtrf.acquire()
                E.emit("tensor", "transpose", ps_trf[:],
                       xnf32[:, 128 * c:128 * (c + 1)], identf[:],
                       waits=[(s_ev, xnf_ev), (s_ld, w_idf)] + bwtf, inc=s_tr)
                E.emit("vector", "tensor_copy", xnTf[:, 128 * c:128 * (c + 1)],
                       ps_trf[:], waits=[(s_tr, E.val(s_tr))], inc=s_ev)
                rtrf.release(itf, s_ev, E.val(s_ev))
            xnT_ev = E.val(s_ev)
            j, bw, it = r1.acquire()
            for c in range(8):
                E.emit("tensor", "matmul", ps1[j][:],
                       xnTf[:, 128 * c:128 * (c + 1)], w1f[:, 512 * c:512 * (c + 1)],
                       start=(c == 0), stop=False,
                       waits=([(s_ld, w_w1t), (s_ev, xnT_ev)] + bw) if c == 0 else ())
            E.emit("tensor", "matmul", ps1[j][:], onesf[:], biashrow[:],
                   start=False, stop=True, inc=s_pe)
            E.emit("scalar", "activation", htile, ps1[j][:], AF.Gelu,
                   waits=[(s_pe, E.val(s_pe))], inc=s_act)
            r1.release(it, s_act, E.val(s_act))
            for c in range(4):
                jtf, bwtf, itf = rtrf.acquire()
                E.emit("tensor", "transpose", ps_trf[:],
                       htile[:, 128 * c:128 * (c + 1)], identf[:],
                       waits=[(s_act, E.val(s_act))] + bwtf, inc=s_tr)
                E.emit("vector", "tensor_copy", hTf[:, 128 * c:128 * (c + 1)],
                       ps_trf[:], waits=[(s_tr, E.val(s_tr))], inc=s_ev)
                rtrf.release(itf, s_ev, E.val(s_ev))
            hT_ev = E.val(s_ev)
            jl, bwl, il = rlg.acquire()
            j, bw, it = r1.acquire()
            for c in range(4):
                E.emit("tensor", "matmul", ps1[j][:, 0:NSL],
                       hTf[:, 128 * c:128 * (c + 1)], w2f[:, NSL * c:NSL * (c + 1)],
                       start=(c == 0), stop=False,
                       waits=([(s_ld, w_f32c), (s_ev, hT_ev)] + bw + bwl)
                             if c == 0 else ())
            E.emit("tensor", "matmul", ps1[j][:, 0:NSL], onesf[:], b2f[:],
                   start=False, stop=True, inc=s_pe)
            E.emit("vector", "tensor_copy", lgt[:, NSL * jl:NSL * (jl + 1)],
                   ps1[j][:, 0:NSL], waits=[(s_pe, E.val(s_pe))], inc=s_ev)
            r1.release(it, s_ev, E.val(s_ev))
            lg_ready = E.val(s_ev)
            lgv = lgt[:, NSL * jl:NSL * (jl + 1)]
            jx, bwx, ix = rxn.acquire()
            E.emit("sync", "dma_start", xntile[:, D * jx:D * (jx + 1)],
                   xn_in[xoff + 128 * i: xoff + 128 * (i + 1), :],
                   waits=bwx, inc=s_ld, incby=16)
            ld_xn = E.val(s_ld)
            E.emit("vector", "max", out=m8[:], in_=lgv, waits=[(s_ev, lg_ready)])
            E.emit("vector", "tensor_reduce", negm1[:], lgv, AX.X, ALU.max, negate=True)
            E.emit("vector", "tensor_tensor", negm1c[:], negm1[:], ctile[:], ALU.mult)
            E.emit("vector", "tensor_scalar", mask64[:], lgv, m8[:, 7:8], None,
                   ALU.is_ge, inc=s_ev)
            E.emit("scalar", "activation", e8[:], m8[:], AF.Exp,
                   scale=ctile[:], bias=negm1c[:], waits=[(s_ev, E.val(s_ev))])
            E.emit("scalar", "activation", e64[:], lgv, AF.Exp,
                   scale=ctile[:], bias=negm1c[:], inc=s_act)
            ja, bwa, ia = ral.acquire()
            alv = albuf[:, 128 * ja + 64 * b: 128 * ja + 64 * (b + 1)]
            alz = albuf[:, 128 * ja + 64 * (1 - b): 128 * ja + 64 * (2 - b)]
            alfull = albuf[:, 128 * ja:128 * (ja + 1)]
            E.emit("vector", "memset", alz, 0.0, waits=bwa)
            E.emit("vector", "tensor_reduce", den8[:], e8[:], AX.X, ALU.add,
                   waits=[(s_act, E.val(s_act))], inc=s_hz)
            E.emit("vector", "reciprocal", rden8[:], den8[:],
                   waits=[(s_hz, E.val(s_hz))], inc=s_hz)
            E.emit("vector", "tensor_tensor", tmask[:], e64[:], mask64[:], ALU.mult)
            E.emit("vector", "tensor_scalar_mul", alv, tmask[:], rden8[:],
                   waits=[(s_hz, E.val(s_hz))], inc=s_ev)
            al_ready = E.val(s_ev)
            if debug and b == 0 and i == 0:
                for nm, sap in (("d_ctile", ctile[:]), ("d_m8", m8[:]), ("d_e8", e8[:]),
                                ("d_den8", den8[:]), ("d_rden8", rden8[:]),
                                ("d_negm1c", negm1c[:]), ("d_al0", alv),
                                ("d_lgt0", lgv)):
                    E.emit("sync", "dma_start", dbg[nm], sap,
                           waits=[(s_ev, al_ready)], inc=s_st, incby=16)
            jt, bwt, itr = rtr.acquire()
            E.emit("tensor", "transpose", ps_tr[jt][:], alfull, identb[:],
                   waits=[(s_ev, al_ready), (s_w, w_id)] + bwt, inc=s_tr)
            E.emit("vector", "tensor_copy",
                   alphaT[64 * b:64 * (b + 1), xoff + 128 * i: xoff + 128 * (i + 1)],
                   ps_tr[jt][64 * b:64 * (b + 1), :],
                   waits=[(s_tr, E.val(s_tr))], inc=s_ev)
            rtr.release(itr, s_ev, E.val(s_ev))
            for hh in range(2):
                E.emit("tensor", "matmul", ps_si[hh][:],
                       alv, xntile[:, D * jx + 512 * hh: D * jx + 512 * (hh + 1)],
                       start=(i == 0), stop=(i == NTS - 1),
                       waits=[(s_ev, al_ready), (s_ld, ld_xn)] if hh == 0 else ())
            E.emit("tensor", "matmul", ps_d[:], alfull, onec128[:],
                   start=(b == 0 and i == 0), stop=(b == BPC - 1 and i == NTS - 1),
                   inc=s_pe)
            ral.release(ia, s_pe, E.val(s_pe))
            rxn.release(ix, s_pe, E.val(s_pe))
        si_pe = E.val(s_pe)
        for hh in range(2):
            E.emit("vector", "tensor_copy", slin[:, 512 * hh:512 * (hh + 1)],
                   ps_si[hh][:], waits=[(s_pe, si_pe)], inc=s_ev)
        slin_ev = E.val(s_ev)
        for c in range(8):
            j, bwt, it = rtr.acquire()
            E.emit("tensor", "transpose", ps_tr[j][0:128, 0:NSL],
                   slin[:, 128 * c:128 * (c + 1)], identb[0:NSL, 0:NSL],
                   waits=[(s_ev, slin_ev), (s_w, w_id)] + bwt, inc=s_tr)
            E.emit("vector", "tensor_copy",
                   siT[:, 128 * c + 64 * b: 128 * c + 64 * (b + 1)],
                   ps_tr[j][0:128, 0:NSL], waits=[(s_tr, E.val(s_tr))], inc=s_ev)
            rtr.release(it, s_ev, E.val(s_ev))
    E.emit("vector", "tensor_scalar", rdcol2[:], ps_d[:],
           1e-8, None, ALU.add, waits=[(s_pe, E.val(s_pe))], inc=s_hz)
    E.emit("vector", "reciprocal", rdcol[:], rdcol2[:],
           waits=[(s_hz, E.val(s_hz))], inc=s_hz)
    rd_hz = E.val(s_hz)
    siT_ev = E.val(s_ev)

    pe_gs_done = pe_ht_done
    for gs in range(6):
        wihw = [(s_pe, pe_gs_done)]
        w_ih = ld_chunked(wgih, wih_in[:, 512 * gs:512 * (gs + 1)], 8, waits=wihw)
        w_hh = ld_chunked(wghh, whh_in[:, 512 * gs:512 * (gs + 1)], 8)
        j, bw, it = r1.acquire()
        for c in range(8):
            E.emit("tensor", "matmul", ps1[j][:],
                   sinitT2[:, 128 * c:128 * (c + 1)],
                   wghh[:, 512 * c:512 * (c + 1)],
                   start=(c == 0), stop=False,
                   waits=([(s_w, w_hh), (s_ev, siT_ev)] + bw) if c == 0 else ())
        E.emit("tensor", "matmul", ps1[j][:], onesb[:, 0:128],
               rows["bhh"][:, 512 * gs:512 * (gs + 1)], start=False, stop=True,
               inc=s_pe)
        evac(ghb[:, 512 * gs:512 * (gs + 1)], ps1[j][:], E.val(s_pe))
        r1.release(it, s_ev, E.val(s_ev))
        j, bw, it = r1.acquire()
        for c in range(8):
            E.emit("tensor", "matmul", ps1[j][:],
                   siT[:, 128 * c:128 * (c + 1)],
                   wgih[:, 512 * c:512 * (c + 1)],
                   start=(c == 0), stop=(c == 7),
                   waits=([(s_w, w_ih)] + bw) if c == 0 else (),
                   inc=(s_pe if c == 7 else None))
        E.emit("vector", "tensor_scalar_mul", gib[:, 512 * gs:512 * (gs + 1)],
               ps1[j][:], rdcol[:],
               waits=[(s_pe, E.val(s_pe)), (s_hz, rd_hz)], inc=s_ev)
        r1.release(it, s_ev, E.val(s_ev))
        j, bw, it = r1.acquire()
        E.emit("tensor", "matmul", ps1[j][:], onesb[:, 0:128],
               rows["bih"][:, 512 * gs:512 * (gs + 1)],
               start=True, stop=True, waits=[(s_w, w_rows)] + bw, inc=s_pe)
        E.emit("vector", "tensor_tensor", gib[:, 512 * gs:512 * (gs + 1)],
               gib[:, 512 * gs:512 * (gs + 1)], ps1[j][:], ALU.add,
               waits=[(s_pe, E.val(s_pe))], inc=s_ev)
        r1.release(it, s_ev, E.val(s_ev))
        pe_gs_done = E.val(s_pe)
    gi_done = E.val(s_ev)

    E.emit("vector", "tensor_tensor", gt1[:], gib[:, 0:D], ghb[:, 0:D], ALU.add,
           waits=[(s_ev, gi_done)], inc=s_ev)
    E.emit("vector", "tensor_tensor", gt2[:], gib[:, D:2 * D], ghb[:, D:2 * D],
           ALU.add, inc=s_ev)
    E.emit("scalar", "activation", gr[:], gt1[:], AF.Sigmoid,
           waits=[(s_ev, E.val(s_ev))])
    E.emit("scalar", "activation", gz[:], gt2[:], AF.Sigmoid, inc=s_act)
    E.emit("vector", "tensor_tensor", gt1[:], gr[:], ghb[:, 2 * D:3 * D], ALU.mult,
           waits=[(s_act, E.val(s_act))])
    E.emit("vector", "tensor_tensor", gt1[:], gt1[:], gib[:, 2 * D:3 * D], ALU.add,
           inc=s_ev)
    E.emit("scalar", "activation", gn[:], gt1[:], AF.Tanh,
           waits=[(s_ev, E.val(s_ev))], inc=s_act)
    E.emit("vector", "tensor_tensor", gt2[:], sinit2[:], gn[:], ALU.subtract,
           waits=[(s_act, E.val(s_act)), (s_w, w_sinit2b)])
    E.emit("vector", "tensor_tensor", gt2[:], gz[:], gt2[:], ALU.mult)
    E.emit("vector", "tensor_tensor", snew[:], gn[:], gt2[:], ALU.add)
    E.emit("vector", "tensor_copy", snewb[:], snew[:], inc=s_ev)
    snew_ev = E.val(s_ev)
    if debug:
        E.emit("sync", "dma_start", dbg["d_snewb"], snewb[:],
               waits=[(s_ev, snew_ev)], inc=s_st, incby=16)
    for c in range(8):
        j, bw, it = rtr.acquire()
        E.emit("tensor", "transpose", ps_tr[j][:],
               snewb[:, 128 * c:128 * (c + 1)], identb[:],
               waits=[(s_ev, snew_ev)] + bw, inc=s_tr)
        evac(snewT[:, 128 * c:128 * (c + 1)], ps_tr[j][:], 0,
             extra=[(s_tr, E.val(s_tr))])
        rtr.release(it, s_ev, E.val(s_ev))
    snewT_ev = E.val(s_ev)

    w_wvs = ld_chunked(wvsb, wvs_in, 8, waits=[(s_pe, si_pe)])
    w_wvp = ld_chunked(wvpb, wvp_in, 8, waits=[(s_pe, pe_ht_done)])
    w_wo = ld_chunked(wob, wo_in, 8)

    for (dst, wsrc, w_rdy, brow) in ((qTb, wqb, w_wq, "bqs"), (kTb, wkb, w_wk, "bks")):
        for qc in range(2):
            j, bw, it = r1.acquire()
            for c in range(8):
                E.emit("tensor", "matmul", ps1[j][0:128, 0:128],
                       wsrc[:, 256 * c + 128 * qc: 256 * c + 128 * (qc + 1)],
                       snewT[:, 128 * c:128 * (c + 1)],
                       start=(c == 0), stop=False,
                       waits=([(s_w, w_rdy), (s_ev, snewT_ev)] + bw) if c == 0 else ())
            E.emit("tensor", "matmul", ps1[j][0:128, 0:128],
                   rows[brow][:, 128 * qc:128 * (qc + 1)], onesb[:, 0:128],
                   start=False, stop=True, inc=s_pe)
            evac(dst[:, 128 * qc:128 * (qc + 1)], ps1[j][0:128, 0:128], E.val(s_pe))
            r1.release(it, s_ev, E.val(s_ev))
    for half in range(2):
        j, bw, it = r1.acquire()
        for c in range(8):
            E.emit("tensor", "matmul", ps1[j][:],
                   snewT[:, 128 * c:128 * (c + 1)],
                   wvsb[:, D * c + 512 * half: D * c + 512 * (half + 1)],
                   start=(c == 0), stop=False,
                   waits=([(s_w, w_wvs), (s_ev, snewT_ev)] + bw) if c == 0 else ())
        E.emit("tensor", "matmul", ps1[j][:], onesb[:, 0:128],
               rows["bvs"][:, 512 * half:512 * (half + 1)],
               start=False, stop=True, inc=s_pe)
        evac(vb[:, 512 * half:512 * (half + 1)], ps1[j][:], E.val(s_pe))
        r1.release(it, s_ev, E.val(s_ev))
    qkv_ev = E.val(s_ev)

    j, bw, it = r1.acquire()
    for qc in range(2):
        E.emit("tensor", "matmul", ps1[j][0:128, 0:128],
               qTb[:, 128 * qc:128 * (qc + 1)], kTb[:, 128 * qc:128 * (qc + 1)],
               start=(qc == 0), stop=(qc == 1),
               waits=([(s_ev, qkv_ev)] + bw) if qc == 0 else (),
               inc=(s_pe if qc == 1 else None))
    sc_pe = E.val(s_pe)
    E.emit("vector", "tensor_reduce", nm64[:], ps1[j][0:128, 0:128], AX.X, ALU.max,
           negate=True, waits=[(s_pe, sc_pe)])
    E.emit("vector", "tensor_scalar_mul", nm64[:], nm64[:], 256.0 ** -0.5, inc=s_ev)
    E.emit("scalar", "activation", esc[:], ps1[j][0:128, 0:128], AF.Exp,
           scale=256.0 ** -0.5, bias=nm64[:], waits=[(s_ev, E.val(s_ev))], inc=s_act)
    E.emit("vector", "tensor_tensor", esc[:], esc[:], bdmb[:], ALU.mult,
           waits=[(s_act, E.val(s_act)), (s_w, w_bd)])
    E.emit("vector", "tensor_reduce", dsc[:], esc[:], AX.X, ALU.add, inc=s_hz)
    E.emit("vector", "reciprocal", rdsc[:], dsc[:],
           waits=[(s_hz, E.val(s_hz))], inc=s_hz)
    E.emit("vector", "tensor_scalar_mul", attnb[:], esc[:], rdsc[:],
           waits=[(s_hz, E.val(s_hz))], inc=s_ev)
    r1.release(it, s_ev, E.val(s_ev))
    attn_ev = E.val(s_ev)
    jt, bwt, itr = rtr.acquire()
    E.emit("tensor", "transpose", ps_tr[jt][:], attnb[:], identb[:],
           waits=[(s_ev, attn_ev)] + bwt, inc=s_tr)
    E.emit("vector", "tensor_copy", attnTb[:], ps_tr[jt][:],
           waits=[(s_tr, E.val(s_tr))], inc=s_ev)
    if debug:
        E.emit("sync", "dma_start", dbg["d_attnT"], attnTb[:],
               waits=[(s_ev, E.val(s_ev))], inc=s_st, incby=16)
    rtr.release(itr, s_ev, E.val(s_ev))
    attnT_ev = E.val(s_ev)

    for half in range(2):
        j, bw, it = r1.acquire()
        E.emit("tensor", "matmul", ps1[j][:], attnTb[:],
               vb[:, 512 * half:512 * (half + 1)], start=True, stop=True,
               waits=[(s_ev, attnT_ev)] + bw, inc=s_pe)
        E.emit("vector", "tensor_tensor", ycur[:, 512 * half:512 * (half + 1)],
               snew[:, 512 * half:512 * (half + 1)], ps1[j][:], ALU.add,
               waits=[(s_pe, E.val(s_pe))], inc=s_ev)
        r1.release(it, s_ev, E.val(s_ev))
    y_ev = E.val(s_ev)
    ypre = big2[:, 0:2048].bitcast(F32) if debug else None
    if debug:
        E.emit("vector", "tensor_copy", ypre[:], ycur[:], waits=[(s_ev, y_ev)])
    E.emit("vector", "reduce_sum", mu64[:], ycur[:], AX.X, waits=[(s_ev, y_ev)])
    E.emit("vector", "tensor_scalar_mul", mu64[:], mu64[:], 1.0 / D)
    E.emit("vector", "tensor_tensor", musq64[:], mu64[:], mu64[:], ALU.mult)
    E.emit("vector", "tensor_scalar_add", musq64[:], musq64[:], -1e-5, inc=s_ev)
    E.emit("vector", "tensor_tensor", gt1[:], ycur[:], ycur[:], ALU.mult,
           waits=[(s_ev, E.val(s_ev))])
    E.emit("vector", "reduce_sum", sq64[:], gt1[:], AX.X)
    E.emit("vector", "scalar_tensor_tensor", var64[:], sq64[:], 1.0 / D, musq64[:],
           ALU.mult, ALU.subtract, inc=s_ev)
    E.emit("scalar", "activation", lnt64[:], var64[:], AF.Ln,
           waits=[(s_ev, E.val(s_ev))])
    E.emit("scalar", "activation", rs64[:], lnt64[:], AF.Exp, scale=-0.5, inc=s_act)
    E.emit("vector", "tensor_scalar", ycur[:], ycur[:], mu64[:], rs64[:],
           ALU.subtract, ALU.mult, waits=[(s_act, E.val(s_act))], inc=s_ev)
    for half in range(2):
        j, bw, it = r1.acquire()
        E.emit("tensor", "matmul", ps1[j][:], onesb[:, 0:128],
               rows["lng"][:, 512 * half:512 * (half + 1)], start=True, stop=True,
               waits=([(s_w, w_rows), (s_ev, E.val(s_ev))] + bw), inc=s_pe)
        E.emit("vector", "tensor_tensor", ycur[:, 512 * half:512 * (half + 1)],
               ycur[:, 512 * half:512 * (half + 1)], ps1[j][:], ALU.mult,
               waits=[(s_pe, E.val(s_pe))], inc=s_ev)
        r1.release(it, s_ev, E.val(s_ev))
        j, bw, it = r1.acquire()
        E.emit("tensor", "matmul", ps1[j][:], onesb[:, 0:128],
               rows["lnb"][:, 512 * half:512 * (half + 1)], start=True, stop=True,
               waits=([(s_w, w_rows)] + bw), inc=s_pe)
        E.emit("vector", "tensor_tensor", ycur[:, 512 * half:512 * (half + 1)],
               ycur[:, 512 * half:512 * (half + 1)], ps1[j][:], ALU.add,
               waits=[(s_pe, E.val(s_pe))], inc=s_ev)
        r1.release(it, s_ev, E.val(s_ev))
    E.emit("vector", "tensor_copy", ybb[:], ycur[:], inc=s_ev)
    yb_ev = E.val(s_ev)
    for c in range(8):
        j, bw, it = rtr.acquire()
        E.emit("tensor", "transpose", ps_tr[j][:],
               ybb[:, 128 * c:128 * (c + 1)], identb[:],
               waits=[(s_ev, yb_ev)] + bw, inc=s_tr)
        evac(ybT[:, 128 * c:128 * (c + 1)], ps_tr[j][:], 0,
             extra=[(s_tr, E.val(s_tr))])
        rtr.release(it, s_ev, E.val(s_ev))
    ybT_ev = E.val(s_ev)
    for half in range(2):
        j, bw, it = r1.acquire()
        for c in range(8):
            E.emit("tensor", "matmul", ps1[j][:],
                   ybT[:, 128 * c:128 * (c + 1)],
                   wvpb[:, D * c + 512 * half: D * c + 512 * (half + 1)],
                   start=(c == 0), stop=False,
                   waits=([(s_w, w_wvp), (s_ev, ybT_ev)] + bw) if c == 0 else ())
        E.emit("tensor", "matmul", ps1[j][:], onesb[:, 0:128],
               rows["bvp"][:, 512 * half:512 * (half + 1)],
               start=False, stop=True, inc=s_pe)
        evac(svb[:, 512 * half:512 * (half + 1)], ps1[j][:], E.val(s_pe))
        r1.release(it, s_ev, E.val(s_ev))
    sv_ev = E.val(s_ev)

    for b in range(BPC):
        for c in range(8):
            for g in range(NG):
                j, bw, it = r1.acquire()
                E.emit("tensor", "matmul", ps1[j][:],
                       svb[64 * b:64 * (b + 1), 128 * c:128 * (c + 1)],
                       alphaT[64 * b:64 * (b + 1), b * S + 512 * g: b * S + 512 * (g + 1)],
                       start=True, stop=True,
                       waits=[(s_ev, sv_ev)] + bw, inc=s_pe)
                evac(midT[:, S * c + 512 * g: S * c + 512 * (g + 1)],
                     ps1[j][:], E.val(s_pe))
                r1.release(it, s_ev, E.val(s_ev))
        mid_ev = E.val(s_ev)
        for i in range(NTS):
            jt1, bwt1, it1 = rt1r.acquire()
            E.emit("sync", "dma_start", t1tile[:, D * jt1:D * (jt1 + 1)],
                   tmp1_in[b * S + 128 * i: b * S + 128 * (i + 1), :],
                   waits=bwt1, inc=s_ld, incby=16)
            ld_t1 = E.val(s_ld)
            jx1, bwx1, ix1 = rx1.acquire()
            for half in range(2):
                j, bw, it = r1.acquire()
                for c in range(8):
                    E.emit("tensor", "matmul", ps1[j][:],
                           midT[:, S * c + 128 * i: S * c + 128 * (i + 1)],
                           wob[:, D * c + 512 * half: D * c + 512 * (half + 1)],
                           start=(c == 0), stop=False,
                           waits=([(s_w, w_wo), (s_ev, mid_ev)] + bw
                                  + (bwx1 if half == 0 else [])) if c == 0 else ())
                E.emit("tensor", "matmul", ps1[j][:], onesb[:, 0:128],
                       rows["bo"][:, 512 * half:512 * (half + 1)],
                       start=False, stop=True, inc=s_pe)
                E.emit("vector", "tensor_tensor",
                       x1sb[:, D * jx1 + 512 * half: D * jx1 + 512 * (half + 1)],
                       ps1[j][:],
                       t1tile[:, D * jt1 + 512 * half: D * jt1 + 512 * (half + 1)],
                       ALU.add, waits=[(s_pe, E.val(s_pe)), (s_ld, ld_t1)], inc=s_ev)
                r1.release(it, s_ev, E.val(s_ev))
            rt1r.release(it1, s_ev, E.val(s_ev))
            E.emit("sync", "dma_start", x1_out[b * S + 128 * i: b * S + 128 * (i + 1), :],
                   x1sb[:, D * jx1:D * (jx1 + 1)],
                   waits=[(s_ev, E.val(s_ev))], inc=s_st, incby=16)
            rx1.release(ix1, s_st, E.val(s_st))

    if debug:
        for nm, src_ap in (("d_attnb", attnb[:]), ("d_dsc", dsc[:]), ("d_rdsc", rdsc[:]),
                           ("d_var64", var64[:]), ("d_nm64", nm64[:]), ("d_ypre", ypre[:]),
                           ("d_rs64", rs64[:]), ("d_mu64", mu64[:]),
                           ("d_alphaT", alphaT[:]), ("d_siT", siT[:]), ("d_rdcol", rdcol[:]),
                           ("d_gib", gib[:]), ("d_ghb", ghb[:]), ("d_snew", snew[:]),
                           ("d_svb", svb[:]), ("d_esc", esc[:]), ("d_vb", vb[:]), ("d_ycur", ycur[:]),
                           ("d_midT", midT[:])):
            E.emit("sync", "dma_start", dbg[nm], src_ap, waits=[(s_ev, E.val(s_ev))],
                   inc=s_st, incby=16)
    E.wait("sync", s_st)


def build_l1_attn(T, debug=False):
    S = T // BPC
    W = 16
    nc = new_nc()
    ext = {"xn_fm": nc.dram_tensor("xn_fm", [D, T], BF16, kind="ExternalInput").ap(),
           "xn": nc.dram_tensor("xn", [T, D], BF16, kind="ExternalInput").ap(),
           "gamma": nc.dram_tensor("gamma", [D], F32, kind="ExternalInput").ap(),
           "beta": nc.dram_tensor("beta", [D], F32, kind="ExternalInput").ap(),
           "bandm": nc.dram_tensor("bandm", [128, 128], F32, kind="ExternalInput").ap(),
           "topm": nc.dram_tensor("topm", [15, 128], F32, kind="ExternalInput").ap(),
           "mask0": nc.dram_tensor("mask0", [128, W * 128], F32, kind="ExternalInput").ap(),
           "attn1": nc.dram_tensor("attn1", [T, D], BF16, kind="ExternalOutput").ap()}
    for n in ("WqT", "WkT", "WvT", "WkpT", "WoT"):
        ext["l1a_" + n] = nc.dram_tensor(n, [D, D], F32, kind="ExternalInput").ap()
    dbg = {}
    if debug:
        for nm, shp, dt in (("d_zt", [128, 8 * (T // BPC)], BF16),
                            ("d_expb", [128, 2 * 2048], BF16),
                            ("d_exptop", [15, 2 * 2048], BF16),
                            ("d_tmpc", [128, 2048], BF16),
                            ("d_ztt", [128, 128], F32),
                            ("d_ra", [1, 128], BF16), ("d_rb", [1, 128], BF16),
                            ("d_xb", [128, 2048], BF16)):
            dbg[nm] = nc.dram_tensor(nm, shp, dt, kind="ExternalOutput").ap()
    E = Em(nc)
    emit_l1_attn(nc, E, T, ext, debug=debug, dbg=dbg)
    E.finalize()
    return nc


def emit_l1_attn(nc, E, T, ext, ns="", salloc=None, palloc=None, debug=False, dbg=None):
    S = T // BPC
    NB = S // 128
    W = 16
    xnfm_in = ext["xn_fm"]; xn_in = ext["xn"]
    wts = {n: ext["l1a_" + n] for n in ("WqT", "WkT", "WvT", "WkpT", "WoT")}
    gamma_in = ext["gamma"]; beta_in = ext["beta"]
    bandm_in = ext["bandm"]; topm_in = ext["topm"]; mask0_in = ext["mask0"]
    attn1_out = ext["attn1"]
    sbA = salloc or (lambda n, shp, dt=F32: nc.alloc_sbuf_tensor(n, shp, dt).ap())
    paA = palloc or (lambda n, shp, dt=F32: nc.alloc_psum_tensor(n, shp, dt).ap())
    sb = lambda n, shp, dt=F32: sbA(ns + n, shp, dt)

    HH = 4
    SP = S + 32
    qt = sb("qt", [128, HH * S], BF16)
    kpad = sb("kpad", [128, HH * SP], BF16)
    kppad = sb("kppad", [128, HH * SP], BF16)
    vtpad = sb("vtpad", [128, HH * SP], BF16)
    vw = sb("vw", [128, 17 * 512], BF16)
    zt = sb("zt", [128, 8 * S], BF16)
    xnfm = sb("xnfm_l1", [128, 8 * S], BF16)
    xb = sb("xb", [128, W * 128], BF16)
    expb = sb("expb", [128, 2 * W * 128], BF16)
    exptop = sb("exptop", [15, 2 * W * 128], BF16)
    tmpc = sb("tmpc", [128, W * 128], BF16)
    ztt = sb("ztt", [128, 128])
    rbc_sb = sb("rbc_sb", [128, 128])
    wbf = [sb(f"l1wb{j}", [128, 8 * 512], BF16) for j in range(2)]
    bandb = sb("bandb", [128, 128], BF16)
    topb = sb("topb", [15, 128], BF16)
    mask0b = sb("mask0b", [128, W * 128], BF16)
    negC5 = sb("negC5", [128, 1])
    onesc = sb("onesc_l1", [128, 1], BF16)
    selrow = sb("selrow", [1, 256], BF16)
    ra = sb("ra_l1", [1, 128], BF16); rbr = sb("rb_l1", [1, 128], BF16)
    gbc = sb("gbc", [128, D], BF16); bbc = sb("bbc", [128, D], BF16)
    grow = sb("grow", [1, D], BF16); brow = sb("brow", [1, D], BF16)
    onesr = sb("onesr_l1", [1, 128], BF16)
    xn1t = sb("xn1t", [128, 2 * D], BF16)
    otile = tmpc[:].bitcast(F32)
    oslot = sb("oslot", [128, D], BF16)
    vtmp = sb("vtmp", [128, 512], BF16)
    mu_ = sb("mu_l1", [128, 1]); musq_ = sb("musq_l1", [128, 1])
    sq_ = sb("sq_l1", [128, 1]); var_ = sb("var_l1", [128, 1])
    lnt_ = sb("lnt_l1", [128, 1]); rs_ = sb("rs_l1", [128, 1])
    junk_ = xb[:].bitcast(F32)

    PS0 = paA(ns + "l1_ps0", [128, 2048], F32)
    PS1 = paA(ns + "l1_ps1", [128, 2048], F32)

    _s = lambda n: E.sem(ns + n)
    s_w = _s("s_w"); s_ld = _s("s_ld")
    s_pe = _s("s_pe"); s_ev = _s("s_ev"); s_act = _s("s_act")
    s_hz = _s("s_hz"); s_st = _s("s_st")

    rxn1 = Rot(2); ros1 = Rot(1)

    def g_cast(dst, src, waits=()):
        E.emit("gpsimd", "dma_start", dst, src, waits=waits, inc=s_w, incby=16)
        return E.val(s_w)

    w_bm = g_cast(bandb[:], bandm_in)
    w_tm = g_cast(topb[:], topm_in)
    w_m0 = g_cast(mask0b[:], mask0_in)
    w_g = g_cast(grow[:], gamma_in.unsqueeze(0))
    w_b = g_cast(brow[:], beta_in.unsqueeze(0))
    E.emit("vector", "memset", negC5[:], -C_L1)
    E.emit("vector", "memset", onesc[:], 1.0)
    E.emit("vector", "memset", onesr[:], 1.0)
    E.emit("vector", "memset", selrow[:, 0:64], 1.0)
    E.emit("vector", "memset", selrow[:, 64:192], 0.0)
    E.emit("vector", "memset", selrow[:, 192:256], 1.0)
    for buf in (kpad, kppad, vtpad):
        for cch in range(HH):
            E.emit("vector", "memset", buf[:, SP * cch: SP * cch + 15], 0.0)
            E.emit("vector", "memset", buf[:, SP * cch + 15 + S: SP * (cch + 1)], 0.0)
    E.emit("vector", "memset", vw[:, 16 * 512:17 * 512], 0.0, inc=s_ev)
    prep_ev = E.val(s_ev)
    for half in range(2):
        E.emit("tensor", "matmul", PS0[:, 512 * half:512 * (half + 1)],
               onesr[:], grow[:, 512 * half:512 * (half + 1)],
               start=True, stop=True,
               waits=[(s_w, w_g), (s_ev, prep_ev)], inc=s_pe)
        E.emit("tensor", "matmul", PS1[:, 512 * half:512 * (half + 1)],
               onesr[:], brow[:, 512 * half:512 * (half + 1)],
               start=True, stop=True, waits=[(s_w, w_b)], inc=s_pe)
        E.emit("vector", "tensor_copy", gbc[:, 512 * half:512 * (half + 1)],
               PS0[:, 512 * half:512 * (half + 1)], waits=[(s_pe, E.val(s_pe))])
        E.emit("vector", "tensor_copy", bbc[:, 512 * half:512 * (half + 1)],
               PS1[:, 512 * half:512 * (half + 1)], inc=s_ev)
    gb_ev = E.val(s_ev)

    PSP = [PS0[:, 512 * q:512 * (q + 1)] for q in range(4)]
    rps = Rot(4)
    qrowstride = qt.ap[0][0]
    kprowstride = kppad.ap[0][0]
    vtrowstride = vtpad.ap[0][0]
    vwrowstride = vw.ap[0][0]
    tmrowstride = tmpc.ap[0][0]

    st_count = 0
    for b in range(BPC):
        xoff = b * S
        for p in range(8):
            E.emit("sync", "dma_start", xnfm[:, S * p:S * (p + 1)],
                   xnfm_in[128 * p:128 * (p + 1), xoff:xoff + S],
                   waits=[(s_pe, E.val(s_pe))], inc=s_ld, incby=16)
        ld_x = E.val(s_ld)
        for half in range(2):
            coff = 4 * half

            def ldw(name, j):
                E.emit("gpsimd", "dma_start",
                       wbf[j][:].rearrange("p (c n) -> p c n", c=8),
                       wts[name][:, 512 * half:512 * (half + 1)]
                           .rearrange("(c p) n -> p c n", p=128),
                       waits=[(s_pe, E.val(s_pe))], inc=s_w, incby=16)
                return E.val(s_w)

            def proj_fm(dst, wj, w_rdy, dst_off, dst_stride):
                for m in range(HH):
                    for g in range(S // 512):
                        jj, bwp, itp = rps.acquire()
                        ps = PSP[jj]
                        for c in range(8):
                            E.emit("tensor", "matmul", ps,
                                   wbf[wj][:, 512 * c + 128 * m: 512 * c + 128 * (m + 1)],
                                   xnfm[:, S * c + 512 * g: S * c + 512 * (g + 1)],
                                   start=(c == 0), stop=(c == 7),
                                   waits=([(s_ld, ld_x), (s_w, w_rdy),
                                           (s_ev, E.val(s_ev))] + bwp) if c == 0 else (),
                                   inc=(s_pe if c == 7 else None))
                        E.emit("vector", "tensor_copy",
                               dst[:, dst_stride * m + dst_off + 512 * g:
                                   dst_stride * m + dst_off + 512 * (g + 1)],
                               ps, waits=[(s_pe, E.val(s_pe))], inc=s_ev)
                        rps.release(itp, s_ev, E.val(s_ev))

            wq_ld = ldw("WqT", 0)
            proj_fm(qt, 0, wq_ld, 0, S)
            wk_ld = ldw("WkT", 1)
            proj_fm(kpad, 1, wk_ld, 15, SP)
            wkp_ld = ldw("WkpT", 0)
            proj_fm(kppad, 0, wkp_ld, 15, SP)
            wv_ld = ldw("WvT", 1)
            proj_fm(vtpad, 1, wv_ld, 15, SP)
            for i in range(NB + 1):
                jj, bwp, itp = rps.acquire()
                ps = PSP[jj]
                lo = 128 * i - 15
                if i == 0:
                    m_sz, src_lo = 113, 0
                elif i == NB:
                    m_sz, src_lo = 15, S - 15
                else:
                    m_sz, src_lo = 128, lo
                for c in range(8):
                    E.emit("tensor", "matmul", ps[0:m_sz, :],
                           xnfm[:, S * c + src_lo: S * c + src_lo + m_sz],
                           wbf[1][:, 512 * c:512 * (c + 1)],
                           start=(c == 0), stop=(c == 7),
                           waits=([(s_w, wv_ld), (s_ev, E.val(s_ev))] + bwp) if c == 0 else (),
                           inc=(s_pe if c == 7 else None))
                if i == 0:
                    E.emit("vector", "tensor_copy", vtmp[0:113, :], ps[0:113, :],
                           waits=[(s_pe, E.val(s_pe))], inc=s_ev)
                    E.emit("gpsimd", "dma_start", vw[15:128, 0:512], vtmp[0:113, :],
                           waits=[(s_ev, E.val(s_ev))], inc=s_w, incby=16)
                    E.emit("vector", "memset", vw[0:15, 0:512], 0.0, inc=s_ev)
                else:
                    E.emit("vector", "tensor_copy", vw[0:m_sz, 512 * i:512 * i + 512],
                           ps[0:m_sz, :], waits=[(s_pe, E.val(s_pe))], inc=s_ev)
                rps.release(itp, s_ev, E.val(s_ev))
            vw_dma = E.val(s_w)
            proj_ev = E.val(s_ev)

            for m in range(HH):
                gch = coff + m
                for blk in range(NB):
                    s0 = 128 * blk
                    q_b = bass.AP(tensor=qt.tensor, offset=qt.offset + S * m + s0,
                                  ap=[[qrowstride, 128], [0, W], [1, 128]])
                    kp_b = bass.AP(tensor=kppad.tensor,
                                   offset=kppad.offset + SP * m + s0,
                                   ap=[[kprowstride, 128], [1, W], [1, 128]])
                    E.emit("vector", "tensor_tensor",
                           xb[:, 0:2048].rearrange("p (k t) -> p k t", k=W),
                           q_b, kp_b, ALU.mult,
                           waits=[(s_ev, proj_ev), (s_pe, E.val(s_pe))], inc=s_ev)
                    xb_ev = E.val(s_ev)
                    for hh2 in range(2):
                        roff = 64 * hh2
                        for q4 in range(4):
                            aw = ()
                            if q4 == 0:
                                aw = ([(s_ev, xb_ev)] if hh2 == 0
                                      else [(s_act, E.val(s_act))])
                            E.emit("tensor", "matmul", PS0[:, 512 * q4:512 * (q4 + 1)],
                                   kpad[roff:roff + 64, SP * m + s0: SP * m + s0 + 128],
                                   xb[roff:roff + 64, 512 * q4:512 * (q4 + 1)],
                                   start=True, stop=True,
                                   waits=aw)
                            E.emit("tensor", "matmul", PS1[0:15, 512 * q4:512 * (q4 + 1)],
                                   kpad[roff:roff + 64, SP * m + s0 + 128: SP * m + s0 + 143],
                                   xb[roff:roff + 64, 512 * q4:512 * (q4 + 1)],
                                   start=True, stop=True,
                                   inc=(s_pe if q4 == 3 else None))
                        E.emit("scalar", "activation", expb[:, 2048 * hh2:2048 * (hh2 + 1)],
                               PS0[:], AF.Exp, scale=0.125, bias=negC5[:],
                               waits=[(s_pe, E.val(s_pe)), (s_ev, prep_ev)])
                        E.emit("scalar", "activation",
                               exptop[:, 2048 * hh2:2048 * (hh2 + 1)],
                               PS1[0:15, :], AF.Exp, scale=0.125, bias=negC5[0:15, :],
                               inc=s_act)
                        mv = expb[:, 2048 * hh2:2048 * (hh2 + 1)]
                        if blk == 0:
                            E.emit("vector", "tensor_tensor", mv, mv, mask0b[:],
                                   ALU.mult, waits=[(s_act, E.val(s_act)), (s_w, w_m0)])
                        else:
                            E.emit("vector", "tensor_tensor",
                                   mv.rearrange("p (k t) -> p k t", k=W),
                                   mv.rearrange("p (k t) -> p k t", k=W),
                                   bandb[:].unsqueeze(1).to_broadcast([128, W, 128]),
                                   ALU.mult, waits=[(s_act, E.val(s_act)), (s_w, w_bm)])
                        tv = exptop[:, 2048 * hh2:2048 * (hh2 + 1)]
                        E.emit("vector", "tensor_tensor",
                               tv.rearrange("p (k t) -> p k t", k=W),
                               tv.rearrange("p (k t) -> p k t", k=W),
                               topb[:].unsqueeze(1).to_broadcast([15, W, 128]),
                               ALU.mult, waits=[(s_w, w_tm)], inc=s_ev)
                    mexp_ev = E.val(s_ev)
                    for hh2 in range(2):
                        dcol = 512 * hh2
                        for k in range(W):
                            E.emit("tensor", "matmul", PS0[0:1, dcol:dcol + 128],
                                   onesc[:], expb[:, 2048 * hh2 + 128 * k:
                                                  2048 * hh2 + 128 * (k + 1)],
                                   start=(k == 0), stop=False,
                                   waits=[(s_ev, mexp_ev)] if k == 0 and hh2 == 0 else ())
                            E.emit("tensor", "matmul", PS0[0:1, dcol:dcol + 128],
                                   onesc[0:15, :], exptop[0:15, 2048 * hh2 + 128 * k:
                                                          2048 * hh2 + 128 * (k + 1)],
                                   start=False, stop=(k == W - 1),
                                   inc=(s_pe if k == W - 1 else None))
                    d_pe = E.val(s_pe)
                    E.emit("vector", "reciprocal", ra[:], PS0[0:1, 0:128],
                           waits=[(s_pe, d_pe)])
                    E.emit("vector", "reciprocal", rbr[:], PS0[0:1, 512:640], inc=s_ev)
                    E.emit("tensor", "matmul", PS0[:, 1024:1152], selrow[:, 0:128],
                           ra[:], start=True, stop=False,
                           waits=[(s_ev, E.val(s_ev))])
                    E.emit("tensor", "matmul", PS0[:, 1024:1152], selrow[:, 128:256],
                           rbr[:], start=False, stop=True, inc=s_pe)
                    E.emit("vector", "tensor_copy", rbc_sb[:], PS0[:, 1024:1152],
                           waits=[(s_pe, E.val(s_pe))], inc=s_ev)
                    dr_ev = E.val(s_ev)
                    for hh2 in range(2):
                        roff = 64 * hh2
                        WPS = PS0 if hh2 == 0 else PS1
                        vcol = 64 * (2 * m + hh2)
                        vap_m = bass.AP(tensor=vw.tensor,
                                        offset=vw.offset + 512 * blk + vcol,
                                        ap=[[vwrowstride, 128], [1, 64]])
                        vap_t = bass.AP(tensor=vw.tensor,
                                        offset=vw.offset + 512 * (blk + 1) + vcol,
                                        ap=[[vwrowstride, 15], [1, 64]])
                        for q4 in range(4):
                            E.emit("tensor", "matmul",
                                   WPS[roff:roff + 64, 512 * q4:512 * (q4 + 1)],
                                   vap_m, expb[:, 2048 * hh2 + 512 * q4:
                                               2048 * hh2 + 512 * (q4 + 1)],
                                   start=True, stop=False,
                                   waits=[(s_ev, dr_ev), (s_w, vw_dma)]
                                         if q4 == 0 and hh2 == 0 else ())
                            E.emit("tensor", "matmul",
                                   WPS[roff:roff + 64, 512 * q4:512 * (q4 + 1)],
                                   vap_t, exptop[0:15, 2048 * hh2 + 512 * q4:
                                                 2048 * hh2 + 512 * (q4 + 1)],
                                   start=False, stop=True,
                                   inc=(s_pe if q4 == 3 and hh2 == 1 else None))
                    wd_pe = E.val(s_pe)
                    for hh2 in range(2):
                        roff = 64 * hh2
                        WPS = PS0 if hh2 == 0 else PS1
                        vt_b = bass.AP(tensor=vtpad.tensor,
                                       offset=vtpad.offset + vtrowstride * roff
                                              + SP * m + s0,
                                       ap=[[vtrowstride, 64], [1, W], [1, 128]])
                        E.emit("vector", "tensor_tensor",
                               tmpc[roff:roff + 64, :].rearrange("p (k t) -> p k t", k=W),
                               WPS[roff:roff + 64, :].rearrange("p (k t) -> p k t", k=W),
                               vt_b, ALU.mult,
                               waits=[(s_pe, wd_pe)] if hh2 == 0 else ())
                    red_in = bass.AP(tensor=tmpc.tensor, offset=tmpc.offset,
                                     ap=[[tmrowstride, 128], [1, 128], [128, W]])
                    E.emit("vector", "tensor_reduce", ztt[:].unsqueeze(2),
                           red_in, AX.X, ALU.add)
                    E.emit("vector", "tensor_tensor",
                           zt[:, S * gch + s0: S * gch + s0 + 128],
                           ztt[:], rbc_sb[:], ALU.mult, inc=s_ev)
        z_done = E.val(s_ev)
        if debug and b == 0:
            for nm, src_ap in (("d_zt", zt[:]), ("d_expb", expb[:]),
                               ("d_exptop", exptop[:]), ("d_tmpc", tmpc[:]),
                               ("d_ztt", ztt[:]), ("d_ra", ra[:]), ("d_rb", rbr[:]),
                               ("d_xb", xb[:, 0:2048])):
                E.emit("sync", "dma_start", dbg[nm], src_ap,
                       waits=[(s_ev, z_done)], inc=s_st, incby=16)
        wo_ld = []
        for j in range(2):
            E.emit("gpsimd", "dma_start",
                   wbf[j][:].rearrange("p (c n) -> p c n", c=8),
                   wts["WoT"][:, 512 * j:512 * (j + 1)].rearrange("(c p) n -> p c n", p=128),
                   waits=[(s_pe, E.val(s_pe))], inc=s_w, incby=16)
            wo_ld.append(E.val(s_w))
        for i in range(NB):
            jx1, bwx1, ix1 = rxn1.acquire()
            E.emit("sync", "dma_start", xn1t[:, D * jx1:D * (jx1 + 1)],
                   xn_in[xoff + 128 * i: xoff + 128 * (i + 1), :],
                   waits=bwx1, inc=s_ld, incby=16)
            ld_t = E.val(s_ld)
            for half2 in range(2):
                jj, bwp, itp = rps.acquire()
                ps = PSP[jj]
                for c in range(8):
                    E.emit("tensor", "matmul", ps,
                           zt[:, S * c + 128 * i: S * c + 128 * (i + 1)],
                           wbf[half2][:, 512 * c:512 * (c + 1)],
                           start=(c == 0), stop=(c == 7),
                           waits=([(s_ev, z_done), (s_w, wo_ld[half2])] + bwp)
                                 if c == 0 else (),
                           inc=(s_pe if c == 7 else None))
                E.emit("vector", "tensor_tensor",
                       otile[:, 512 * half2:512 * (half2 + 1)], ps,
                       xn1t[:, D * jx1 + 512 * half2: D * jx1 + 512 * (half2 + 1)],
                       ALU.add, waits=[(s_pe, E.val(s_pe)), (s_ld, ld_t)], inc=s_ev)
                rps.release(itp, s_ev, E.val(s_ev))
            rxn1.release(ix1, s_ev, E.val(s_ev))
            E.emit("vector", "reduce_sum", mu_[:], otile[:], AX.X)
            E.emit("vector", "tensor_scalar_mul", mu_[:], mu_[:], 1.0 / D)
            E.emit("vector", "tensor_tensor", musq_[:], mu_[:], mu_[:], ALU.mult)
            E.emit("vector", "tensor_scalar_add", musq_[:], musq_[:], -1e-5, inc=s_ev)
            E.emit("vector", "tensor_tensor", junk_[:], otile[:], otile[:], ALU.mult,
                   waits=[(s_ev, E.val(s_ev))])
            E.emit("vector", "reduce_sum", sq_[:], junk_[:], AX.X)
            E.emit("vector", "scalar_tensor_tensor", var_[:], sq_[:], 1.0 / D,
                   musq_[:], ALU.mult, ALU.subtract, inc=s_ev)
            E.emit("scalar", "activation", lnt_[:], var_[:], AF.Ln,
                   waits=[(s_ev, E.val(s_ev))])
            E.emit("scalar", "activation", rs_[:], lnt_[:], AF.Exp, scale=-0.5,
                   inc=s_act)
            jo, bwo, io = ros1.acquire()
            ov = oslot[:, D * jo:D * (jo + 1)]
            E.emit("vector", "tensor_scalar", otile[:], otile[:], mu_[:], rs_[:],
                   ALU.subtract, ALU.mult, waits=[(s_act, E.val(s_act))] + bwo)
            E.emit("vector", "tensor_tensor", otile[:], otile[:], gbc[:], ALU.mult,
                   waits=[(s_ev, gb_ev)])
            E.emit("vector", "tensor_tensor", ov, otile[:], bbc[:], ALU.add, inc=s_ev)
            E.emit("sync", "dma_start",
                   attn1_out[xoff + 128 * i: xoff + 128 * (i + 1), :], ov,
                   waits=[(s_ev, E.val(s_ev))], inc=s_st, incby=16)
            ros1.release(io, s_st, E.val(s_st))

    E.wait("sync", s_st)


def build_l1_ffn(T):
    F = 2816
    E8 = 8
    nc = new_nc()
    ext = {"xn_fm": nc.dram_tensor("xn_fm", [D, T], BF16, kind="ExternalInput").ap(),
           "x1": nc.dram_tensor("x1", [T, D], BF16, kind="ExternalInput").ap(),
           "attn1": nc.dram_tensor("attn1", [T, D], BF16, kind="ExternalInput").ap(),
           "router": nc.dram_tensor("router", [D, E8], F32, kind="ExternalInput").ap(),
           "w1e": nc.dram_tensor("w1e", [E8, D, F], F32, kind="ExternalInput").ap(),
           "w3e": nc.dram_tensor("w3e", [E8, D, F], F32, kind="ExternalInput").ap(),
           "w2e": nc.dram_tensor("w2e", [E8, F, D], F32, kind="ExternalInput").ap(),
           "ws1": nc.dram_tensor("ws1", [D, F], F32, kind="ExternalInput").ap(),
           "ws3": nc.dram_tensor("ws3", [D, F], F32, kind="ExternalInput").ap(),
           "ws2": nc.dram_tensor("ws2", [F, D], F32, kind="ExternalInput").ap(),
           "ident": nc.dram_tensor("ident", [128, 128], F32, kind="ExternalInput").ap(),
           "out": nc.dram_tensor("out", [T, D], F32, kind="ExternalOutput").ap()}
    E = Em(nc)
    emit_l1_ffn(nc, E, T, ext)
    E.finalize()
    return nc


def emit_l1_ffn(nc, E, T, ext, ns="", salloc=None, palloc=None):
    S = T // BPC
    NG = S // 512
    NTS = S // 128
    E8 = 8
    F = 2816
    NFC = F // 128
    xnfm_in = ext["xn_fm"]; x1_in = ext["x1"]; attn1_in = ext["attn1"]
    router_in = ext["router"]
    w1_in = ext["w1e"]; w3_in = ext["w3e"]; w2_in = ext["w2e"]
    ws1_in = ext["ws1"]; ws3_in = ext["ws3"]; ws2_in = ext["ws2"]
    ident_in = ext["ident"]
    out_out = ext["out"]
    sbA = salloc or (lambda n, shp, dt=F32: nc.alloc_sbuf_tensor(n, shp, dt).ap())
    paA = palloc or (lambda n, shp, dt=F32: nc.alloc_psum_tensor(n, shp, dt).ap())
    sb = lambda n, shp, dt=F32: sbA(ns + n, shp, dt)

    xnfm = sb("xnfm_f1", [128, 8 * S], BF16)
    xg = sb("xg_f1", [128, 8 * S], BF16)
    mid = sb("mid_f1", [128, NFC * S], BF16)
    acc = sb("acc_f1", [128, NTS * D], BF16)
    w13 = sb("w13_f1", [128, 2 * 8 * 128], BF16)
    w2s = sb("w2s_f1", [128, 2 * 512], BF16)
    routerb = sb("router_f1", [128, 8 * E8], BF16)
    identb = sb("ident_f1", [128, 128], BF16)
    scs = sb("scs_f1", [128, E8])
    mx1 = sb("mx1_f1", [128, 1])
    msk8 = sb("msk8_f1", [128, E8])
    sig8 = sb("sig8_f1", [128, E8])
    gates = sb("gates_f1", [128, E8], BF16)
    gatesall = sb("gatesall_f1", [128, E8 * (SEQ // 128)], BF16)
    growsm = sb("growsm_f1", [1, 512], BF16)
    onesrow = sb("onesrow_f1", [1, 512], BF16)
    gbcs = sb("gbcs_f1", [128, 512], BF16)
    silub = sb("silu_f1", [128, 512], BF16)
    t2tile = sb("t2_f1", [128, 2 * D], BF16)
    outf = sb("outf_f1", [128, 2 * D])

    ps_u = paA(ns + "f1_psu", [128, 512], F32)
    ps_v = paA(ns + "f1_psv", [128, 512], F32)
    ps_g = paA(ns + "f1_psg", [128, 512], F32)
    ps_w2 = [paA(ns + f"f1_psw{j}", [128, 512], F32) for j in range(4)]
    ps_tr = paA(ns + "f1_pstr", [128, 128], BF16)

    _s = lambda n: E.sem(ns + n)
    s_w = _s("s_w"); s_ld = _s("s_ld")
    s_pe = _s("s_pe"); s_ev = _s("s_ev"); s_act = _s("s_act")
    s_st = _s("s_st"); s_hz = _s("s_hz")

    rt2 = Rot(2); rout = Rot(2)

    def g_cast(dst, src, waits=()):
        E.emit("gpsimd", "dma_start", dst, src, waits=waits, inc=s_w, incby=16)
        return E.val(s_w)

    w_id = g_cast(identb[:], ident_in)
    E.emit("vector", "memset", onesrow[:], 1.0)
    E.emit("gpsimd", "dma_start", routerb[:].rearrange("p (c n) -> p c n", c=8),
           router_in.rearrange("(c p) n -> p c n", p=128), inc=s_w, incby=16)
    w_rt = E.val(s_w)

    prev_store_ev = 0
    for b in range(BPC):
        xoff = b * S
        for p in range(8):
            E.emit("sync", "dma_start", xnfm[:, S * p:S * (p + 1)],
                   xnfm_in[128 * p:128 * (p + 1), xoff:xoff + S],
                   waits=[(s_pe, E.val(s_pe))], inc=s_ld, incby=16)
        ld_x = E.val(s_ld)
        for i in range(NTS):
            jt, bwt, it = rt2.acquire()
            E.emit("sync", "dma_start", t2tile[:, D * jt:D * (jt + 1)],
                   x1_in[xoff + 128 * i: xoff + 128 * (i + 1), :],
                   waits=bwt, inc=s_ld, incby=16)
            E.emit("gpsimd", "dma_start", acc[:, D * i:D * (i + 1)],
                   attn1_in[xoff + 128 * i: xoff + 128 * (i + 1), :],
                   waits=[(s_ev, prev_store_ev)], inc=s_w, incby=16)
            E.emit("vector", "tensor_tensor", acc[:, D * i:D * (i + 1)],
                   acc[:, D * i:D * (i + 1)], t2tile[:, D * jt:D * (jt + 1)], ALU.add,
                   waits=[(s_ld, E.val(s_ld)), (s_w, E.val(s_w))], inc=s_ev)
            rt2.release(it, s_ev, E.val(s_ev))
        for i in range(NTS):
            for c in range(8):
                E.emit("tensor", "matmul", ps_g[:, 0:E8],
                       xnfm[:, S * c + 128 * i: S * c + 128 * (i + 1)],
                       routerb[:, E8 * c:E8 * (c + 1)],
                       start=(c == 0), stop=(c == 7),
                       waits=[(s_ld, ld_x), (s_w, w_rt),
                              (s_ev, E.val(s_ev))] if c == 0 else (),
                       inc=(s_pe if c == 7 else None))
            E.emit("vector", "tensor_copy", scs[:], ps_g[:, 0:E8],
                   waits=[(s_pe, E.val(s_pe))])
            E.emit("vector", "tensor_reduce", mx1[:], scs[:], AX.X, ALU.max, inc=s_hz)
            E.emit("scalar", "activation", sig8[:], scs[:], AF.Sigmoid,
                   waits=[(s_pe, E.val(s_pe))], inc=s_act)
            E.emit("vector", "tensor_scalar", msk8[:], scs[:], mx1[:], None, ALU.is_ge,
                   waits=[(s_act, E.val(s_act)), (s_hz, E.val(s_hz))])
            E.emit("vector", "tensor_tensor",
                   gatesall[:, E8 * i:E8 * (i + 1)], sig8[:], msk8[:], ALU.mult,
                   inc=s_ev)
        gates_ev = E.val(s_ev)

        for se in range(9):
            routed = se < E8
            if routed:
                for g4 in range(NG):
                    for sub in range(4):
                        ti = 4 * g4 + sub
                        E.emit("tensor", "matmul", ps_w2[sub][0:1, 0:128],
                               gatesall[:, E8 * ti + se:E8 * ti + se + 1], identb[:],
                               start=True, stop=True,
                               waits=[(s_ev, E.val(s_ev)), (s_pe, E.val(s_pe)),
                                      (s_w, w_id)] if sub == 0 else (),
                               inc=(s_pe if sub == 3 else None))
                    E.emit("vector", "tensor_copy", growsm[:, 0:128],
                           ps_w2[0][0:1, 0:128], waits=[(s_pe, E.val(s_pe))])
                    E.emit("vector", "tensor_copy", growsm[:, 128:256],
                           ps_w2[1][0:1, 0:128])
                    E.emit("vector", "tensor_copy", growsm[:, 256:384],
                           ps_w2[2][0:1, 0:128])
                    E.emit("vector", "tensor_copy", growsm[:, 384:512],
                           ps_w2[3][0:1, 0:128], inc=s_ev)
                    E.emit("tensor", "matmul", ps_g[:],
                           onesrow[:, 0:128], growsm[:],
                           start=True, stop=True,
                           waits=[(s_ev, E.val(s_ev))], inc=s_pe)
                    E.emit("vector", "tensor_copy", gbcs[:], ps_g[:],
                           waits=[(s_pe, E.val(s_pe))])
                    for c in range(8):
                        E.emit("vector", "tensor_tensor",
                               xg[:, S * c + 512 * g4: S * c + 512 * (g4 + 1)],
                               xnfm[:, S * c + 512 * g4: S * c + 512 * (g4 + 1)],
                               gbcs[:], ALU.mult,
                               inc=(s_ev if c == 7 else None))
                xg_ev = E.val(s_ev)
                src = xg
            else:
                xg_ev = E.val(s_ev)
                src = xnfm
            for fc in range(NFC):
                E.emit("gpsimd", "dma_start",
                       w13[:, 0:8 * 128].rearrange("p (c n) -> p c n", c=8),
                       (w1_in[se] if routed else ws1_in)[:, 128 * fc:128 * (fc + 1)]
                           .rearrange("(c p) n -> p c n", p=128),
                       waits=[(s_pe, E.val(s_pe))], inc=s_w, incby=16)
                E.emit("gpsimd", "dma_start",
                       w13[:, 8 * 128:16 * 128].rearrange("p (c n) -> p c n", c=8),
                       (w3_in[se] if routed else ws3_in)[:, 128 * fc:128 * (fc + 1)]
                           .rearrange("(c p) n -> p c n", p=128),
                       inc=s_w, incby=16)
                w13_ld = E.val(s_w)
                for g4 in range(NG):
                    for c in range(8):
                        E.emit("tensor", "matmul", ps_u[:],
                               w13[:, 128 * c:128 * (c + 1)],
                               src[:, S * c + 512 * g4: S * c + 512 * (g4 + 1)],
                               start=(c == 0), stop=(c == 7),
                               waits=[(s_w, w13_ld), (s_ev, xg_ev),
                                      (s_act, E.val(s_act)),
                                      (s_ev, E.val(s_ev))] if c == 0 else (),
                               inc=(s_pe if c == 7 else None))
                    E.emit("scalar", "activation", silub[:], ps_u[:], AF.Silu,
                           waits=[(s_pe, E.val(s_pe))], inc=s_act)
                    for c in range(8):
                        E.emit("tensor", "matmul", ps_v[:],
                               w13[:, 8 * 128 + 128 * c:8 * 128 + 128 * (c + 1)],
                               src[:, S * c + 512 * g4: S * c + 512 * (g4 + 1)],
                               start=(c == 0), stop=(c == 7),
                               waits=[(s_ev, E.val(s_ev))] if c == 0 else (),
                               inc=(s_pe if c == 7 else None))
                    E.emit("vector", "tensor_tensor",
                           mid[:, S * fc + 512 * g4: S * fc + 512 * (g4 + 1)],
                           ps_v[:], silub[:], ALU.mult,
                           waits=[(s_pe, E.val(s_pe)), (s_act, E.val(s_act))],
                           inc=s_ev)
            mid_ev = E.val(s_ev)
            for half in range(2):
                for ctg in range(NTS // 4):
                    fc_pe = {}
                    for fc in range(NFC):
                        wv2 = [(s_pe, fc_pe[fc - 2])] if fc >= 2 else [(s_pe, E.val(s_pe))]
                        E.emit("gpsimd", "dma_start",
                               w2s[:, 512 * (fc % 2):512 * (fc % 2 + 1)],
                               (w2_in[se] if routed else ws2_in)
                                   [128 * fc:128 * (fc + 1),
                                    512 * half:512 * (half + 1)],
                               waits=wv2, inc=s_w, incby=16)
                        w2_ld = E.val(s_w)
                        for q in range(4):
                            ct = 4 * ctg + q
                            E.emit("tensor", "matmul", ps_w2[q][:],
                                   mid[:, S * fc + 128 * ct: S * fc + 128 * (ct + 1)],
                                   w2s[:, 512 * (fc % 2):512 * (fc % 2 + 1)],
                                   start=(fc == 0), stop=(fc == NFC - 1),
                                   waits=[(s_w, w2_ld), (s_ev, mid_ev),
                                          (s_ev, E.val(s_ev))]
                                         if q == 0 and fc == 0 else
                                         ([(s_w, w2_ld)] if q == 0 else ()),
                                   inc=(s_pe if q == 3 else None))
                        fc_pe[fc] = E.val(s_pe)
                    w2_pe = E.val(s_pe)
                    for q in range(4):
                        ct = 4 * ctg + q
                        av = acc[:, D * ct + 512 * half: D * ct + 512 * (half + 1)]
                        E.emit("vector", "tensor_tensor", av, av, ps_w2[q][:],
                               ALU.add, waits=[(s_pe, w2_pe)], inc=s_ev)
        fin_ev = E.val(s_ev)
        for i in range(NTS):
            jo, bwo, io = rout.acquire()
            E.emit("vector", "tensor_copy", outf[:, D * jo:D * (jo + 1)],
                   acc[:, D * i:D * (i + 1)], waits=[(s_ev, fin_ev)] + bwo, inc=s_ev)
            E.emit("sync", "dma_start", out_out[xoff + 128 * i: xoff + 128 * (i + 1), :],
                   outf[:, D * jo:D * (jo + 1)], waits=[(s_ev, E.val(s_ev))],
                   inc=s_st, incby=16)
            rout.release(io, s_st, E.val(s_st))
        prev_store_ev = E.val(s_ev)

    E.wait("sync", s_st)


def host_consts():
    c = {}
    c["ident"] = np.eye(128, dtype=np.float32)
    perm = np.zeros((128, 128), np.float32)
    for m in range(128):
        base = (m // 64) * 64
        r = m % 64
        rho = base + ((r + 32) % 64)
        perm[rho, m] = 1.0
    c["perm"] = perm
    selb = np.zeros((1, 256), np.float32)
    selb[0, 0:64] = 1.0
    selb[0, 128 + 64:256] = 1.0
    c["selb"] = selb
    dm = np.triu(np.ones((128, 128), np.float32))
    c["dmask"] = dm
    t = np.arange(SEQ, dtype=np.float64)
    inv_freq = 1.0 / (10000.0 ** (np.arange(0, DH, 2).astype(np.float64) / DH))
    r = np.arange(128)
    d_in_head = r % 64
    freq_idx = d_in_head % 32
    ang = np.outer(inv_freq[freq_idx], t)
    c["rope_cos"] = np.cos(ang).astype(np.float32)
    sgn = np.where(d_in_head < 32, -1.0, 1.0)[:, None]
    c["rope_ssin"] = (np.sin(ang) * sgn).astype(np.float32)
    bd = np.zeros((128, 128), np.float32)
    bd[:64, :64] = 1.0; bd[64:, 64:] = 1.0
    c["bdmask"] = bd
    ml, tl = np.arange(128)[:, None], np.arange(128)[None, :]
    c["bandm"] = ((ml >= tl) & (ml <= tl + 15)).astype(np.float32)
    rr = np.arange(15)[:, None]
    c["topm"] = (tl >= 113 + rr).astype(np.float32)
    kk = np.arange(16)[:, None, None]
    m0 = ((ml[None] >= tl[None]) & (ml[None] <= tl[None] + 15)
          & (ml[None] >= 15) & ((tl[None] + kk) >= 15)).astype(np.float32)
    c["mask0"] = np.ascontiguousarray(m0.transpose(1, 0, 2).reshape(128, 16 * 128))
    return c


def build_fused(T):
    import contextlib
    NT = T // 128
    NSL = 64
    F = 2816
    E8 = 8
    W = 16
    S = T // BPC
    nc = new_nc()
    E = Em(nc)
    din = lambda n, shp, dt=F32: nc.dram_tensor(n, shp, dt, kind="ExternalInput").ap()
    mid = lambda n, shp, dt: nc.dram_tensor(n, shp, dt, kind="Internal").ap()

    x_in = din("x", [T, D])
    ident = din("ident", [128, 128])
    consts = {"perm": din("perm", [128, 128]), "selb": din("selb", [1, 256]),
              "dmask": din("dmask", [128, 128]),
              "rope_cos": din("rope_cos", [128, S]),
              "rope_ssin": din("rope_ssin", [128, S]),
              "bdmask": din("bdmask", [128, 128]),
              "bandm": din("bandm", [128, 128]), "topm": din("topm", [15, 128]),
              "mask0": din("mask0", [128, W * 128])}
    l0aw = {n: din("l0" + n, [D, D]) for n in ("WqT", "WkT", "WvT", "WoT")}
    l1aw = {n: din("l1" + n, [D, D]) for n in ("WqT", "WkT", "WvT", "WkpT", "WoT")}
    l0f = {"slot_init": din("slot_init", [NSL, D]), "w1": din("w1", [2 * D, 512]),
           "b1": din("b1", [512]), "w2": din("w2", [512, NSL]), "b2": din("b2", [NSL]),
           "tau": din("tau", [1]), "WihT": din("WihT", [D, 3 * D]),
           "WhhT": din("WhhT", [D, 3 * D]), "bih": din("bih", [3 * D]),
           "bhh": din("bhh", [3 * D]), "wqs": din("wqs", [D, 256]),
           "bqs": din("bqs", [256]), "wks": din("wks", [D, 256]),
           "bks": din("bks", [256]), "wvs": din("wvs", [D, D]), "bvs": din("bvs", [D]),
           "lng": din("lng", [D]), "lnb": din("lnb", [D]), "wvp": din("wvp", [D, D]),
           "bvp": din("bvp", [D]), "wo": din("wo", [D, D]), "bo": din("bo", [D])}
    l1f = {"router": din("router", [D, E8]), "w1e": din("w1e", [E8, D, F]),
           "w3e": din("w3e", [E8, D, F]), "w2e": din("w2e", [E8, F, D]),
           "ws1": din("ws1", [D, F]), "ws3": din("ws3", [D, F]),
           "ws2": din("ws2", [F, D])}
    gamma = din("gamma", [D]); beta = din("beta", [D])
    out_ap = nc.dram_tensor("out", [T, D], F32, kind="ExternalOutput").ap()

    xn = mid("i_xn", [T, D], BF16)
    xnfm = mid("i_xnfm", [D, T], BF16)
    stats = mid("i_stats", [128, 2 * NT], F32)
    tmp1 = mid("i_tmp1", [T, D], BF16)
    x1 = mid("i_x1", [T, D], BF16)
    xn1 = mid("i_xn1", [T, D], BF16)
    xn1fm = mid("i_xn1fm", [D, T], BF16)
    stats1 = mid("i_stats1", [128, 2 * NT], F32)
    attn1 = mid("i_attn1", [T, D], BF16)

    def phase(ns, fn):
        with contextlib.ExitStack() as st:
            salloc = lambda n, shp, dt=F32: st.enter_context(
                nc.sbuf_tensor(n, shp, dt)).ap()
            palloc = lambda n, shp, dt=F32: st.enter_context(
                nc.psum_tensor(n, shp, dt)).ap()
            fn(salloc, palloc)
        E.barrier()

    phase("p0", lambda sa, pa: emit_ln(nc, E, T,
          {"x": x_in, "ident": ident, "xn": xn, "xn_fm": xnfm, "stats": stats},
          ns="p0", salloc=sa, palloc=pa))
    e1 = {"xn_fm": xnfm, "x": x_in, "tmp1": tmp1, "perm": consts["perm"],
          "selb": consts["selb"], "dmask": consts["dmask"],
          "rope_cos": consts["rope_cos"], "rope_ssin": consts["rope_ssin"]}
    e1.update({"l0a_" + n: l0aw[n] for n in l0aw})
    phase("p1", lambda sa, pa: emit_l0_attn(nc, E, T, e1, ns="p1", salloc=sa, palloc=pa))
    e2 = {"xraw": x_in, "stats": stats, "xn": xn, "tmp1": tmp1, "x1": x1,
          "ident": ident, "bdmask": consts["bdmask"]}
    e2.update(l0f)
    phase("p2", lambda sa, pa: emit_l0_ffn(nc, E, T, e2, ns="p2", salloc=sa, palloc=pa))
    phase("p3", lambda sa, pa: emit_ln(nc, E, T,
          {"x": x1, "ident": ident, "xn": xn1, "xn_fm": xn1fm, "stats": stats1},
          ns="p3", salloc=sa, palloc=pa))
    e4 = {"xn_fm": xn1fm, "xn": xn1, "gamma": gamma, "beta": beta,
          "bandm": consts["bandm"], "topm": consts["topm"],
          "mask0": consts["mask0"], "attn1": attn1}
    e4.update({"l1a_" + n: l1aw[n] for n in l1aw})
    phase("p4", lambda sa, pa: emit_l1_attn(nc, E, T, e4, ns="p4", salloc=sa, palloc=pa))
    e5 = {"xn_fm": xn1fm, "x1": x1, "attn1": attn1, "ident": ident, "out": out_ap}
    e5.update(l1f)
    phase("p5", lambda sa, pa: emit_l1_ffn(nc, E, T, e5, ns="p5", salloc=sa, palloc=pa))
    E.finalize()
    return nc


def kernel(**inputs):
    T = BPC * SEQ
    C = lambda a: np.ascontiguousarray(np.asarray(a, dtype=np.float32))
    CT = lambda a: np.ascontiguousarray(np.asarray(a, dtype=np.float32).T)
    consts = host_consts()
    x_full = C(inputs["L0_attn_x"]).reshape(16, SEQ, D)
    xs = [np.ascontiguousarray(x_full[2 * i:2 * i + 2].reshape(T, D))
          for i in range(NCORES)]

    shared = {
        "ident": consts["ident"], "perm": consts["perm"], "selb": consts["selb"],
        "dmask": consts["dmask"], "rope_cos": consts["rope_cos"],
        "rope_ssin": consts["rope_ssin"], "bdmask": consts["bdmask"],
        "bandm": consts["bandm"], "topm": consts["topm"], "mask0": consts["mask0"],
        "l0WqT": CT(inputs["L0_attn_Wq"]), "l0WkT": CT(inputs["L0_attn_Wk"]),
        "l0WvT": CT(inputs["L0_attn_Wv"]), "l0WoT": CT(inputs["L0_attn_Wo"]),
        "l1WqT": CT(inputs["L1_attn_Wq"]), "l1WkT": CT(inputs["L1_attn_Wk"]),
        "l1WvT": CT(inputs["L1_attn_Wv"]), "l1WkpT": CT(inputs["L1_attn_Wkp"]),
        "l1WoT": CT(inputs["L1_attn_Wo"]),
        "slot_init": C(inputs["L0_ffn_slot_init"]), "w1": C(inputs["L0_ffn_w1"]),
        "b1": C(inputs["L0_ffn_b1"]), "w2": C(inputs["L0_ffn_w2"]),
        "b2": C(inputs["L0_ffn_b2"]), "tau": C(inputs["L0_ffn_tau"]),
        "WihT": CT(inputs["L0_ffn_Wih"]), "WhhT": CT(inputs["L0_ffn_Whh"]),
        "bih": C(inputs["L0_ffn_bih"]), "bhh": C(inputs["L0_ffn_bhh"]),
        "wqs": C(inputs["L0_ffn_wq"]), "bqs": C(inputs["L0_ffn_bq"]),
        "wks": C(inputs["L0_ffn_wk"]), "bks": C(inputs["L0_ffn_bk"]),
        "wvs": C(inputs["L0_ffn_wv"]), "bvs": C(inputs["L0_ffn_bv"]),
        "lng": C(inputs["L0_ffn_ln_g"]), "lnb": C(inputs["L0_ffn_ln_b"]),
        "wvp": C(inputs["L0_ffn_wvp"]), "bvp": C(inputs["L0_ffn_bvp"]),
        "wo": C(inputs["L0_ffn_wo"]), "bo": C(inputs["L0_ffn_bo"]),
        "gamma": C(inputs["L1_attn_gamma"]), "beta": C(inputs["L1_attn_beta"]),
        "router": C(inputs["L1_ffn_router_DE"]), "w1e": C(inputs["L1_ffn_w1_eDF"]),
        "w3e": C(inputs["L1_ffn_w3_eDF"]), "w2e": C(inputs["L1_ffn_w2_eFD"]),
        "ws1": C(inputs["L1_ffn_ws1_DF"]), "ws3": C(inputs["L1_ffn_ws3_DF"]),
        "ws2": C(inputs["L1_ffn_ws2_FD"]),
    }
    nc = build_fused(T)
    maps = [dict(x=xs[i], **shared) for i in range(NCORES)]
    res = run_bass_kernel_spmd(nc, maps, core_ids=list(range(NCORES))).results
    out = np.empty((16, SEQ, D), dtype=np.float32)
    for i in range(NCORES):
        out[2 * i:2 * i + 2] = np.asarray(res[i]["out"], np.float32).reshape(2, SEQ, D)
    return out


# version 05
# speedup vs baseline: 1.4618x; 1.2587x over previous
import math
import numpy as np
import concourse.bass as bass
import concourse.mybir as mybir
from concourse.bass_utils import run_bass_kernel_spmd

F32 = mybir.dt.float32
BF16 = mybir.dt.bfloat16
AF = mybir.ActivationFunctionType
ALU = mybir.AluOpType
AX = mybir.AxisListType

D = 1024
SEQ = 2048
BPC = 2
NCORES = 8
H = 16
DH = 64
C_L0 = 28.0
C_L1 = 50.0

class Rec:
    __slots__ = ("meth", "a", "kw", "waits", "incs", "auto_inc", "auto_waits",
                 "hz_val", "_writes", "post")

    def __init__(self, meth, a, kw, waits, incs):
        self.meth = meth
        self.a = a
        self.kw = kw
        self.waits = list(waits)
        self.incs = list(incs)
        self.auto_inc = False
        self.auto_waits = []
        self.hz_val = 0
        self._writes = set()
        self.post = []


def _ap_name(x):
    try:
        return x.tensor.name
    except Exception:
        return None


class Em:
    ENGINES = ("sync", "gpsimd", "tensor", "vector", "scalar")
    AUTO = ("vector", "scalar")
    AUTO_DEPTH = 3

    def __init__(self, nc):
        self.nc = nc
        self.q = {e: [] for e in self.ENGINES}
        self.cnt = {}
        self.auto_sem = {e: self.sem(f"s_auto_{e}") for e in self.AUTO}

    def sem(self, name):
        if not hasattr(self, "_sem_by_name"):
            self._sem_by_name = {}
        if name in self._sem_by_name:
            return self._sem_by_name[name]
        h = self.nc.alloc_semaphore(name)
        self.cnt[h] = 0
        self._sem_by_name[name] = h
        return h

    def barrier(self):
        if not hasattr(self, "_bar_scr"):
            self._bar_scr = self.nc.alloc_sbuf_tensor("bar_scratch", [1, 2],
                                                      F32).ap()
        sbar = self.sem("s_barrier")
        for e in self.ENGINES:
            if e == "vector":
                r = Rec("memset", (self._bar_scr[:, 0:1], 0.0), {}, [], [(sbar, 1)])
            elif e == "scalar":
                r = Rec("memzero", (self._bar_scr[:, 1:2],), {}, [], [(sbar, 1)])
            else:
                r = Rec("nop", (), {}, [], [(sbar, 1)])
            self.cnt[sbar] += 1
            r.post = [(sbar, self.cnt[sbar])]
            self.q[e].append(r)
        tgt = self.cnt[sbar]
        for e in self.ENGINES:
            self.wait(e, sbar, tgt)

    def val(self, s):
        return self.cnt[s]

    def emit(self, eng, meth, *a, waits=(), inc=None, incby=1, **kw):
        if inc is not None:
            self.cnt[inc] += incby
        rec = Rec(meth, a, kw, waits, [(inc, incby)] if inc is not None else [])
        if inc is not None:
            rec.post = [(inc, self.cnt[inc])]
        if eng in self.AUTO and meth != "memset":
            writes = set()
            reads = set()
            names = [_ap_name(x) for x in a]
            if kw.get("out") is not None:
                writes.add(_ap_name(kw["out"]))
                reads.update(n for n in (_ap_name(kw.get("in_")),) if n)
            else:
                if names:
                    writes.add(names[0])
                reads.update(n for n in names[1:] if n)
            for k2 in ("bias", "scale", "scalar1", "scalar2", "in_", "in_values",
                       "in_max", "in_to_replace"):
                n = _ap_name(kw.get(k2))
                if n:
                    reads.add(n)
            if kw.get("accum_out") is not None:
                writes.add(_ap_name(kw["accum_out"]))
            writes.discard(None); reads.discard(None)
            prev = [r for r in self.q[eng][-self.AUTO_DEPTH:] if isinstance(r, Rec)]
            for pr in prev:
                if pr._writes and (pr._writes & reads):
                    if pr.post:
                        rec.waits.append(pr.post[0])
                    else:
                        pr.auto_inc = True
                        rec.auto_waits.append(pr)
            rec._writes = writes
        self.q[eng].append(rec)
        return rec

    def wait(self, eng, sem, v=None):
        v = self.cnt[sem] if v is None else v
        r = Rec(None, None, None, [(sem, v)], [])
        r._writes = set()
        self.q[eng].append(r)

    def finalize(self):
        for e in self.AUTO:
            c = 0
            for r in self.q[e]:
                if isinstance(r, Rec) and r.auto_inc:
                    c += 1
                    r.hz_val = c
        with self.nc.allow_low_precision("bf16 outputs within validated end-to-end tolerance"), \
             self.nc.Block() as blk:
            for e in self.ENGINES:
                if self.q[e]:
                    def run_all(eng_obj, _e=e):
                        hsem = self.auto_sem.get(_e)
                        for r in self.q[_e]:
                            if not isinstance(r, Rec):
                                r(eng_obj)
                                continue
                            for (s, v) in r.waits:
                                eng_obj.wait_ge(s, v)
                            for pr in r.auto_waits:
                                eng_obj.wait_ge(hsem, pr.hz_val)
                            if r.meth is None:
                                continue
                            inst = getattr(eng_obj, r.meth)(*r.a, **r.kw)
                            for (s, by) in r.incs:
                                inst.then_inc(s, by)
                            if r.auto_inc:
                                inst.then_inc(hsem, 1)
                    getattr(blk, e)(run_all)


def new_nc():
    return bass.Bass("TRN2", target_bir_lowering=False, detect_race_conditions=False)


def build_ln(T, in_dtype=F32):
    NT = T // 128
    nc = new_nc()
    ext = {
        "x": nc.dram_tensor("x", [T, D], in_dtype, kind="ExternalInput").ap(),
        "ident": nc.dram_tensor("ident", [128, 128], F32, kind="ExternalInput").ap(),
        "xn": nc.dram_tensor("xn", [T, D], BF16, kind="ExternalOutput").ap(),
        "xn_fm": nc.dram_tensor("xn_fm", [D, T], BF16, kind="ExternalOutput").ap(),
        "stats": nc.dram_tensor("stats", [128, 2 * NT], F32, kind="ExternalOutput").ap(),
    }
    E = Em(nc)
    emit_ln(nc, E, T, ext)
    E.finalize()
    return nc


def emit_ln(nc, E, T, ext, ns="", salloc=None, palloc=None):
    NT = T // 128
    x = ext["x"]
    ident_in = ext["ident"]
    xn_out = ext["xn"]
    xnfm_out = ext["xn_fm"]
    stats_out = ext["stats"]
    sbA = salloc or (lambda n, shp, dt=F32: nc.alloc_sbuf_tensor(n, shp, dt).ap())
    paA = palloc or (lambda n, shp, dt=F32: nc.alloc_psum_tensor(n, shp, dt).ap())
    sb = lambda n, shp, dt=F32: sbA(ns + n, shp, dt)

    NSLOT = 4
    xslot = sb("xslot", [128, NSLOT * D], BF16)
    xslot32 = sb("xslot32", [128, 2 * D])
    xn_sb = sb("xn_sb", [128, NT * D], BF16)
    xnfm_sb = sb("xnfm_sb", [128, 8 * T], BF16)
    stats = sb("stats_sb", [128, 2 * NT])
    identb = sb("identb", [128, 128], BF16)
    acc = sb("acc", [128, 1])
    musq = sb("musq", [128, 1])
    sq = sb("sq", [128, 2])
    var = sb("var", [128, 2])
    lntmp = sb("lntmp", [128, 1])
    junk = sb("junk", [128, D])

    ps_tr = [paA(ns + f"ps_tr{i}", [128, 128], BF16) for i in range(2)]

    s_x = E.sem(ns + "s_x"); s_id = E.sem(ns + "s_id"); s_ld32 = E.sem(ns + "s_ld32")
    s_sq = E.sem(ns + "s_sq"); s_var = E.sem(ns + "s_var"); s_rs = E.sem(ns + "s_rs")
    s_xn = E.sem(ns + "s_xn"); s_tr = E.sem(ns + "s_tr"); s_ev = E.sem(ns + "s_ev")
    s_st = E.sem(ns + "s_st")

    E.emit("gpsimd", "dma_start", identb[:], ident_in, inc=s_id, incby=16)

    for i in range(NT):
        slot = xslot[:, (i % NSLOT) * D:(i % NSLOT + 1) * D]
        slot32 = xslot32[:, (i % 2) * D:(i % 2 + 1) * D]
        w = [] if i < NSLOT else [(s_xn, i - NSLOT + 1)]
        E.emit("gpsimd", "dma_start", slot, x[i * 128:(i + 1) * 128, :],
               waits=w, inc=s_x, incby=16)
        w32 = [] if i < 2 else [(s_var, i - 1)]
        E.emit("gpsimd", "dma_start", slot32, x[i * 128:(i + 1) * 128, :],
               waits=w32, inc=s_x, incby=16)
        mu = stats[:, 2 * i:2 * i + 1]
        rstd = stats[:, 2 * i + 1:2 * i + 2]
        E.emit("vector", "reduce_sum", acc[:], slot32, AX.X,
               waits=[(s_x, 32 * (i + 1))])
        E.emit("vector", "tensor_scalar_mul", mu, acc[:], 1.0 / D)
        E.emit("vector", "tensor_tensor", musq[:], mu, mu, ALU.mult)
        E.emit("vector", "tensor_scalar_add", musq[:], musq[:], -1e-5)
        E.emit("vector", "tensor_tensor", junk[:], slot32, slot32, ALU.mult)
        E.emit("vector", "reduce_sum", sq[:, i % 2:i % 2 + 1], junk[:], AX.X,
               inc=s_sq)
        E.emit("vector", "scalar_tensor_tensor", var[:, i % 2:i % 2 + 1],
               sq[:, i % 2:i % 2 + 1], 1.0 / D, musq[:], ALU.mult, ALU.subtract,
               waits=[(s_sq, i + 1), (s_rs, max(0, i - 1))], inc=s_var)
        E.emit("scalar", "activation", lntmp[:], var[:, i % 2:i % 2 + 1], AF.Ln,
               waits=[(s_var, i + 1)])
        E.emit("scalar", "activation", rstd, lntmp[:], AF.Exp, scale=-0.5, inc=s_rs)
        E.emit("vector", "tensor_scalar", xn_sb[:, i * D:(i + 1) * D], slot,
               mu, rstd, ALU.subtract, ALU.mult,
               waits=[(s_rs, i + 1), (s_x, 32 * (i + 1))], inc=s_xn)
        for c in range(8):
            k = 8 * i + c
            w = [(s_xn, i + 1), (s_id, 16)] if c == 0 else []
            if k >= 2:
                w = list(w) + [(s_ev, k - 1)]
            E.emit("tensor", "transpose", ps_tr[k % 2][:],
                   xn_sb[:, i * D + c * 128: i * D + (c + 1) * 128], identb[:],
                   waits=w, inc=s_tr)
            E.emit("vector", "tensor_copy",
                   xnfm_sb[:, c * T + i * 128: c * T + (i + 1) * 128], ps_tr[k % 2][:],
                   waits=[(s_tr, k + 1)], inc=s_ev)
        E.emit("sync", "dma_start", xn_out[i * 128:(i + 1) * 128, :],
               xn_sb[:, i * D:(i + 1) * D], waits=[(s_xn, i + 1)], inc=s_st, incby=16)

    for c in range(8):
        E.emit("sync", "dma_start", xnfm_out[c * 128:(c + 1) * 128, :],
               xnfm_sb[:, c * T:(c + 1) * T], waits=[(s_ev, 8 * NT)], inc=s_st, incby=16)
    E.emit("sync", "dma_start", stats_out, stats[:], waits=[(s_rs, NT)], inc=s_st, incby=16)
    E.wait("sync", s_st)


class Rot:
    def __init__(self, n):
        self.n = n
        self.inst = 0
        self.freed = {}

    def acquire(self):
        i = self.inst
        self.inst += 1
        w = self.freed.get(i - self.n)
        return i % self.n, ([w] if w else []), i

    def release(self, i, sem, cnt):
        self.freed[i] = (sem, cnt)


def build_l0_attn(T):
    S = T // BPC
    nc = new_nc()
    ext = {"xn_fm": nc.dram_tensor("xn_fm", [D, T], BF16, kind="ExternalInput").ap(),
           "x": nc.dram_tensor("x", [T, D], F32, kind="ExternalInput").ap(),
           "perm": nc.dram_tensor("perm", [128, 128], F32, kind="ExternalInput").ap(),
           "selb": nc.dram_tensor("selb", [1, 256], F32, kind="ExternalInput").ap(),
           "dmask": nc.dram_tensor("dmask", [128, 128], F32, kind="ExternalInput").ap(),
           "rope_cos": nc.dram_tensor("rope_cos", [128, S], F32, kind="ExternalInput").ap(),
           "rope_ssin": nc.dram_tensor("rope_ssin", [128, S], F32, kind="ExternalInput").ap(),
           "tmp1": nc.dram_tensor("tmp1", [T, D], BF16, kind="ExternalOutput").ap()}
    for n in ("WqT", "WkT", "WvT", "WoT"):
        ext["l0a_" + n] = nc.dram_tensor(n, [D, D], F32, kind="ExternalInput").ap()
    E = Em(nc)
    emit_l0_attn(nc, E, T, ext)
    E.finalize()
    return nc


def emit_l0_attn(nc, E, T, ext, ns="", salloc=None, palloc=None):
    S = T // BPC
    NG = S // 512
    NKT = S // 128
    xnfm_in = ext["xn_fm"]
    x_in = ext["x"]
    wts = {n: ext["l0a_" + n] for n in ("WqT", "WkT", "WvT", "WoT")}
    perm_in = ext["perm"]; selb_in = ext["selb"]; dmask_in = ext["dmask"]
    cos_in = ext["rope_cos"]; ssin_in = ext["rope_ssin"]
    tmp1_out = ext["tmp1"]
    sbA = salloc or (lambda n, shp, dt=F32: nc.alloc_sbuf_tensor(n, shp, dt).ap())
    paA = palloc or (lambda n, shp, dt=F32: nc.alloc_psum_tensor(n, shp, dt).ap())
    sb = lambda n, shp, dt=F32: sbA(ns + n, shp, dt)

    xnfm = sb("xnfm", [128, 8 * S], BF16)
    zt = xnfm
    qt = sb("qt", [128, 8 * S], BF16)
    kt = sb("kt", [128, 8 * S], BF16)
    VW = 65 * H
    vpad = sb("vpad", [128, NKT * VW], BF16)
    wb = [sb(f"wb{j}", [128, 8 * D], BF16) for j in range(2)]
    cosb = sb("cosb", [128, S], BF16)
    ssinb = sb("ssinb", [128, S], BF16)
    permb = sb("permb", [128, 128], BF16)
    selbb = sb("selbb", [1, 256], BF16)
    dmaskb = sb("dmaskb", [128, 128], BF16)
    expb = [sb(f"expb{j}", [128, 512], BF16) for j in range(2)]
    rab = [sb("rab0", [1, S], BF16), sb("rab1", [1, S], BF16)]
    rt1 = sb("rt1", [128, 512])
    rt2 = sb("rt2", [128, 512])
    negC = sb("negC", [128, 1])
    xres = sb("xres", [128, 4 * D], BF16)
    osb = sb("osb", [128, 4 * D], BF16)

    ps_a = [paA(ns + f"ps_a{j}", [128, 512], F32) for j in range(2)]
    ps_b = [paA(ns + f"ps_b{j}", [128, 512], F32) for j in range(2)]
    ps_c = paA(ns + "ps_c", [128, 512], F32)

    _s = lambda n: E.sem(ns + n)
    s_w = _s("s_w"); s_ld = _s("s_ld")
    s_p1 = _s("s_p1"); s_ev1 = _s("s_ev1")
    s_rpe = _s("s_rpe"); s_rv = _s("s_rv")
    s_sc = _s("s_sc"); s_exp = _s("s_exp"); s_pv = _s("s_pv")
    s_rec = _s("s_rec"); s_zev = _s("s_zev")
    s_nb = _s("s_nb"); s_nrm = _s("s_nrm")
    s_res = _s("s_res"); s_st = _s("s_st"); s_prep = _s("s_prep")

    rpa = Rot(2)
    rexp = Rot(2)
    rpb = Rot(2)
    rpc = Rot(1)
    rxr = Rot(4)
    ros = Rot(4)
    rwb = Rot(2)

    E.emit("gpsimd", "dma_start", permb[:], perm_in, inc=s_w, incby=16)
    E.emit("gpsimd", "dma_start", selbb[:], selb_in, inc=s_w, incby=16)
    E.emit("gpsimd", "dma_start", dmaskb[:], dmask_in, inc=s_w, incby=16)
    E.emit("gpsimd", "dma_start", cosb[:], cos_in, inc=s_w, incby=16)
    E.emit("gpsimd", "dma_start", ssinb[:], ssin_in, inc=s_w, incby=16)
    E.emit("vector", "memset", negC[:], -C_L0, inc=s_prep)
    CONSTS_W = E.val(s_w)

    def kloop(psum, lhsT_slices, rhs_slices, waits_first, inc):
        for p in range(8):
            w = waits_first if p == 0 else ()
            E.emit("tensor", "matmul", psum, lhsT_slices[p], rhs_slices[p],
                   start=(p == 0), stop=(p == 7), waits=w,
                   inc=(inc if p == 7 else None))

    wslice = lambda j, p, a, b: wb[j][:, 1024 * p + a: 1024 * p + b]

    def load_weight(name):
        j, waits, inst = rwb.acquire()
        E.emit("gpsimd", "dma_start", wb[j][:].rearrange("p (c n) -> p c n", c=8),
               wts[name].rearrange("(c p) n -> p c n", p=128),
               waits=waits, inc=s_w, incby=16)
        return j, inst, E.val(s_w)

    res_total = 0
    for b in range(BPC):
        xoff = b * S
        for p in range(8):
            E.emit("sync", "dma_start", xnfm[:, S * p:S * (p + 1)],
                   xnfm_in[128 * p:128 * (p + 1), xoff:xoff + S],
                   waits=[(s_res, res_total)], inc=s_ld, incby=16)
        ld_done = E.val(s_ld)

        jq, iq, wq_ld = load_weight("WqT")
        jk, ik, wk_ld = load_weight("WkT")

        def proj_fm(dst, wj, w_ready):
            for m in range(8):
                for n in range(NG):
                    j, bw, inst = rpa.acquire()
                    waits = [(s_ld, ld_done), (s_w, w_ready)] + bw
                    kloop(ps_a[j][:],
                          [wslice(wj, p, 128 * m, 128 * (m + 1)) for p in range(8)],
                          [xnfm[:, S * p + 512 * n: S * p + 512 * (n + 1)] for p in range(8)],
                          waits, s_p1)
                    E.emit("vector", "tensor_copy",
                           dst[:, S * m + 512 * n: S * m + 512 * (n + 1)], ps_a[j][:],
                           waits=[(s_p1, E.val(s_p1))], inc=s_ev1)
                    rpa.release(inst, s_ev1, E.val(s_ev1))
        proj_fm(qt, jq, wq_ld)
        p1_q_done = E.val(s_p1)
        rwb.release(iq, s_p1, p1_q_done)
        proj_fm(kt, jk, wk_ld)
        p1_k_done = E.val(s_p1)
        rwb.release(ik, s_p1, p1_k_done)

        jv, iv, wv_ld = load_weight("WvT")

        for i in range(NKT):
            for half in range(2):
                j, bw, inst = rpa.acquire()
                waits = [(s_w, wv_ld)] + bw
                kloop(ps_a[j][:],
                      [xnfm[:, S * p + 128 * i: S * p + 128 * (i + 1)] for p in range(8)],
                      [wslice(jv, p, 512 * half, 512 * (half + 1)) for p in range(8)],
                      waits, s_p1)
                dst = bass.AP(tensor=vpad.tensor, offset=i * VW + half * 520,
                              ap=[[NKT * VW, 128], [65, 8], [1, 64]])
                E.emit("vector", "tensor_copy", dst,
                       ps_a[j][:].rearrange("p (h e) -> p h e", h=8),
                       waits=[(s_p1, E.val(s_p1))], inc=s_ev1)
                rpa.release(inst, s_ev1, E.val(s_ev1))
            ones_ap = bass.AP(tensor=vpad.tensor, offset=i * VW + 64,
                              ap=[[NKT * VW, 128], [65, 16], [1, 1]])
            E.emit("vector", "memset", ones_ap, 1.0, inc=s_ev1)
        p1_done = E.val(s_ev1)
        rwb.release(iv, s_p1, E.val(s_p1))

        jo, io, wo_ld = load_weight("WoT")

        for X in (qt, kt):
            for m in range(8):
                for n in range(NG):
                    j2, bw, inst = rpb.acquire()
                    sl = slice(S * m + 512 * n, S * m + 512 * (n + 1))
                    csl = slice(512 * n, 512 * (n + 1))
                    E.emit("tensor", "matmul", ps_b[j2][:], permb[:], X[:, sl],
                           start=True, stop=True,
                           waits=[(s_ev1, p1_done), (s_w, CONSTS_W)] + bw, inc=s_rpe)
                    E.emit("vector", "tensor_tensor", rt1[:], X[:, sl], cosb[:, csl],
                           ALU.mult, waits=[(s_ev1, p1_done), (s_w, CONSTS_W)])
                    E.emit("vector", "tensor_tensor", rt2[:], ps_b[j2][:], ssinb[:, csl],
                           ALU.mult, waits=[(s_rpe, E.val(s_rpe))])
                    E.emit("vector", "tensor_tensor", X[:, sl], rt1[:], rt2[:],
                           ALU.add, inc=s_rv)
                    rpb.release(inst, s_rv, E.val(s_rv))
        rope_done = E.val(s_rv)

        for h in range(16):
            mch = h // 2
            roff = 64 * (h % 2)
            qhead = qt[roff:roff + 64, S * mch: S * (mch + 1)]
            khead = kt[roff:roff + 64, S * mch: S * (mch + 1)]
            for g in range(NG):
                co, bwo, inst_o = rpb.acquire()
                nkt_g = 4 * g + 4
                for ktl in range(nkt_g):
                    cs, bws, inst_s = rpa.acquire()
                    ce, bwe, inst_e = rexp.acquire()
                    w_sc = [(s_rv, rope_done), (s_ev1, p1_done)] + bws
                    E.emit("tensor", "matmul", ps_a[cs][:],
                           khead[:, 128 * ktl:128 * (ktl + 1)],
                           qhead[:, 512 * g:512 * (g + 1)],
                           start=True, stop=True, waits=w_sc, inc=s_sc)
                    w_e = [(s_sc, E.val(s_sc)), (s_prep, 1)] + bwe
                    E.emit("scalar", "activation", expb[ce][:], ps_a[cs][:], AF.Exp,
                           scale=0.125, bias=negC[:], waits=w_e, inc=s_exp)
                    if ktl >= 4 * g:
                        off = 128 * ktl - 512 * g
                        if off > 0:
                            E.emit("vector", "memset", expb[ce][:, 0:off], 0.0,
                                   waits=[(s_exp, E.val(s_exp))], inc=s_exp)
                        E.emit("vector", "tensor_tensor",
                               expb[ce][:, off:off + 128], expb[ce][:, off:off + 128],
                               dmaskb[:], ALU.mult,
                               waits=[(s_exp, E.val(s_exp)), (s_w, CONSTS_W)], inc=s_exp)
                    rpa.release(inst_s, s_exp, E.val(s_exp))
                    vap = bass.AP(tensor=vpad.tensor, offset=ktl * VW + 65 * h,
                                  ap=[[NKT * VW, 128], [1, 65]])
                    w_pv = [(s_exp, E.val(s_exp))] + (bwo if ktl == 0 else [])
                    E.emit("tensor", "matmul", ps_b[co][0:65, :], vap, expb[ce][:],
                           start=(ktl == 0), stop=(ktl == nkt_g - 1),
                           waits=w_pv, inc=s_pv)
                    rexp.release(inst_e, s_pv, E.val(s_pv))
                E.emit("vector", "reciprocal",
                       rab[h % 2][:, 512 * g:512 * (g + 1)],
                       ps_b[co][64:65, :], waits=[(s_pv, E.val(s_pv))], inc=s_rec)
                E.emit("vector", "tensor_copy",
                       zt[roff:roff + 64, S * mch + 512 * g: S * mch + 512 * (g + 1)],
                       ps_b[co][0:64, :], inc=s_zev)
                rpb.release(inst_o, s_zev, E.val(s_zev))
            if h % 2 == 1:
                for g4 in range(NG):
                    jc, bwc, inst_c = rpc.acquire()
                    E.emit("tensor", "matmul", ps_c[:], selbb[:, 0:128],
                           rab[0][:, 512 * g4:512 * (g4 + 1)],
                           start=True, stop=False,
                           waits=[(s_rec, E.val(s_rec)), (s_zev, E.val(s_zev)),
                                  (s_w, CONSTS_W)] + bwc)
                    E.emit("tensor", "matmul", ps_c[:], selbb[:, 128:256],
                           rab[1][:, 512 * g4:512 * (g4 + 1)],
                           start=False, stop=True, inc=s_nb)
                    E.emit("vector", "tensor_tensor",
                           zt[:, S * mch + 512 * g4: S * mch + 512 * (g4 + 1)],
                           zt[:, S * mch + 512 * g4: S * mch + 512 * (g4 + 1)],
                           ps_c[:], ALU.mult,
                           waits=[(s_nb, E.val(s_nb))], inc=s_nrm)
                    rpc.release(inst_c, s_nrm, E.val(s_nrm))
        nrm_done = E.val(s_nrm)

        for i in range(NKT):
            xr, bwx, inst_x = rxr.acquire()
            E.emit("gpsimd", "dma_start", xres[:, D * xr: D * (xr + 1)],
                   x_in[xoff + 128 * i: xoff + 128 * (i + 1), :],
                   waits=bwx, inc=s_w, incby=16)
            xr_ld = E.val(s_w)
            so, bwos, inst_os = ros.acquire()
            for half in range(2):
                j, bw, inst = rpa.acquire()
                waits = [(s_nrm, nrm_done), (s_w, wo_ld)] + bw + (bwos if half == 0 else [])
                kloop(ps_a[j][:],
                      [zt[:, S * p + 128 * i: S * p + 128 * (i + 1)] for p in range(8)],
                      [wslice(jo, p, 512 * half, 512 * (half + 1)) for p in range(8)],
                      waits, s_p1)
                E.emit("vector", "tensor_tensor",
                       osb[:, D * so + 512 * half: D * so + 512 * (half + 1)],
                       ps_a[j][:],
                       xres[:, D * xr + 512 * half: D * xr + 512 * (half + 1)],
                       ALU.add,
                       waits=[(s_p1, E.val(s_p1)), (s_w, xr_ld)], inc=s_res)
                rpa.release(inst, s_res, E.val(s_res))
            rxr.release(inst_x, s_res, E.val(s_res))
            E.emit("sync", "dma_start",
                   tmp1_out[xoff + 128 * i: xoff + 128 * (i + 1), :],
                   osb[:, D * so: D * (so + 1)],
                   waits=[(s_res, E.val(s_res))], inc=s_st, incby=16)
            ros.release(inst_os, s_st, E.val(s_st))
        res_total = E.val(s_res)
        rwb.release(io, s_p1, E.val(s_p1))

    E.wait("sync", s_st)


def build_l0_ffn(T, debug=False):
    S = T // BPC
    NSL = 64
    nc = new_nc()
    ext = {
        "xraw": nc.dram_tensor("xraw", [T, D], F32, kind="ExternalInput").ap(),
        "stats": nc.dram_tensor("stats", [128, 2 * (T // 128)], F32, kind="ExternalInput").ap(),
        "xn": nc.dram_tensor("xn", [T, D], BF16, kind="ExternalInput").ap(),
        "tmp1": nc.dram_tensor("tmp1", [T, D], BF16, kind="ExternalInput").ap(),
        "slot_init": nc.dram_tensor("slot_init", [NSL, D], F32, kind="ExternalInput").ap(),
        "w1": nc.dram_tensor("w1", [2 * D, 512], F32, kind="ExternalInput").ap(),
        "b1": nc.dram_tensor("b1", [512], F32, kind="ExternalInput").ap(),
        "w2": nc.dram_tensor("w2", [512, NSL], F32, kind="ExternalInput").ap(),
        "b2": nc.dram_tensor("b2", [NSL], F32, kind="ExternalInput").ap(),
        "tau": nc.dram_tensor("tau", [1], F32, kind="ExternalInput").ap(),
        "WihT": nc.dram_tensor("WihT", [D, 3 * D], F32, kind="ExternalInput").ap(),
        "WhhT": nc.dram_tensor("WhhT", [D, 3 * D], F32, kind="ExternalInput").ap(),
        "bih": nc.dram_tensor("bih", [3 * D], F32, kind="ExternalInput").ap(),
        "bhh": nc.dram_tensor("bhh", [3 * D], F32, kind="ExternalInput").ap(),
        "wqs": nc.dram_tensor("wqs", [D, 256], F32, kind="ExternalInput").ap(),
        "bqs": nc.dram_tensor("bqs", [256], F32, kind="ExternalInput").ap(),
        "wks": nc.dram_tensor("wks", [D, 256], F32, kind="ExternalInput").ap(),
        "bks": nc.dram_tensor("bks", [256], F32, kind="ExternalInput").ap(),
        "wvs": nc.dram_tensor("wvs", [D, D], F32, kind="ExternalInput").ap(),
        "bvs": nc.dram_tensor("bvs", [D], F32, kind="ExternalInput").ap(),
        "lng": nc.dram_tensor("lng", [D], F32, kind="ExternalInput").ap(),
        "lnb": nc.dram_tensor("lnb", [D], F32, kind="ExternalInput").ap(),
        "wvp": nc.dram_tensor("wvp", [D, D], F32, kind="ExternalInput").ap(),
        "bvp": nc.dram_tensor("bvp", [D], F32, kind="ExternalInput").ap(),
        "wo": nc.dram_tensor("wo", [D, D], F32, kind="ExternalInput").ap(),
        "bo": nc.dram_tensor("bo", [D], F32, kind="ExternalInput").ap(),
        "ident": nc.dram_tensor("ident", [128, 128], F32, kind="ExternalInput").ap(),
        "bdmask": nc.dram_tensor("bdmask", [128, 128], F32, kind="ExternalInput").ap(),
        "x1": nc.dram_tensor("x1", [T, D], BF16, kind="ExternalOutput").ap(),
    }
    dbg = {}
    if debug:
        for nm, shp, dt in (("d_alphaT", [128, T], BF16), ("d_siT", [128, 1024], BF16),
                            ("d_rdcol", [128, 1], F32), ("d_gib", [128, 3 * D], BF16),
                            ("d_ghb", [128, 3 * D], BF16), ("d_snew", [128, D], F32),
                            ("d_svb", [128, D], BF16), ("d_esc", [128, 128], F32),
                            ("d_vb", [128, D], BF16), ("d_ycur", [128, D], F32),
                            ("d_midT", [128, 8 * S], BF16),
                            ("d_ctile", [128, 1], F32), ("d_m8", [128, 8], F32),
                            ("d_e8", [128, 8], F32), ("d_den8", [128, 1], F32),
                            ("d_rden8", [128, 1], F32), ("d_negm1c", [128, 1], F32),
                            ("d_al0", [128, 64], BF16), ("d_lgt0", [128, 64], F32),
                            ("d_attnb", [128, 128], BF16), ("d_dsc", [128, 1], F32),
                            ("d_rdsc", [128, 1], F32), ("d_var64", [128, 1], F32),
                            ("d_nm64", [128, 1], F32), ("d_ypre", [128, D], F32),
                            ("d_rs64", [128, 1], F32), ("d_mu64", [128, 1], F32),
                            ("d_snewb", [128, D], BF16), ("d_attnT", [128, 128], BF16)):
            dbg[nm] = nc.dram_tensor(nm, shp, dt, kind="ExternalOutput").ap()
    E = Em(nc)
    emit_l0_ffn(nc, E, T, ext, debug=debug, dbg=dbg)
    E.finalize()
    return nc


def emit_l0_ffn(nc, E, T, ext, ns="", salloc=None, palloc=None, debug=False, dbg=None):
    S = T // BPC
    NG = S // 512
    NTS = S // 128
    NSL = 64
    xraw_in = ext["xraw"]; stats_in = ext["stats"]; xn_in = ext["xn"]
    tmp1_in = ext["tmp1"]; sinit_in = ext["slot_init"]
    w1_in = ext["w1"]; b1_in = ext["b1"]; w2_in = ext["w2"]; b2_in = ext["b2"]
    tau_in = ext["tau"]; wih_in = ext["WihT"]; whh_in = ext["WhhT"]
    bih_in = ext["bih"]; bhh_in = ext["bhh"]
    wqs_in = ext["wqs"]; bqs_in = ext["bqs"]; wks_in = ext["wks"]; bks_in = ext["bks"]
    wvs_in = ext["wvs"]; bvs_in = ext["bvs"]; lng_in = ext["lng"]; lnb_in = ext["lnb"]
    wvp_in = ext["wvp"]; bvp_in = ext["bvp"]; wo_in = ext["wo"]; bo_in = ext["bo"]
    ident_in = ext["ident"]; bdmask_in = ext["bdmask"]
    x1_out = ext["x1"]
    sbA = salloc or (lambda n, shp, dt=F32: nc.alloc_sbuf_tensor(n, shp, dt).ap())
    paA = palloc or (lambda n, shp, dt=F32: nc.alloc_psum_tensor(n, shp, dt).ap())
    sb = lambda n, shp, dt=F32: sbA(ns + n, shp, dt)

    big0 = sb("big0", [128, 16 * D], BF16)
    wvpb = big0[:, 0:8 * D]
    wob = big0[:, 8 * D:16 * D]
    big1 = sb("big1", [128, 8 * D], BF16)
    big1f = big1[:].bitcast(F32)
    xnf32 = big1f[:, 0:1024]
    xnTf = big1f[:, 1024:2048]
    htile = big1f[:, 2048:2560]
    hTf = big1f[:, 2560:3072]
    wvsb = big1[:, 0:8 * D]
    big2 = sb("big2", [128, 16 * 512], BF16)
    wgih = big2[:, 0:8 * 512]
    wghh = big2[:, 8 * 512:16 * 512]
    midT = sb("midT", [128, 8 * S], BF16)
    w1f = midT[:].bitcast(F32)[:, 0:8 * 512]
    sinitf = w1f[0:NSL, 0:1024]

    wqb = sb("wqb", [128, 8 * 256], BF16)
    wkb = sb("wkb", [128, 8 * 256], BF16)
    identb = sb("identb", [128, 128], BF16)
    identf = sb("identf", [128, 128], F32)
    bdmb = sb("bdmb", [128, 128], BF16)
    sinit2 = sb("sinit2", [128, D], BF16)
    sinit = sinit2[0:NSL, :]
    sinitT2 = sb("sinitT2", [128, 8 * 128], BF16)
    smTf = sb("smTf", [128, 8])
    statsb = sb("statsb", [128, 2 * (T // 128)])
    xrawt = sb("xrawt", [128, D])
    w2f = sb("w2f", [128, 4 * NSL])
    b1f = sb("b1f", [1, 512]); b2f = sb("b2f", [1, NSL])
    onesf = sb("onesf", [1, 128]); onec64f = sb("onec64f", [NSL, 1])
    biashrow = sb("biashrow", [1, 512])
    onesb = sb("onesb", [1, 512], BF16)
    onec128 = sb("onec128", [128, 1], BF16)
    rows = {}
    for nm, ln in (("bih", 3 * D), ("bhh", 3 * D),
                   ("bqs", 256), ("bks", 256), ("bvs", D), ("lng", D), ("lnb", D),
                   ("bvp", D), ("bo", D)):
        rows[nm] = sb("row_" + nm, [1, ln], BF16)
    taur = sb("taur", [1, 1]); taua = sb("taua", [1, 1]); taub2 = sb("taub2", [1, 1])
    ctaub = sb("ctaub", [1, 1], BF16); ctile = sb("ctile", [128, 1])
    lgt = sb("lgt", [128, 2 * NSL])
    albuf = sb("albuf", [128, 2 * 128], BF16)
    alphaT = sb("alphaT", [128, BPC * S], BF16)
    siT = sb("siT", [128, 8 * 128], BF16)
    slin = sb("slin", [NSL, D], BF16)
    rdcol = sb("rdcol", [128, 1]); rdcol2 = sb("rdcol2", [128, 1])
    m8 = sb("m8", [128, 8]); e8 = sb("e8", [128, 8])
    negm1 = sb("negm1", [128, 1]); negm1c = sb("negm1c", [128, 1])
    mask64 = sb("mask64", [128, NSL]); e64 = sb("e64", [128, NSL])
    tmask = sb("tmask", [128, NSL])
    den8 = sb("den8", [128, 1]); rden8 = sb("rden8", [128, 1])
    xntile = sb("xntile", [128, 2 * D], BF16)
    ghb = sb("ghb", [128, 3 * D], BF16)
    gib = sb("gib", [128, 3 * D], BF16)
    gr = sb("gr", [128, D], BF16); gz = sb("gz", [128, D], BF16)
    gn = sb("gn", [128, D], BF16)
    gt1 = sb("gt1", [128, D], BF16); gt2 = sb("gt2", [128, D], BF16)
    snew = sb("snew", [128, D]); snewb = sb("snewb", [128, D], BF16)
    snewT = sb("snewT", [128, 8 * 128], BF16)
    qTb = sb("qTb", [128, 2 * 128], BF16); kTb = sb("kTb", [128, 2 * 128], BF16)
    vb = sb("vb", [128, D], BF16)
    nm64 = sb("nm64", [128, 1]); esc = sb("esc", [128, 128])
    dsc = sb("dsc", [128, 1]); rdsc = sb("rdsc", [128, 1])
    attnb = sb("attnb", [128, 128], BF16); attnTb = sb("attnTb", [128, 128], BF16)
    ycur = snew
    mu64 = sb("mu64", [128, 1]); musq64 = sb("musq64", [128, 1])
    sq64 = sb("sq64", [128, 1]); var64 = sb("var64", [128, 1])
    lnt64 = sb("lnt64", [128, 1]); rs64 = sb("rs64", [128, 1])
    junk64 = None
    ybb = snewb
    ybT = sb("ybT", [128, 8 * 128], BF16)
    svb = sb("svb", [128, D], BF16)
    t1tile = sb("t1tile", [128, 2 * D], BF16)
    x1sb = sb("x1sb", [128, 2 * D], BF16)

    ps1 = [paA(ns + f"f_ps{j}", [128, 512], F32) for j in range(2)]
    ps_si = [paA(ns + f"f_pssi{h}", [NSL, 512], F32) for h in range(2)]
    ps_d = paA(ns + "f_psd", [128, 1], F32)
    ps_tr = [paA(ns + f"f_pstr{j}", [128, 128], BF16) for j in range(2)]
    ps_trf = paA(ns + "f_pstrf", [128, 128], F32)

    _s = lambda n: E.sem(ns + n)
    s_w = _s("s_w"); s_ld = _s("s_ld")
    s_pe = _s("s_pe"); s_ev = _s("s_ev"); s_act = _s("s_act")
    s_hz = _s("s_hz"); s_st = _s("s_st"); s_tr = _s("s_tr")

    r1 = Rot(2); rtr = Rot(2); rtrf = Rot(1)
    rlg = Rot(2); ral = Rot(2); rxn = Rot(2); rt1r = Rot(2); rx1 = Rot(2)

    def g_cast(dst, src, waits=()):
        E.emit("gpsimd", "dma_start", dst, src, waits=waits, inc=s_w, incby=16)
        return E.val(s_w)

    def ld_chunked(dst, src, c, waits=()):
        E.emit("gpsimd", "dma_start", dst.rearrange("p (c n) -> p c n", c=c),
               src.rearrange("(c p) n -> p c n", p=128), waits=waits, inc=s_w, incby=16)
        return E.val(s_w)

    def evac(dst, src_ps, wpe, inc=s_ev, extra=()):
        E.emit("vector", "tensor_copy", dst, src_ps,
               waits=[(s_pe, wpe)] + list(extra), inc=inc)

    w_id = g_cast(identb[:], ident_in)
    w_bd = g_cast(bdmb[:], bdmask_in)
    E.emit("sync", "dma_start", identf[:], ident_in, inc=s_ld, incby=16)
    w_idf = E.val(s_ld)
    w_sinit2a = g_cast(sinit2[0:NSL, :], sinit_in)
    w_sinit2b = g_cast(sinit2[NSL:128, :], sinit_in)
    w_sinit = w_sinit2b
    E.emit("sync", "dma_start", statsb[:], stats_in, inc=s_ld, incby=16)
    E.emit("sync", "dma_start", sinitf, sinit_in, inc=s_ld, incby=16)
    E.emit("sync", "dma_start", w2f[:].rearrange("p (c n) -> p c n", c=4),
           w2_in.rearrange("(c p) n -> p c n", p=128), inc=s_ld, incby=16)
    E.emit("sync", "dma_start", b1f[:], b1_in.unsqueeze(0), inc=s_ld, incby=16)
    E.emit("sync", "dma_start", b2f[:], b2_in.unsqueeze(0), inc=s_ld, incby=16)
    w_f32c = E.val(s_ld)
    for nm, src in (("bih", bih_in), ("bhh", bhh_in),
                    ("bqs", bqs_in), ("bks", bks_in), ("bvs", bvs_in), ("lng", lng_in),
                    ("lnb", lnb_in), ("bvp", bvp_in), ("bo", bo_in)):
        g_cast(rows[nm], src.unsqueeze(0))
    w_rows = E.val(s_w)
    w_wq = ld_chunked(wqb, wqs_in, 8)
    w_wk = ld_chunked(wkb, wks_in, 8)
    E.emit("sync", "dma_start", taur[:], tau_in.unsqueeze(0), inc=s_ld, incby=16)
    w_tau = E.val(s_ld)

    E.emit("vector", "memset", onesb[:], 1.0)
    E.emit("vector", "memset", onec128[:], 1.0)
    E.emit("vector", "memset", alphaT[:], 0.0)
    E.emit("vector", "memset", onesf[:], 1.0)
    E.emit("vector", "memset", onec64f[:], 1.0 / NSL, inc=s_ev)
    prep_ev = E.val(s_ev)

    for c in range(8):
        j, bw, it = rtr.acquire()
        E.emit("tensor", "transpose", ps_tr[j][:],
               sinit2[:, 128 * c:128 * (c + 1)], identb[:],
               waits=[(s_w, max(w_sinit2b, w_id))] + bw, inc=s_tr)
        evac(sinitT2[:, 128 * c:128 * (c + 1)], ps_tr[j][:], 0,
             extra=[(s_tr, E.val(s_tr))])
        rtr.release(it, s_ev, E.val(s_ev))
    for c in range(8):
        j, bw, it = r1.acquire()
        E.emit("tensor", "matmul", ps1[j][0:128, 0:1],
               sinitf[:, 128 * c:128 * (c + 1)], onec64f[:],
               start=True, stop=True,
               waits=[(s_ld, w_f32c), (s_ev, prep_ev)] + bw, inc=s_pe)
        evac(smTf[:, c:c + 1], ps1[j][0:128, 0:1], E.val(s_pe))
        r1.release(it, s_ev, E.val(s_ev))
    E.emit("sync", "dma_start", w1f.rearrange("p (c n) -> p c n", c=8),
           w1_in[D:2 * D, :].rearrange("(c p) n -> p c n", p=128),
           waits=[(s_ev, E.val(s_ev))], inc=s_ld, incby=16)
    w_w1b = E.val(s_ld)
    j, bw, it = r1.acquire()
    for c in range(8):
        E.emit("tensor", "matmul", ps1[j][0:1, :],
               smTf[:, c:c + 1], w1f[:, 512 * c:512 * (c + 1)],
               start=(c == 0), stop=False,
               waits=([(s_ld, w_w1b), (s_ev, E.val(s_ev))] + bw) if c == 0 else ())
    E.emit("tensor", "matmul", ps1[j][0:1, :], onesf[:, 0:1], b1f[:],
           start=False, stop=True, inc=s_pe)
    evac(biashrow[:], ps1[j][0:1, :], E.val(s_pe))
    r1.release(it, s_ev, E.val(s_ev))
    E.emit("sync", "dma_start", w1f.rearrange("p (c n) -> p c n", c=8),
           w1_in[0:D, :].rearrange("(c p) n -> p c n", p=128),
           waits=[(s_pe, E.val(s_pe))], inc=s_ld, incby=16)
    w_w1t = E.val(s_ld)
    E.emit("scalar", "activation", taua[:], taur[:], AF.Abs,
           waits=[(s_ld, w_tau)], inc=s_act)
    E.emit("vector", "tensor_scalar_add", taua[:], taua[:], 0.1,
           waits=[(s_act, E.val(s_act))], inc=s_hz)
    E.emit("vector", "reciprocal", taub2[:], taua[:],
           waits=[(s_hz, E.val(s_hz))])
    E.emit("vector", "tensor_copy", ctaub[:], taub2[:], inc=s_ev)
    j, bw, it = r1.acquire()
    E.emit("tensor", "matmul", ps1[j][0:128, 0:1], onesb[:, 0:128], ctaub[:],
           start=True, stop=True, waits=[(s_ev, E.val(s_ev))] + bw, inc=s_pe)
    evac(ctile[:], ps1[j][0:128, 0:1], E.val(s_pe))
    r1.release(it, s_ev, E.val(s_ev))

    pe_ht_done = 0
    for b in range(BPC):
        xoff = b * S
        for i in range(NTS):
            gidx = b * NTS + i
            E.emit("sync", "dma_start", xrawt[:],
                   xraw_in[xoff + 128 * i: xoff + 128 * (i + 1), :],
                   waits=[(s_ev, E.val(s_ev))], inc=s_ld, incby=16)
            E.emit("vector", "tensor_scalar", xnf32, xrawt[:],
                   statsb[:, 2 * gidx:2 * gidx + 1],
                   statsb[:, 2 * gidx + 1:2 * gidx + 2],
                   ALU.subtract, ALU.mult,
                   waits=[(s_ld, E.val(s_ld)), (s_ld, w_f32c)], inc=s_ev)
            xnf_ev = E.val(s_ev)
            for c in range(8):
                jtf, bwtf, itf = rtrf.acquire()
                E.emit("tensor", "transpose", ps_trf[:],
                       xnf32[:, 128 * c:128 * (c + 1)], identf[:],
                       waits=[(s_ev, xnf_ev), (s_ld, w_idf)] + bwtf, inc=s_tr)
                E.emit("vector", "tensor_copy", xnTf[:, 128 * c:128 * (c + 1)],
                       ps_trf[:], waits=[(s_tr, E.val(s_tr))], inc=s_ev)
                rtrf.release(itf, s_ev, E.val(s_ev))
            xnT_ev = E.val(s_ev)
            j, bw, it = r1.acquire()
            for c in range(8):
                E.emit("tensor", "matmul", ps1[j][:],
                       xnTf[:, 128 * c:128 * (c + 1)], w1f[:, 512 * c:512 * (c + 1)],
                       start=(c == 0), stop=False,
                       waits=([(s_ld, w_w1t), (s_ev, xnT_ev)] + bw) if c == 0 else ())
            E.emit("tensor", "matmul", ps1[j][:], onesf[:], biashrow[:],
                   start=False, stop=True, inc=s_pe)
            E.emit("scalar", "activation", htile, ps1[j][:], AF.Gelu,
                   waits=[(s_pe, E.val(s_pe))], inc=s_act)
            r1.release(it, s_act, E.val(s_act))
            for c in range(4):
                jtf, bwtf, itf = rtrf.acquire()
                E.emit("tensor", "transpose", ps_trf[:],
                       htile[:, 128 * c:128 * (c + 1)], identf[:],
                       waits=[(s_act, E.val(s_act))] + bwtf, inc=s_tr)
                E.emit("vector", "tensor_copy", hTf[:, 128 * c:128 * (c + 1)],
                       ps_trf[:], waits=[(s_tr, E.val(s_tr))], inc=s_ev)
                rtrf.release(itf, s_ev, E.val(s_ev))
            hT_ev = E.val(s_ev)
            jl, bwl, il = rlg.acquire()
            j, bw, it = r1.acquire()
            for c in range(4):
                E.emit("tensor", "matmul", ps1[j][:, 0:NSL],
                       hTf[:, 128 * c:128 * (c + 1)], w2f[:, NSL * c:NSL * (c + 1)],
                       start=(c == 0), stop=False,
                       waits=([(s_ld, w_f32c), (s_ev, hT_ev)] + bw + bwl)
                             if c == 0 else ())
            E.emit("tensor", "matmul", ps1[j][:, 0:NSL], onesf[:], b2f[:],
                   start=False, stop=True, inc=s_pe)
            E.emit("vector", "tensor_copy", lgt[:, NSL * jl:NSL * (jl + 1)],
                   ps1[j][:, 0:NSL], waits=[(s_pe, E.val(s_pe))], inc=s_ev)
            r1.release(it, s_ev, E.val(s_ev))
            lg_ready = E.val(s_ev)
            lgv = lgt[:, NSL * jl:NSL * (jl + 1)]
            jx, bwx, ix = rxn.acquire()
            E.emit("sync", "dma_start", xntile[:, D * jx:D * (jx + 1)],
                   xn_in[xoff + 128 * i: xoff + 128 * (i + 1), :],
                   waits=bwx, inc=s_ld, incby=16)
            ld_xn = E.val(s_ld)
            E.emit("vector", "max", out=m8[:], in_=lgv, waits=[(s_ev, lg_ready)])
            E.emit("vector", "tensor_reduce", negm1[:], lgv, AX.X, ALU.max, negate=True)
            E.emit("vector", "tensor_tensor", negm1c[:], negm1[:], ctile[:], ALU.mult)
            E.emit("vector", "tensor_scalar", mask64[:], lgv, m8[:, 7:8], None,
                   ALU.is_ge, inc=s_ev)
            E.emit("scalar", "activation", e8[:], m8[:], AF.Exp,
                   scale=ctile[:], bias=negm1c[:], waits=[(s_ev, E.val(s_ev))])
            E.emit("scalar", "activation", e64[:], lgv, AF.Exp,
                   scale=ctile[:], bias=negm1c[:], inc=s_act)
            ja, bwa, ia = ral.acquire()
            alv = albuf[:, 128 * ja + 64 * b: 128 * ja + 64 * (b + 1)]
            alz = albuf[:, 128 * ja + 64 * (1 - b): 128 * ja + 64 * (2 - b)]
            alfull = albuf[:, 128 * ja:128 * (ja + 1)]
            E.emit("vector", "memset", alz, 0.0, waits=bwa)
            E.emit("vector", "tensor_reduce", den8[:], e8[:], AX.X, ALU.add,
                   waits=[(s_act, E.val(s_act))], inc=s_hz)
            E.emit("vector", "reciprocal", rden8[:], den8[:],
                   waits=[(s_hz, E.val(s_hz))], inc=s_hz)
            E.emit("vector", "tensor_tensor", tmask[:], e64[:], mask64[:], ALU.mult)
            E.emit("vector", "tensor_scalar_mul", alv, tmask[:], rden8[:],
                   waits=[(s_hz, E.val(s_hz))], inc=s_ev)
            al_ready = E.val(s_ev)
            if debug and b == 0 and i == 0:
                for nm, sap in (("d_ctile", ctile[:]), ("d_m8", m8[:]), ("d_e8", e8[:]),
                                ("d_den8", den8[:]), ("d_rden8", rden8[:]),
                                ("d_negm1c", negm1c[:]), ("d_al0", alv),
                                ("d_lgt0", lgv)):
                    E.emit("sync", "dma_start", dbg[nm], sap,
                           waits=[(s_ev, al_ready)], inc=s_st, incby=16)
            jt, bwt, itr = rtr.acquire()
            E.emit("tensor", "transpose", ps_tr[jt][:], alfull, identb[:],
                   waits=[(s_ev, al_ready), (s_w, w_id)] + bwt, inc=s_tr)
            E.emit("vector", "tensor_copy",
                   alphaT[64 * b:64 * (b + 1), xoff + 128 * i: xoff + 128 * (i + 1)],
                   ps_tr[jt][64 * b:64 * (b + 1), :],
                   waits=[(s_tr, E.val(s_tr))], inc=s_ev)
            rtr.release(itr, s_ev, E.val(s_ev))
            for hh in range(2):
                E.emit("tensor", "matmul", ps_si[hh][:],
                       alv, xntile[:, D * jx + 512 * hh: D * jx + 512 * (hh + 1)],
                       start=(i == 0), stop=(i == NTS - 1),
                       waits=[(s_ev, al_ready), (s_ld, ld_xn)] if hh == 0 else ())
            E.emit("tensor", "matmul", ps_d[:], alfull, onec128[:],
                   start=(b == 0 and i == 0), stop=(b == BPC - 1 and i == NTS - 1),
                   inc=s_pe)
            ral.release(ia, s_pe, E.val(s_pe))
            rxn.release(ix, s_pe, E.val(s_pe))
        si_pe = E.val(s_pe)
        for hh in range(2):
            E.emit("vector", "tensor_copy", slin[:, 512 * hh:512 * (hh + 1)],
                   ps_si[hh][:], waits=[(s_pe, si_pe)], inc=s_ev)
        slin_ev = E.val(s_ev)
        for c in range(8):
            j, bwt, it = rtr.acquire()
            E.emit("tensor", "transpose", ps_tr[j][0:128, 0:NSL],
                   slin[:, 128 * c:128 * (c + 1)], identb[0:NSL, 0:NSL],
                   waits=[(s_ev, slin_ev), (s_w, w_id)] + bwt, inc=s_tr)
            E.emit("vector", "tensor_copy",
                   siT[:, 128 * c + 64 * b: 128 * c + 64 * (b + 1)],
                   ps_tr[j][0:128, 0:NSL], waits=[(s_tr, E.val(s_tr))], inc=s_ev)
            rtr.release(it, s_ev, E.val(s_ev))
    E.emit("vector", "tensor_scalar", rdcol2[:], ps_d[:],
           1e-8, None, ALU.add, waits=[(s_pe, E.val(s_pe))], inc=s_hz)
    E.emit("vector", "reciprocal", rdcol[:], rdcol2[:],
           waits=[(s_hz, E.val(s_hz))], inc=s_hz)
    rd_hz = E.val(s_hz)
    siT_ev = E.val(s_ev)

    pe_gs_done = pe_ht_done
    for gs in range(6):
        wihw = [(s_pe, pe_gs_done)]
        w_ih = ld_chunked(wgih, wih_in[:, 512 * gs:512 * (gs + 1)], 8, waits=wihw)
        w_hh = ld_chunked(wghh, whh_in[:, 512 * gs:512 * (gs + 1)], 8)
        j, bw, it = r1.acquire()
        for c in range(8):
            E.emit("tensor", "matmul", ps1[j][:],
                   sinitT2[:, 128 * c:128 * (c + 1)],
                   wghh[:, 512 * c:512 * (c + 1)],
                   start=(c == 0), stop=False,
                   waits=([(s_w, w_hh), (s_ev, siT_ev)] + bw) if c == 0 else ())
        E.emit("tensor", "matmul", ps1[j][:], onesb[:, 0:128],
               rows["bhh"][:, 512 * gs:512 * (gs + 1)], start=False, stop=True,
               inc=s_pe)
        evac(ghb[:, 512 * gs:512 * (gs + 1)], ps1[j][:], E.val(s_pe))
        r1.release(it, s_ev, E.val(s_ev))
        j, bw, it = r1.acquire()
        for c in range(8):
            E.emit("tensor", "matmul", ps1[j][:],
                   siT[:, 128 * c:128 * (c + 1)],
                   wgih[:, 512 * c:512 * (c + 1)],
                   start=(c == 0), stop=(c == 7),
                   waits=([(s_w, w_ih)] + bw) if c == 0 else (),
                   inc=(s_pe if c == 7 else None))
        E.emit("vector", "tensor_scalar_mul", gib[:, 512 * gs:512 * (gs + 1)],
               ps1[j][:], rdcol[:],
               waits=[(s_pe, E.val(s_pe)), (s_hz, rd_hz)], inc=s_ev)
        r1.release(it, s_ev, E.val(s_ev))
        j, bw, it = r1.acquire()
        E.emit("tensor", "matmul", ps1[j][:], onesb[:, 0:128],
               rows["bih"][:, 512 * gs:512 * (gs + 1)],
               start=True, stop=True, waits=[(s_w, w_rows)] + bw, inc=s_pe)
        E.emit("vector", "tensor_tensor", gib[:, 512 * gs:512 * (gs + 1)],
               gib[:, 512 * gs:512 * (gs + 1)], ps1[j][:], ALU.add,
               waits=[(s_pe, E.val(s_pe))], inc=s_ev)
        r1.release(it, s_ev, E.val(s_ev))
        pe_gs_done = E.val(s_pe)
    gi_done = E.val(s_ev)

    E.emit("vector", "tensor_tensor", gt1[:], gib[:, 0:D], ghb[:, 0:D], ALU.add,
           waits=[(s_ev, gi_done)], inc=s_ev)
    E.emit("vector", "tensor_tensor", gt2[:], gib[:, D:2 * D], ghb[:, D:2 * D],
           ALU.add, inc=s_ev)
    E.emit("scalar", "activation", gr[:], gt1[:], AF.Sigmoid,
           waits=[(s_ev, E.val(s_ev))])
    E.emit("scalar", "activation", gz[:], gt2[:], AF.Sigmoid, inc=s_act)
    E.emit("vector", "tensor_tensor", gt1[:], gr[:], ghb[:, 2 * D:3 * D], ALU.mult,
           waits=[(s_act, E.val(s_act))])
    E.emit("vector", "tensor_tensor", gt1[:], gt1[:], gib[:, 2 * D:3 * D], ALU.add,
           inc=s_ev)
    E.emit("scalar", "activation", gn[:], gt1[:], AF.Tanh,
           waits=[(s_ev, E.val(s_ev))], inc=s_act)
    E.emit("vector", "tensor_tensor", gt2[:], sinit2[:], gn[:], ALU.subtract,
           waits=[(s_act, E.val(s_act)), (s_w, w_sinit2b)])
    E.emit("vector", "tensor_tensor", gt2[:], gz[:], gt2[:], ALU.mult)
    E.emit("vector", "tensor_tensor", snew[:], gn[:], gt2[:], ALU.add)
    E.emit("vector", "tensor_copy", snewb[:], snew[:], inc=s_ev)
    snew_ev = E.val(s_ev)
    if debug:
        E.emit("sync", "dma_start", dbg["d_snewb"], snewb[:],
               waits=[(s_ev, snew_ev)], inc=s_st, incby=16)
    for c in range(8):
        j, bw, it = rtr.acquire()
        E.emit("tensor", "transpose", ps_tr[j][:],
               snewb[:, 128 * c:128 * (c + 1)], identb[:],
               waits=[(s_ev, snew_ev)] + bw, inc=s_tr)
        evac(snewT[:, 128 * c:128 * (c + 1)], ps_tr[j][:], 0,
             extra=[(s_tr, E.val(s_tr))])
        rtr.release(it, s_ev, E.val(s_ev))
    snewT_ev = E.val(s_ev)

    w_wvs = ld_chunked(wvsb, wvs_in, 8, waits=[(s_pe, si_pe)])
    w_wvp = ld_chunked(wvpb, wvp_in, 8, waits=[(s_pe, pe_ht_done)])
    w_wo = ld_chunked(wob, wo_in, 8)

    for (dst, wsrc, w_rdy, brow) in ((qTb, wqb, w_wq, "bqs"), (kTb, wkb, w_wk, "bks")):
        for qc in range(2):
            j, bw, it = r1.acquire()
            for c in range(8):
                E.emit("tensor", "matmul", ps1[j][0:128, 0:128],
                       wsrc[:, 256 * c + 128 * qc: 256 * c + 128 * (qc + 1)],
                       snewT[:, 128 * c:128 * (c + 1)],
                       start=(c == 0), stop=False,
                       waits=([(s_w, w_rdy), (s_ev, snewT_ev)] + bw) if c == 0 else ())
            E.emit("tensor", "matmul", ps1[j][0:128, 0:128],
                   rows[brow][:, 128 * qc:128 * (qc + 1)], onesb[:, 0:128],
                   start=False, stop=True, inc=s_pe)
            evac(dst[:, 128 * qc:128 * (qc + 1)], ps1[j][0:128, 0:128], E.val(s_pe))
            r1.release(it, s_ev, E.val(s_ev))
    for half in range(2):
        j, bw, it = r1.acquire()
        for c in range(8):
            E.emit("tensor", "matmul", ps1[j][:],
                   snewT[:, 128 * c:128 * (c + 1)],
                   wvsb[:, D * c + 512 * half: D * c + 512 * (half + 1)],
                   start=(c == 0), stop=False,
                   waits=([(s_w, w_wvs), (s_ev, snewT_ev)] + bw) if c == 0 else ())
        E.emit("tensor", "matmul", ps1[j][:], onesb[:, 0:128],
               rows["bvs"][:, 512 * half:512 * (half + 1)],
               start=False, stop=True, inc=s_pe)
        evac(vb[:, 512 * half:512 * (half + 1)], ps1[j][:], E.val(s_pe))
        r1.release(it, s_ev, E.val(s_ev))
    qkv_ev = E.val(s_ev)

    j, bw, it = r1.acquire()
    for qc in range(2):
        E.emit("tensor", "matmul", ps1[j][0:128, 0:128],
               qTb[:, 128 * qc:128 * (qc + 1)], kTb[:, 128 * qc:128 * (qc + 1)],
               start=(qc == 0), stop=(qc == 1),
               waits=([(s_ev, qkv_ev)] + bw) if qc == 0 else (),
               inc=(s_pe if qc == 1 else None))
    sc_pe = E.val(s_pe)
    E.emit("vector", "tensor_reduce", nm64[:], ps1[j][0:128, 0:128], AX.X, ALU.max,
           negate=True, waits=[(s_pe, sc_pe)])
    E.emit("vector", "tensor_scalar_mul", nm64[:], nm64[:], 256.0 ** -0.5, inc=s_ev)
    E.emit("scalar", "activation", esc[:], ps1[j][0:128, 0:128], AF.Exp,
           scale=256.0 ** -0.5, bias=nm64[:], waits=[(s_ev, E.val(s_ev))], inc=s_act)
    E.emit("vector", "tensor_tensor", esc[:], esc[:], bdmb[:], ALU.mult,
           waits=[(s_act, E.val(s_act)), (s_w, w_bd)])
    E.emit("vector", "tensor_reduce", dsc[:], esc[:], AX.X, ALU.add, inc=s_hz)
    E.emit("vector", "reciprocal", rdsc[:], dsc[:],
           waits=[(s_hz, E.val(s_hz))], inc=s_hz)
    E.emit("vector", "tensor_scalar_mul", attnb[:], esc[:], rdsc[:],
           waits=[(s_hz, E.val(s_hz))], inc=s_ev)
    r1.release(it, s_ev, E.val(s_ev))
    attn_ev = E.val(s_ev)
    jt, bwt, itr = rtr.acquire()
    E.emit("tensor", "transpose", ps_tr[jt][:], attnb[:], identb[:],
           waits=[(s_ev, attn_ev)] + bwt, inc=s_tr)
    E.emit("vector", "tensor_copy", attnTb[:], ps_tr[jt][:],
           waits=[(s_tr, E.val(s_tr))], inc=s_ev)
    if debug:
        E.emit("sync", "dma_start", dbg["d_attnT"], attnTb[:],
               waits=[(s_ev, E.val(s_ev))], inc=s_st, incby=16)
    rtr.release(itr, s_ev, E.val(s_ev))
    attnT_ev = E.val(s_ev)

    for half in range(2):
        j, bw, it = r1.acquire()
        E.emit("tensor", "matmul", ps1[j][:], attnTb[:],
               vb[:, 512 * half:512 * (half + 1)], start=True, stop=True,
               waits=[(s_ev, attnT_ev)] + bw, inc=s_pe)
        E.emit("vector", "tensor_tensor", ycur[:, 512 * half:512 * (half + 1)],
               snew[:, 512 * half:512 * (half + 1)], ps1[j][:], ALU.add,
               waits=[(s_pe, E.val(s_pe))], inc=s_ev)
        r1.release(it, s_ev, E.val(s_ev))
    y_ev = E.val(s_ev)
    ypre = big2[:, 0:2048].bitcast(F32) if debug else None
    if debug:
        E.emit("vector", "tensor_copy", ypre[:], ycur[:], waits=[(s_ev, y_ev)])
    E.emit("vector", "reduce_sum", mu64[:], ycur[:], AX.X, waits=[(s_ev, y_ev)])
    E.emit("vector", "tensor_scalar_mul", mu64[:], mu64[:], 1.0 / D)
    E.emit("vector", "tensor_tensor", musq64[:], mu64[:], mu64[:], ALU.mult)
    E.emit("vector", "tensor_scalar_add", musq64[:], musq64[:], -1e-5, inc=s_ev)
    E.emit("vector", "tensor_tensor", gt1[:], ycur[:], ycur[:], ALU.mult,
           waits=[(s_ev, E.val(s_ev))])
    E.emit("vector", "reduce_sum", sq64[:], gt1[:], AX.X)
    E.emit("vector", "scalar_tensor_tensor", var64[:], sq64[:], 1.0 / D, musq64[:],
           ALU.mult, ALU.subtract, inc=s_ev)
    E.emit("scalar", "activation", lnt64[:], var64[:], AF.Ln,
           waits=[(s_ev, E.val(s_ev))])
    E.emit("scalar", "activation", rs64[:], lnt64[:], AF.Exp, scale=-0.5, inc=s_act)
    E.emit("vector", "tensor_scalar", ycur[:], ycur[:], mu64[:], rs64[:],
           ALU.subtract, ALU.mult, waits=[(s_act, E.val(s_act))], inc=s_ev)
    for half in range(2):
        j, bw, it = r1.acquire()
        E.emit("tensor", "matmul", ps1[j][:], onesb[:, 0:128],
               rows["lng"][:, 512 * half:512 * (half + 1)], start=True, stop=True,
               waits=([(s_w, w_rows), (s_ev, E.val(s_ev))] + bw), inc=s_pe)
        E.emit("vector", "tensor_tensor", ycur[:, 512 * half:512 * (half + 1)],
               ycur[:, 512 * half:512 * (half + 1)], ps1[j][:], ALU.mult,
               waits=[(s_pe, E.val(s_pe))], inc=s_ev)
        r1.release(it, s_ev, E.val(s_ev))
        j, bw, it = r1.acquire()
        E.emit("tensor", "matmul", ps1[j][:], onesb[:, 0:128],
               rows["lnb"][:, 512 * half:512 * (half + 1)], start=True, stop=True,
               waits=([(s_w, w_rows)] + bw), inc=s_pe)
        E.emit("vector", "tensor_tensor", ycur[:, 512 * half:512 * (half + 1)],
               ycur[:, 512 * half:512 * (half + 1)], ps1[j][:], ALU.add,
               waits=[(s_pe, E.val(s_pe))], inc=s_ev)
        r1.release(it, s_ev, E.val(s_ev))
    E.emit("vector", "tensor_copy", ybb[:], ycur[:], inc=s_ev)
    yb_ev = E.val(s_ev)
    for c in range(8):
        j, bw, it = rtr.acquire()
        E.emit("tensor", "transpose", ps_tr[j][:],
               ybb[:, 128 * c:128 * (c + 1)], identb[:],
               waits=[(s_ev, yb_ev)] + bw, inc=s_tr)
        evac(ybT[:, 128 * c:128 * (c + 1)], ps_tr[j][:], 0,
             extra=[(s_tr, E.val(s_tr))])
        rtr.release(it, s_ev, E.val(s_ev))
    ybT_ev = E.val(s_ev)
    for half in range(2):
        j, bw, it = r1.acquire()
        for c in range(8):
            E.emit("tensor", "matmul", ps1[j][:],
                   ybT[:, 128 * c:128 * (c + 1)],
                   wvpb[:, D * c + 512 * half: D * c + 512 * (half + 1)],
                   start=(c == 0), stop=False,
                   waits=([(s_w, w_wvp), (s_ev, ybT_ev)] + bw) if c == 0 else ())
        E.emit("tensor", "matmul", ps1[j][:], onesb[:, 0:128],
               rows["bvp"][:, 512 * half:512 * (half + 1)],
               start=False, stop=True, inc=s_pe)
        evac(svb[:, 512 * half:512 * (half + 1)], ps1[j][:], E.val(s_pe))
        r1.release(it, s_ev, E.val(s_ev))
    sv_ev = E.val(s_ev)

    for b in range(BPC):
        for c in range(8):
            for g in range(NG):
                j, bw, it = r1.acquire()
                E.emit("tensor", "matmul", ps1[j][:],
                       svb[64 * b:64 * (b + 1), 128 * c:128 * (c + 1)],
                       alphaT[64 * b:64 * (b + 1), b * S + 512 * g: b * S + 512 * (g + 1)],
                       start=True, stop=True,
                       waits=[(s_ev, sv_ev)] + bw, inc=s_pe)
                evac(midT[:, S * c + 512 * g: S * c + 512 * (g + 1)],
                     ps1[j][:], E.val(s_pe))
                r1.release(it, s_ev, E.val(s_ev))
        mid_ev = E.val(s_ev)
        for i in range(NTS):
            jt1, bwt1, it1 = rt1r.acquire()
            E.emit("sync", "dma_start", t1tile[:, D * jt1:D * (jt1 + 1)],
                   tmp1_in[b * S + 128 * i: b * S + 128 * (i + 1), :],
                   waits=bwt1, inc=s_ld, incby=16)
            ld_t1 = E.val(s_ld)
            jx1, bwx1, ix1 = rx1.acquire()
            for half in range(2):
                j, bw, it = r1.acquire()
                for c in range(8):
                    E.emit("tensor", "matmul", ps1[j][:],
                           midT[:, S * c + 128 * i: S * c + 128 * (i + 1)],
                           wob[:, D * c + 512 * half: D * c + 512 * (half + 1)],
                           start=(c == 0), stop=False,
                           waits=([(s_w, w_wo), (s_ev, mid_ev)] + bw
                                  + (bwx1 if half == 0 else [])) if c == 0 else ())
                E.emit("tensor", "matmul", ps1[j][:], onesb[:, 0:128],
                       rows["bo"][:, 512 * half:512 * (half + 1)],
                       start=False, stop=True, inc=s_pe)
                E.emit("vector", "tensor_tensor",
                       x1sb[:, D * jx1 + 512 * half: D * jx1 + 512 * (half + 1)],
                       ps1[j][:],
                       t1tile[:, D * jt1 + 512 * half: D * jt1 + 512 * (half + 1)],
                       ALU.add, waits=[(s_pe, E.val(s_pe)), (s_ld, ld_t1)], inc=s_ev)
                r1.release(it, s_ev, E.val(s_ev))
            rt1r.release(it1, s_ev, E.val(s_ev))
            E.emit("sync", "dma_start", x1_out[b * S + 128 * i: b * S + 128 * (i + 1), :],
                   x1sb[:, D * jx1:D * (jx1 + 1)],
                   waits=[(s_ev, E.val(s_ev))], inc=s_st, incby=16)
            rx1.release(ix1, s_st, E.val(s_st))

    if debug:
        for nm, src_ap in (("d_attnb", attnb[:]), ("d_dsc", dsc[:]), ("d_rdsc", rdsc[:]),
                           ("d_var64", var64[:]), ("d_nm64", nm64[:]), ("d_ypre", ypre[:]),
                           ("d_rs64", rs64[:]), ("d_mu64", mu64[:]),
                           ("d_alphaT", alphaT[:]), ("d_siT", siT[:]), ("d_rdcol", rdcol[:]),
                           ("d_gib", gib[:]), ("d_ghb", ghb[:]), ("d_snew", snew[:]),
                           ("d_svb", svb[:]), ("d_esc", esc[:]), ("d_vb", vb[:]), ("d_ycur", ycur[:]),
                           ("d_midT", midT[:])):
            E.emit("sync", "dma_start", dbg[nm], src_ap, waits=[(s_ev, E.val(s_ev))],
                   inc=s_st, incby=16)
    E.wait("sync", s_st)


def build_l1_attn(T, debug=False):
    S = T // BPC
    W = 16
    nc = new_nc()
    ext = {"xn_fm": nc.dram_tensor("xn_fm", [D, T], BF16, kind="ExternalInput").ap(),
           "xn": nc.dram_tensor("xn", [T, D], BF16, kind="ExternalInput").ap(),
           "gamma": nc.dram_tensor("gamma", [D], F32, kind="ExternalInput").ap(),
           "beta": nc.dram_tensor("beta", [D], F32, kind="ExternalInput").ap(),
           "bandm": nc.dram_tensor("bandm", [128, 128], F32, kind="ExternalInput").ap(),
           "topm": nc.dram_tensor("topm", [15, 128], F32, kind="ExternalInput").ap(),
           "mask0": nc.dram_tensor("mask0", [128, W * 128], F32, kind="ExternalInput").ap(),
           "attn1": nc.dram_tensor("attn1", [T, D], BF16, kind="ExternalOutput").ap()}
    for n in ("WqT", "WkT", "WvT", "WkpT", "WoT"):
        ext["l1a_" + n] = nc.dram_tensor(n, [D, D], F32, kind="ExternalInput").ap()
    dbg = {}
    if debug:
        for nm, shp, dt in (("d_zt", [128, 8 * (T // BPC)], BF16),
                            ("d_expb", [128, 2 * 2048], BF16),
                            ("d_exptop", [15, 2 * 2048], BF16),
                            ("d_tmpc", [128, 2048], BF16),
                            ("d_ztt", [128, 128], F32),
                            ("d_ra", [1, 128], BF16), ("d_rb", [1, 128], BF16),
                            ("d_xb", [128, 2048], BF16)):
            dbg[nm] = nc.dram_tensor(nm, shp, dt, kind="ExternalOutput").ap()
    E = Em(nc)
    emit_l1_attn(nc, E, T, ext, debug=debug, dbg=dbg)
    E.finalize()
    return nc


def emit_l1_attn(nc, E, T, ext, ns="", salloc=None, palloc=None, debug=False, dbg=None):
    S = T // BPC
    NB = S // 128
    W = 16
    xnfm_in = ext["xn_fm"]; xn_in = ext["xn"]
    wts = {n: ext["l1a_" + n] for n in ("WqT", "WkT", "WvT", "WkpT", "WoT")}
    gamma_in = ext["gamma"]; beta_in = ext["beta"]
    bandm_in = ext["bandm"]; topm_in = ext["topm"]; mask0_in = ext["mask0"]
    attn1_out = ext["attn1"]
    sbA = salloc or (lambda n, shp, dt=F32: nc.alloc_sbuf_tensor(n, shp, dt).ap())
    paA = palloc or (lambda n, shp, dt=F32: nc.alloc_psum_tensor(n, shp, dt).ap())
    sb = lambda n, shp, dt=F32: sbA(ns + n, shp, dt)

    HH = 4
    SP = S + 32
    qt = sb("qt", [128, HH * S], BF16)
    kpad = sb("kpad", [128, HH * SP], BF16)
    kppad = sb("kppad", [128, HH * SP], BF16)
    vtpad = sb("vtpad", [128, HH * SP], BF16)
    vw = sb("vw", [128, 17 * 512], BF16)
    zt = sb("zt", [128, 8 * S], BF16)
    xnfm = sb("xnfm_l1", [128, 8 * S], BF16)
    xb = sb("xb", [128, W * 128], BF16)
    expb = sb("expb", [128, 2 * W * 128], BF16)
    exptop = sb("exptop", [15, 2 * W * 128], BF16)
    tmpc = sb("tmpc", [128, W * 128], BF16)
    ztt = sb("ztt", [128, 128])
    rbc_sb = sb("rbc_sb", [128, 128])
    wbf = [sb(f"l1wb{j}", [128, 8 * 512], BF16) for j in range(2)]
    bandb = sb("bandb", [128, 128], BF16)
    topb = sb("topb", [15, 128], BF16)
    mask0b = sb("mask0b", [128, W * 128], BF16)
    negC5 = sb("negC5", [128, 1])
    onesc = sb("onesc_l1", [128, 1], BF16)
    selrow = sb("selrow", [1, 256], BF16)
    ra = sb("ra_l1", [1, 128], BF16); rbr = sb("rb_l1", [1, 128], BF16)
    gbc = sb("gbc", [128, D], BF16); bbc = sb("bbc", [128, D], BF16)
    grow = sb("grow", [1, D], BF16); brow = sb("brow", [1, D], BF16)
    onesr = sb("onesr_l1", [1, 128], BF16)
    xn1t = sb("xn1t", [128, 2 * D], BF16)
    otile = tmpc[:].bitcast(F32)
    oslot = sb("oslot", [128, D], BF16)
    vtmp = sb("vtmp", [128, 512], BF16)
    mu_ = sb("mu_l1", [128, 1]); musq_ = sb("musq_l1", [128, 1])
    sq_ = sb("sq_l1", [128, 1]); var_ = sb("var_l1", [128, 1])
    lnt_ = sb("lnt_l1", [128, 1]); rs_ = sb("rs_l1", [128, 1])
    junk_ = xb[:].bitcast(F32)

    PS0 = paA(ns + "l1_ps0", [128, 2048], F32)
    PS1 = paA(ns + "l1_ps1", [128, 2048], F32)

    _s = lambda n: E.sem(ns + n)
    s_w = _s("s_w"); s_ld = _s("s_ld")
    s_pe = _s("s_pe"); s_ev = _s("s_ev"); s_act = _s("s_act")
    s_hz = _s("s_hz"); s_st = _s("s_st")

    rxn1 = Rot(2); ros1 = Rot(1)

    def g_cast(dst, src, waits=()):
        E.emit("gpsimd", "dma_start", dst, src, waits=waits, inc=s_w, incby=16)
        return E.val(s_w)

    w_bm = g_cast(bandb[:], bandm_in)
    w_tm = g_cast(topb[:], topm_in)
    w_m0 = g_cast(mask0b[:], mask0_in)
    w_g = g_cast(grow[:], gamma_in.unsqueeze(0))
    w_b = g_cast(brow[:], beta_in.unsqueeze(0))
    E.emit("vector", "memset", negC5[:], -C_L1)
    E.emit("vector", "memset", onesc[:], 1.0)
    E.emit("vector", "memset", onesr[:], 1.0)
    E.emit("vector", "memset", selrow[:, 0:64], 1.0)
    E.emit("vector", "memset", selrow[:, 64:192], 0.0)
    E.emit("vector", "memset", selrow[:, 192:256], 1.0)
    for buf in (kpad, kppad, vtpad):
        for cch in range(HH):
            E.emit("vector", "memset", buf[:, SP * cch: SP * cch + 15], 0.0)
            E.emit("vector", "memset", buf[:, SP * cch + 15 + S: SP * (cch + 1)], 0.0)
    E.emit("vector", "memset", vw[:, 16 * 512:17 * 512], 0.0, inc=s_ev)
    prep_ev = E.val(s_ev)
    for half in range(2):
        E.emit("tensor", "matmul", PS0[:, 512 * half:512 * (half + 1)],
               onesr[:], grow[:, 512 * half:512 * (half + 1)],
               start=True, stop=True,
               waits=[(s_w, w_g), (s_ev, prep_ev)], inc=s_pe)
        E.emit("tensor", "matmul", PS1[:, 512 * half:512 * (half + 1)],
               onesr[:], brow[:, 512 * half:512 * (half + 1)],
               start=True, stop=True, waits=[(s_w, w_b)], inc=s_pe)
        E.emit("vector", "tensor_copy", gbc[:, 512 * half:512 * (half + 1)],
               PS0[:, 512 * half:512 * (half + 1)], waits=[(s_pe, E.val(s_pe))])
        E.emit("vector", "tensor_copy", bbc[:, 512 * half:512 * (half + 1)],
               PS1[:, 512 * half:512 * (half + 1)], inc=s_ev)
    gb_ev = E.val(s_ev)

    PSP = [PS0[:, 512 * q:512 * (q + 1)] for q in range(4)]
    rps = Rot(4)
    qrowstride = qt.ap[0][0]
    kprowstride = kppad.ap[0][0]
    vtrowstride = vtpad.ap[0][0]
    vwrowstride = vw.ap[0][0]
    tmrowstride = tmpc.ap[0][0]

    st_count = 0
    for b in range(BPC):
        xoff = b * S
        for p in range(8):
            E.emit("sync", "dma_start", xnfm[:, S * p:S * (p + 1)],
                   xnfm_in[128 * p:128 * (p + 1), xoff:xoff + S],
                   waits=[(s_pe, E.val(s_pe))], inc=s_ld, incby=16)
        ld_x = E.val(s_ld)
        for half in range(2):
            coff = 4 * half

            def ldw(name, j):
                E.emit("gpsimd", "dma_start",
                       wbf[j][:].rearrange("p (c n) -> p c n", c=8),
                       wts[name][:, 512 * half:512 * (half + 1)]
                           .rearrange("(c p) n -> p c n", p=128),
                       waits=[(s_pe, E.val(s_pe))], inc=s_w, incby=16)
                return E.val(s_w)

            def proj_fm(dst, wj, w_rdy, dst_off, dst_stride):
                for m in range(HH):
                    for g in range(S // 512):
                        jj, bwp, itp = rps.acquire()
                        ps = PSP[jj]
                        for c in range(8):
                            E.emit("tensor", "matmul", ps,
                                   wbf[wj][:, 512 * c + 128 * m: 512 * c + 128 * (m + 1)],
                                   xnfm[:, S * c + 512 * g: S * c + 512 * (g + 1)],
                                   start=(c == 0), stop=(c == 7),
                                   waits=([(s_ld, ld_x), (s_w, w_rdy),
                                           (s_ev, E.val(s_ev))] + bwp) if c == 0 else (),
                                   inc=(s_pe if c == 7 else None))
                        E.emit("vector", "tensor_copy",
                               dst[:, dst_stride * m + dst_off + 512 * g:
                                   dst_stride * m + dst_off + 512 * (g + 1)],
                               ps, waits=[(s_pe, E.val(s_pe))], inc=s_ev)
                        rps.release(itp, s_ev, E.val(s_ev))

            wq_ld = ldw("WqT", 0)
            proj_fm(qt, 0, wq_ld, 0, S)
            wk_ld = ldw("WkT", 1)
            proj_fm(kpad, 1, wk_ld, 15, SP)
            wkp_ld = ldw("WkpT", 0)
            proj_fm(kppad, 0, wkp_ld, 15, SP)
            wv_ld = ldw("WvT", 1)
            proj_fm(vtpad, 1, wv_ld, 15, SP)
            for i in range(NB + 1):
                jj, bwp, itp = rps.acquire()
                ps = PSP[jj]
                lo = 128 * i - 15
                if i == 0:
                    m_sz, src_lo = 113, 0
                elif i == NB:
                    m_sz, src_lo = 15, S - 15
                else:
                    m_sz, src_lo = 128, lo
                for c in range(8):
                    E.emit("tensor", "matmul", ps[0:m_sz, :],
                           xnfm[:, S * c + src_lo: S * c + src_lo + m_sz],
                           wbf[1][:, 512 * c:512 * (c + 1)],
                           start=(c == 0), stop=(c == 7),
                           waits=([(s_w, wv_ld), (s_ev, E.val(s_ev))] + bwp) if c == 0 else (),
                           inc=(s_pe if c == 7 else None))
                if i == 0:
                    E.emit("vector", "tensor_copy", vtmp[0:113, :], ps[0:113, :],
                           waits=[(s_pe, E.val(s_pe))], inc=s_ev)
                    E.emit("gpsimd", "dma_start", vw[15:128, 0:512], vtmp[0:113, :],
                           waits=[(s_ev, E.val(s_ev))], inc=s_w, incby=16)
                    E.emit("vector", "memset", vw[0:15, 0:512], 0.0, inc=s_ev)
                else:
                    E.emit("vector", "tensor_copy", vw[0:m_sz, 512 * i:512 * i + 512],
                           ps[0:m_sz, :], waits=[(s_pe, E.val(s_pe))], inc=s_ev)
                rps.release(itp, s_ev, E.val(s_ev))
            vw_dma = E.val(s_w)
            proj_ev = E.val(s_ev)

            for m in range(HH):
                gch = coff + m
                for blk in range(NB):
                    s0 = 128 * blk
                    q_b = bass.AP(tensor=qt.tensor, offset=qt.offset + S * m + s0,
                                  ap=[[qrowstride, 128], [0, W], [1, 128]])
                    kp_b = bass.AP(tensor=kppad.tensor,
                                   offset=kppad.offset + SP * m + s0,
                                   ap=[[kprowstride, 128], [1, W], [1, 128]])
                    E.emit("vector", "tensor_tensor",
                           xb[:, 0:2048].rearrange("p (k t) -> p k t", k=W),
                           q_b, kp_b, ALU.mult,
                           waits=[(s_ev, proj_ev), (s_pe, E.val(s_pe))], inc=s_ev)
                    xb_ev = E.val(s_ev)
                    for hh2 in range(2):
                        roff = 64 * hh2
                        for q4 in range(4):
                            aw = ()
                            if q4 == 0:
                                aw = ([(s_ev, xb_ev)] if hh2 == 0
                                      else [(s_act, E.val(s_act))])
                            E.emit("tensor", "matmul", PS0[:, 512 * q4:512 * (q4 + 1)],
                                   kpad[roff:roff + 64, SP * m + s0: SP * m + s0 + 128],
                                   xb[roff:roff + 64, 512 * q4:512 * (q4 + 1)],
                                   start=True, stop=True,
                                   waits=aw)
                            E.emit("tensor", "matmul", PS1[0:15, 512 * q4:512 * (q4 + 1)],
                                   kpad[roff:roff + 64, SP * m + s0 + 128: SP * m + s0 + 143],
                                   xb[roff:roff + 64, 512 * q4:512 * (q4 + 1)],
                                   start=True, stop=True,
                                   inc=(s_pe if q4 == 3 else None))
                        E.emit("scalar", "activation", expb[:, 2048 * hh2:2048 * (hh2 + 1)],
                               PS0[:], AF.Exp, scale=0.125, bias=negC5[:],
                               waits=[(s_pe, E.val(s_pe)), (s_ev, prep_ev)])
                        E.emit("scalar", "activation",
                               exptop[:, 2048 * hh2:2048 * (hh2 + 1)],
                               PS1[0:15, :], AF.Exp, scale=0.125, bias=negC5[0:15, :],
                               inc=s_act)
                        mv = expb[:, 2048 * hh2:2048 * (hh2 + 1)]
                        if blk == 0:
                            E.emit("vector", "tensor_tensor", mv, mv, mask0b[:],
                                   ALU.mult, waits=[(s_act, E.val(s_act)), (s_w, w_m0)])
                        else:
                            E.emit("vector", "tensor_tensor",
                                   mv.rearrange("p (k t) -> p k t", k=W),
                                   mv.rearrange("p (k t) -> p k t", k=W),
                                   bandb[:].unsqueeze(1).to_broadcast([128, W, 128]),
                                   ALU.mult, waits=[(s_act, E.val(s_act)), (s_w, w_bm)])
                        tv = exptop[:, 2048 * hh2:2048 * (hh2 + 1)]
                        E.emit("vector", "tensor_tensor",
                               tv.rearrange("p (k t) -> p k t", k=W),
                               tv.rearrange("p (k t) -> p k t", k=W),
                               topb[:].unsqueeze(1).to_broadcast([15, W, 128]),
                               ALU.mult, waits=[(s_w, w_tm)], inc=s_ev)
                    mexp_ev = E.val(s_ev)
                    for hh2 in range(2):
                        dcol = 512 * hh2
                        for k in range(W):
                            E.emit("tensor", "matmul", PS0[0:1, dcol:dcol + 128],
                                   onesc[:], expb[:, 2048 * hh2 + 128 * k:
                                                  2048 * hh2 + 128 * (k + 1)],
                                   start=(k == 0), stop=False,
                                   waits=[(s_ev, mexp_ev)] if k == 0 and hh2 == 0 else ())
                            E.emit("tensor", "matmul", PS0[0:1, dcol:dcol + 128],
                                   onesc[0:15, :], exptop[0:15, 2048 * hh2 + 128 * k:
                                                          2048 * hh2 + 128 * (k + 1)],
                                   start=False, stop=(k == W - 1),
                                   inc=(s_pe if k == W - 1 else None))
                    d_pe = E.val(s_pe)
                    E.emit("vector", "reciprocal", ra[:], PS0[0:1, 0:128],
                           waits=[(s_pe, d_pe)])
                    E.emit("vector", "reciprocal", rbr[:], PS0[0:1, 512:640], inc=s_ev)
                    E.emit("tensor", "matmul", PS0[:, 1024:1152], selrow[:, 0:128],
                           ra[:], start=True, stop=False,
                           waits=[(s_ev, E.val(s_ev))])
                    E.emit("tensor", "matmul", PS0[:, 1024:1152], selrow[:, 128:256],
                           rbr[:], start=False, stop=True, inc=s_pe)
                    E.emit("vector", "tensor_copy", rbc_sb[:], PS0[:, 1024:1152],
                           waits=[(s_pe, E.val(s_pe))], inc=s_ev)
                    dr_ev = E.val(s_ev)
                    for hh2 in range(2):
                        roff = 64 * hh2
                        WPS = PS0 if hh2 == 0 else PS1
                        vcol = 64 * (2 * m + hh2)
                        vap_m = bass.AP(tensor=vw.tensor,
                                        offset=vw.offset + 512 * blk + vcol,
                                        ap=[[vwrowstride, 128], [1, 64]])
                        vap_t = bass.AP(tensor=vw.tensor,
                                        offset=vw.offset + 512 * (blk + 1) + vcol,
                                        ap=[[vwrowstride, 15], [1, 64]])
                        for q4 in range(4):
                            E.emit("tensor", "matmul",
                                   WPS[roff:roff + 64, 512 * q4:512 * (q4 + 1)],
                                   vap_m, expb[:, 2048 * hh2 + 512 * q4:
                                               2048 * hh2 + 512 * (q4 + 1)],
                                   start=True, stop=False,
                                   waits=[(s_ev, dr_ev), (s_w, vw_dma)]
                                         if q4 == 0 and hh2 == 0 else ())
                            E.emit("tensor", "matmul",
                                   WPS[roff:roff + 64, 512 * q4:512 * (q4 + 1)],
                                   vap_t, exptop[0:15, 2048 * hh2 + 512 * q4:
                                                 2048 * hh2 + 512 * (q4 + 1)],
                                   start=False, stop=True,
                                   inc=(s_pe if q4 == 3 and hh2 == 1 else None))
                    wd_pe = E.val(s_pe)
                    for hh2 in range(2):
                        roff = 64 * hh2
                        WPS = PS0 if hh2 == 0 else PS1
                        vt_b = bass.AP(tensor=vtpad.tensor,
                                       offset=vtpad.offset + vtrowstride * roff
                                              + SP * m + s0,
                                       ap=[[vtrowstride, 64], [1, W], [1, 128]])
                        E.emit("vector", "tensor_tensor",
                               tmpc[roff:roff + 64, :].rearrange("p (k t) -> p k t", k=W),
                               WPS[roff:roff + 64, :].rearrange("p (k t) -> p k t", k=W),
                               vt_b, ALU.mult,
                               waits=[(s_pe, wd_pe)] if hh2 == 0 else ())
                    red_in = bass.AP(tensor=tmpc.tensor, offset=tmpc.offset,
                                     ap=[[tmrowstride, 128], [1, 128], [128, W]])
                    E.emit("vector", "tensor_reduce", ztt[:].unsqueeze(2),
                           red_in, AX.X, ALU.add)
                    E.emit("vector", "tensor_tensor",
                           zt[:, S * gch + s0: S * gch + s0 + 128],
                           ztt[:], rbc_sb[:], ALU.mult, inc=s_ev)
        z_done = E.val(s_ev)
        if debug and b == 0:
            for nm, src_ap in (("d_zt", zt[:]), ("d_expb", expb[:]),
                               ("d_exptop", exptop[:]), ("d_tmpc", tmpc[:]),
                               ("d_ztt", ztt[:]), ("d_ra", ra[:]), ("d_rb", rbr[:]),
                               ("d_xb", xb[:, 0:2048])):
                E.emit("sync", "dma_start", dbg[nm], src_ap,
                       waits=[(s_ev, z_done)], inc=s_st, incby=16)
        wo_ld = []
        for j in range(2):
            E.emit("gpsimd", "dma_start",
                   wbf[j][:].rearrange("p (c n) -> p c n", c=8),
                   wts["WoT"][:, 512 * j:512 * (j + 1)].rearrange("(c p) n -> p c n", p=128),
                   waits=[(s_pe, E.val(s_pe))], inc=s_w, incby=16)
            wo_ld.append(E.val(s_w))
        for i in range(NB):
            jx1, bwx1, ix1 = rxn1.acquire()
            E.emit("sync", "dma_start", xn1t[:, D * jx1:D * (jx1 + 1)],
                   xn_in[xoff + 128 * i: xoff + 128 * (i + 1), :],
                   waits=bwx1, inc=s_ld, incby=16)
            ld_t = E.val(s_ld)
            for half2 in range(2):
                jj, bwp, itp = rps.acquire()
                ps = PSP[jj]
                for c in range(8):
                    E.emit("tensor", "matmul", ps,
                           zt[:, S * c + 128 * i: S * c + 128 * (i + 1)],
                           wbf[half2][:, 512 * c:512 * (c + 1)],
                           start=(c == 0), stop=(c == 7),
                           waits=([(s_ev, z_done), (s_w, wo_ld[half2])] + bwp)
                                 if c == 0 else (),
                           inc=(s_pe if c == 7 else None))
                E.emit("vector", "tensor_tensor",
                       otile[:, 512 * half2:512 * (half2 + 1)], ps,
                       xn1t[:, D * jx1 + 512 * half2: D * jx1 + 512 * (half2 + 1)],
                       ALU.add, waits=[(s_pe, E.val(s_pe)), (s_ld, ld_t)], inc=s_ev)
                rps.release(itp, s_ev, E.val(s_ev))
            rxn1.release(ix1, s_ev, E.val(s_ev))
            E.emit("vector", "reduce_sum", mu_[:], otile[:], AX.X)
            E.emit("vector", "tensor_scalar_mul", mu_[:], mu_[:], 1.0 / D)
            E.emit("vector", "tensor_tensor", musq_[:], mu_[:], mu_[:], ALU.mult)
            E.emit("vector", "tensor_scalar_add", musq_[:], musq_[:], -1e-5, inc=s_ev)
            E.emit("vector", "tensor_tensor", junk_[:], otile[:], otile[:], ALU.mult,
                   waits=[(s_ev, E.val(s_ev))])
            E.emit("vector", "reduce_sum", sq_[:], junk_[:], AX.X)
            E.emit("vector", "scalar_tensor_tensor", var_[:], sq_[:], 1.0 / D,
                   musq_[:], ALU.mult, ALU.subtract, inc=s_ev)
            E.emit("scalar", "activation", lnt_[:], var_[:], AF.Ln,
                   waits=[(s_ev, E.val(s_ev))])
            E.emit("scalar", "activation", rs_[:], lnt_[:], AF.Exp, scale=-0.5,
                   inc=s_act)
            jo, bwo, io = ros1.acquire()
            ov = oslot[:, D * jo:D * (jo + 1)]
            E.emit("vector", "tensor_scalar", otile[:], otile[:], mu_[:], rs_[:],
                   ALU.subtract, ALU.mult, waits=[(s_act, E.val(s_act))] + bwo)
            E.emit("vector", "tensor_tensor", otile[:], otile[:], gbc[:], ALU.mult,
                   waits=[(s_ev, gb_ev)])
            E.emit("vector", "tensor_tensor", ov, otile[:], bbc[:], ALU.add, inc=s_ev)
            E.emit("sync", "dma_start",
                   attn1_out[xoff + 128 * i: xoff + 128 * (i + 1), :], ov,
                   waits=[(s_ev, E.val(s_ev))], inc=s_st, incby=16)
            ros1.release(io, s_st, E.val(s_st))

    E.wait("sync", s_st)


def build_l1_ffn(T, debug=False):
    F = 2816
    E8 = 8
    S = T // BPC
    NTS = S // 128
    nc = new_nc()
    ext = {"xn_fm": nc.dram_tensor("xn_fm", [D, T], BF16, kind="ExternalInput").ap(),
           "x1": nc.dram_tensor("x1", [T, D], BF16, kind="ExternalInput").ap(),
           "attn1": nc.dram_tensor("attn1", [T, D], BF16, kind="ExternalInput").ap(),
           "router": nc.dram_tensor("router", [D, E8], F32, kind="ExternalInput").ap(),
           "w1e": nc.dram_tensor("w1e", [E8, D, F], F32, kind="ExternalInput").ap(),
           "w3e": nc.dram_tensor("w3e", [E8, D, F], F32, kind="ExternalInput").ap(),
           "w2e": nc.dram_tensor("w2e", [E8, F, D], F32, kind="ExternalInput").ap(),
           "ws1": nc.dram_tensor("ws1", [D, F], F32, kind="ExternalInput").ap(),
           "ws3": nc.dram_tensor("ws3", [D, F], F32, kind="ExternalInput").ap(),
           "ws2": nc.dram_tensor("ws2", [F, D], F32, kind="ExternalInput").ap(),
           "ident": nc.dram_tensor("ident", [128, 128], F32, kind="ExternalInput").ap(),
           "out": nc.dram_tensor("out", [T, D], F32, kind="ExternalOutput").ap()}
    if debug:
        ext["d_gates"] = nc.dram_tensor("d_gates", [BPC, 128, E8 * NTS], BF16,
                                        kind="ExternalOutput").ap()
        ext["d_acc0"] = nc.dram_tensor("d_acc0", [BPC, 128, NTS * D], BF16,
                                       kind="ExternalOutput").ap()
        ext["d_xg"] = nc.dram_tensor("d_xg", [BPC, E8, 128, 8 * S], BF16,
                                     kind="ExternalOutput").ap()
    E = Em(nc)
    emit_l1_ffn(nc, E, T, ext, debug=debug)
    E.finalize()
    return nc


def emit_l1_ffn(nc, E, T, ext, ns="", salloc=None, palloc=None, debug=False):
    S = T // BPC
    NG = S // 512
    NTS = S // 128
    E8 = 8
    F = 2816
    NFC = F // 128
    xnfm_in = ext["xn_fm"]; x1_in = ext["x1"]; attn1_in = ext["attn1"]
    router_in = ext["router"]
    w1_in = ext["w1e"]; w3_in = ext["w3e"]; w2_in = ext["w2e"]
    ws1_in = ext["ws1"]; ws3_in = ext["ws3"]; ws2_in = ext["ws2"]
    ident_in = ext["ident"]
    out_out = ext["out"]
    sbA = salloc or (lambda n, shp, dt=F32: nc.alloc_sbuf_tensor(n, shp, dt).ap())
    paA = palloc or (lambda n, shp, dt=F32: nc.alloc_psum_tensor(n, shp, dt).ap())
    sb = lambda n, shp, dt=F32: sbA(ns + n, shp, dt)

    FH = NFC // 2
    xnfm = sb("xnfm_f1", [128, 8 * S], BF16)
    xg = sb("xg_f1", [128, 8 * S], BF16)
    mid = sb("mid_f1", [128, FH * S], BF16)
    acc = sb("acc_f1", [128, NTS * D], BF16)
    w13 = sb("w13_f1", [128, 2 * 16 * 128], BF16)
    w2b = sb("w2b_f1", [128, FH * 1024], BF16)
    routerb = sb("router_f1", [128, 8 * E8], BF16)
    identb = sb("ident_f1", [128, 128], BF16)
    scs = sb("scs_f1", [128, E8])
    mx1 = sb("mx1_f1", [128, 1])
    msk8 = sb("msk8_f1", [128, E8])
    sig8 = sb("sig8_f1", [128, E8])
    gates = sb("gates_f1", [128, E8], BF16)
    gatesall = sb("gatesall_f1", [128, E8 * (SEQ // 128)], BF16)
    growsm = sb("growsm_f1", [1, 512], BF16)
    onesrow = sb("onesrow_f1", [1, 512], BF16)
    gbcs = sb("gbcs_f1", [128, 512], BF16)
    silub = sb("silu_f1", [128, 512], BF16)
    t2tile = sb("t2_f1", [128, 2 * D], BF16)
    outf = sb("outf_f1", [128, 2 * D])
    t3tile = outf[:].bitcast(BF16)[:, 0:2 * D]

    ps_u = paA(ns + "f1_psu", [128, 512], F32)
    ps_v = paA(ns + "f1_psv", [128, 512], F32)
    ps_g = paA(ns + "f1_psg", [128, 512], F32)
    ps_w2 = [paA(ns + f"f1_psw{j}", [128, 512], F32) for j in range(4)]
    ps_tr = paA(ns + "f1_pstr", [128, 128], BF16)

    _s = lambda n: E.sem(ns + n)
    s_w = _s("s_w"); s_ld = _s("s_ld")
    s_pe = _s("s_pe"); s_ev = _s("s_ev"); s_act = _s("s_act")
    s_st = _s("s_st"); s_hz = _s("s_hz")

    rt2 = Rot(2); rout = Rot(2)

    def g_cast(dst, src, waits=()):
        E.emit("gpsimd", "dma_start", dst, src, waits=waits, inc=s_w, incby=16)
        return E.val(s_w)

    w_id = g_cast(identb[:], ident_in)
    E.emit("vector", "memset", onesrow[:], 1.0)
    E.emit("gpsimd", "dma_start", routerb[:].rearrange("p (c n) -> p c n", c=8),
           router_in.rearrange("(c p) n -> p c n", p=128), inc=s_w, incby=16)
    w_rt = E.val(s_w)

    prev_store_ev = 0
    w2_read_pe = 0
    bank_ev = {q: 0 for q in range(4)}
    for b in range(BPC):
        xoff = b * S
        for p in range(8):
            E.emit("sync", "dma_start", xnfm[:, S * p:S * (p + 1)],
                   xnfm_in[128 * p:128 * (p + 1), xoff:xoff + S],
                   waits=[(s_pe, E.val(s_pe))], inc=s_ld, incby=16)
        ld_x = E.val(s_ld)
        for i in range(NTS):
            jt, bwt, it = rt2.acquire()
            E.emit("sync", "dma_start", t2tile[:, D * jt:D * (jt + 1)],
                   x1_in[xoff + 128 * i: xoff + 128 * (i + 1), :],
                   waits=bwt, inc=s_ld, incby=16)
            E.emit("sync", "dma_start", t3tile[:, D * jt:D * (jt + 1)],
                   attn1_in[xoff + 128 * i: xoff + 128 * (i + 1), :],
                   inc=s_ld, incby=16)
            E.emit("vector", "tensor_tensor", acc[:, D * i:D * (i + 1)],
                   t3tile[:, D * jt:D * (jt + 1)], t2tile[:, D * jt:D * (jt + 1)],
                   ALU.add,
                   waits=[(s_ld, E.val(s_ld)), (s_ev, prev_store_ev)], inc=s_ev)
            rt2.release(it, s_ev, E.val(s_ev))
        for i in range(NTS):
            for c in range(8):
                E.emit("tensor", "matmul", ps_g[:, 0:E8],
                       xnfm[:, S * c + 128 * i: S * c + 128 * (i + 1)],
                       routerb[:, E8 * c:E8 * (c + 1)],
                       start=(c == 0), stop=(c == 7),
                       waits=[(s_ld, ld_x), (s_w, w_rt),
                              (s_ev, E.val(s_ev))] if c == 0 else (),
                       inc=(s_pe if c == 7 else None))
            E.emit("vector", "tensor_copy", scs[:], ps_g[:, 0:E8],
                   waits=[(s_pe, E.val(s_pe))])
            E.emit("vector", "tensor_reduce", mx1[:], scs[:], AX.X, ALU.max, inc=s_hz)
            E.emit("scalar", "activation", sig8[:], scs[:], AF.Sigmoid,
                   waits=[(s_hz, E.val(s_hz))], inc=s_act)
            E.emit("vector", "tensor_scalar", msk8[:], scs[:], mx1[:], None, ALU.is_ge,
                   waits=[(s_act, E.val(s_act)), (s_hz, E.val(s_hz))])
            E.emit("vector", "tensor_tensor",
                   gatesall[:, E8 * i:E8 * (i + 1)], sig8[:], msk8[:], ALU.mult,
                   inc=s_ev)
        gates_ev = E.val(s_ev)
        dbg_acc_st = 0
        if debug:
            E.emit("sync", "dma_start", ext["d_gates"][b], gatesall[:],
                   waits=[(s_ev, gates_ev)], inc=s_st, incby=16)
            E.emit("sync", "dma_start", ext["d_acc0"][b], acc[:],
                   waits=[(s_ev, gates_ev)], inc=s_st, incby=16)
            dbg_acc_st = E.val(s_st)

        for se in range(9):
            routed = se < E8
            if routed:
                for g4 in range(NG):
                    for sub in range(4):
                        ti = 4 * g4 + sub
                        E.emit("tensor", "matmul", ps_w2[sub][0:1, 0:128],
                               gatesall[:, E8 * ti + se:E8 * ti + se + 1], identb[:],
                               start=True, stop=True,
                               waits=[(s_ev, E.val(s_ev)), (s_pe, E.val(s_pe)),
                                      (s_w, w_id)] if sub == 0 else (),
                               inc=(s_pe if sub == 3 else None))
                    E.emit("vector", "tensor_copy", growsm[:, 0:128],
                           ps_w2[0][0:1, 0:128], waits=[(s_pe, E.val(s_pe))])
                    E.emit("vector", "tensor_copy", growsm[:, 128:256],
                           ps_w2[1][0:1, 0:128])
                    E.emit("vector", "tensor_copy", growsm[:, 256:384],
                           ps_w2[2][0:1, 0:128])
                    E.emit("vector", "tensor_copy", growsm[:, 384:512],
                           ps_w2[3][0:1, 0:128], inc=s_ev)
                    E.emit("tensor", "matmul", ps_g[:],
                           onesrow[:, 0:128], growsm[:],
                           start=True, stop=True,
                           waits=[(s_ev, E.val(s_ev))], inc=s_pe)
                    E.emit("vector", "tensor_copy", gbcs[:], ps_g[:],
                           waits=[(s_pe, E.val(s_pe))])
                    for c in range(8):
                        E.emit("vector", "tensor_tensor",
                               xg[:, S * c + 512 * g4: S * c + 512 * (g4 + 1)],
                               xnfm[:, S * c + 512 * g4: S * c + 512 * (g4 + 1)],
                               gbcs[:], ALU.mult,
                               waits=([(s_st, E.val(s_st))]
                                      if (debug and c == 0 and g4 == 0) else ()),
                               inc=(s_ev if c == 7 else None))
                xg_ev = E.val(s_ev)
                src = xg
                if debug:
                    E.emit("sync", "dma_start", ext["d_xg"][b, se], xg[:],
                           waits=[(s_ev, xg_ev)], inc=s_st, incby=16)
            else:
                xg_ev = E.val(s_ev)
                src = xnfm
            for fhalf in range(2):
                fcpe = {}
                for fc in range(FH):
                    gfc = fhalf * FH + fc
                    j13 = (fc % 2) * 16 * 128
                    wv13 = ([(s_pe, fcpe[fc - 2])] if fc >= 2
                            else [(s_pe, E.val(s_pe))])
                    E.emit("gpsimd", "dma_start",
                           w13[:, j13:j13 + 8 * 128].rearrange("p (c n) -> p c n", c=8),
                           (w1_in[se] if routed else ws1_in)[:, 128 * gfc:128 * (gfc + 1)]
                               .rearrange("(c p) n -> p c n", p=128),
                           waits=wv13, inc=s_w, incby=16)
                    E.emit("gpsimd", "dma_start",
                           w13[:, j13 + 8 * 128:j13 + 16 * 128]
                               .rearrange("p (c n) -> p c n", c=8),
                           (w3_in[se] if routed else ws3_in)[:, 128 * gfc:128 * (gfc + 1)]
                               .rearrange("(c p) n -> p c n", p=128),
                           inc=s_w, incby=16)
                    w13_ld = E.val(s_w)
                    E.emit("gpsimd", "dma_start", w2b[:, 1024 * fc:1024 * (fc + 1)],
                           (w2_in[se] if routed else ws2_in)
                               [128 * gfc:128 * (gfc + 1), :],
                           waits=([(s_pe, w2_read_pe)] if fc == 0 else ()),
                           inc=s_w, incby=16)
                    for g4 in range(NG):
                        for c in range(8):
                            E.emit("tensor", "matmul", ps_u[:],
                                   w13[:, j13 + 128 * c:j13 + 128 * (c + 1)],
                                   src[:, S * c + 512 * g4: S * c + 512 * (g4 + 1)],
                                   start=(c == 0), stop=(c == 7),
                                   waits=[(s_w, w13_ld), (s_ev, xg_ev),
                                          (s_act, E.val(s_act)),
                                          (s_ev, E.val(s_ev))] if c == 0 else (),
                                   inc=(s_pe if c == 7 else None))
                        E.emit("scalar", "activation", silub[:], ps_u[:], AF.Silu,
                               waits=[(s_pe, E.val(s_pe))], inc=s_act)
                        for c in range(8):
                            E.emit("tensor", "matmul", ps_v[:],
                                   w13[:, j13 + 8 * 128 + 128 * c:
                                       j13 + 8 * 128 + 128 * (c + 1)],
                                   src[:, S * c + 512 * g4: S * c + 512 * (g4 + 1)],
                                   start=(c == 0), stop=(c == 7),
                                   waits=[(s_ev, E.val(s_ev))] if c == 0 else (),
                                   inc=(s_pe if c == 7 else None))
                        E.emit("vector", "tensor_tensor",
                               mid[:, S * fc + 512 * g4: S * fc + 512 * (g4 + 1)],
                               ps_v[:], silub[:], ALU.mult,
                               waits=[(s_pe, E.val(s_pe)), (s_act, E.val(s_act))]
                                     + ([(s_pe, w2_read_pe)]
                                        if fc == 0 and g4 == 0 else []),
                               inc=s_ev)
                    fcpe[fc] = E.val(s_pe)
                w2_ld = E.val(s_w)
                mid_ev = E.val(s_ev)
                for half in range(2):
                    for ctg in range(NTS // 4):
                        for fc in range(FH):
                            for q in range(4):
                                ct = 4 * ctg + q
                                wv2 = []
                                if fc == 0:
                                    wv2 = [(s_ev, bank_ev[q])]
                                    if q == 0 and half == 0 and ctg == 0:
                                        wv2 += [(s_w, w2_ld), (s_ev, mid_ev)]
                                E.emit("tensor", "matmul", ps_w2[q][:],
                                       mid[:, S * fc + 128 * ct: S * fc + 128 * (ct + 1)],
                                       w2b[:, 1024 * fc + 512 * half:
                                           1024 * fc + 512 * half + 512],
                                       start=(fc == 0), stop=(fc == FH - 1),
                                       waits=wv2,
                                       inc=(s_pe if q == 3 and fc == FH - 1 else None))
                        w2_pe = E.val(s_pe)
                        for q in range(4):
                            ct = 4 * ctg + q
                            av = acc[:, D * ct + 512 * half: D * ct + 512 * (half + 1)]
                            E.emit("vector", "tensor_tensor", av, av, ps_w2[q][:],
                                   ALU.add,
                                   waits=[(s_pe, w2_pe)]
                                         + ([(s_st, dbg_acc_st)]
                                            if (debug and se == 0 and fhalf == 0
                                                and half == 0 and ctg == 0) else []),
                                   inc=s_ev)
                            bank_ev[q] = E.val(s_ev)
                w2_read_pe = E.val(s_pe)
        fin_ev = E.val(s_ev)
        for i in range(NTS):
            jo, bwo, io = rout.acquire()
            E.emit("vector", "tensor_copy", outf[:, D * jo:D * (jo + 1)],
                   acc[:, D * i:D * (i + 1)], waits=[(s_ev, fin_ev)] + bwo, inc=s_ev)
            E.emit("sync", "dma_start", out_out[xoff + 128 * i: xoff + 128 * (i + 1), :],
                   outf[:, D * jo:D * (jo + 1)], waits=[(s_ev, E.val(s_ev))],
                   inc=s_st, incby=16)
            rout.release(io, s_st, E.val(s_st))
        prev_store_ev = E.val(s_ev)

    E.wait("sync", s_st)


def host_consts():
    c = {}
    c["ident"] = np.eye(128, dtype=np.float32)
    perm = np.zeros((128, 128), np.float32)
    for m in range(128):
        base = (m // 64) * 64
        r = m % 64
        rho = base + ((r + 32) % 64)
        perm[rho, m] = 1.0
    c["perm"] = perm
    selb = np.zeros((1, 256), np.float32)
    selb[0, 0:64] = 1.0
    selb[0, 128 + 64:256] = 1.0
    c["selb"] = selb
    dm = np.triu(np.ones((128, 128), np.float32))
    c["dmask"] = dm
    t = np.arange(SEQ, dtype=np.float64)
    inv_freq = 1.0 / (10000.0 ** (np.arange(0, DH, 2).astype(np.float64) / DH))
    r = np.arange(128)
    d_in_head = r % 64
    freq_idx = d_in_head % 32
    ang = np.outer(inv_freq[freq_idx], t)
    c["rope_cos"] = np.cos(ang).astype(np.float32)
    sgn = np.where(d_in_head < 32, -1.0, 1.0)[:, None]
    c["rope_ssin"] = (np.sin(ang) * sgn).astype(np.float32)
    bd = np.zeros((128, 128), np.float32)
    bd[:64, :64] = 1.0; bd[64:, 64:] = 1.0
    c["bdmask"] = bd
    ml, tl = np.arange(128)[:, None], np.arange(128)[None, :]
    c["bandm"] = ((ml >= tl) & (ml <= tl + 15)).astype(np.float32)
    rr = np.arange(15)[:, None]
    c["topm"] = (tl >= 113 + rr).astype(np.float32)
    kk = np.arange(16)[:, None, None]
    m0 = ((ml[None] >= tl[None]) & (ml[None] <= tl[None] + 15)
          & (ml[None] >= 15) & ((tl[None] + kk) >= 15)).astype(np.float32)
    c["mask0"] = np.ascontiguousarray(m0.transpose(1, 0, 2).reshape(128, 16 * 128))
    return c


def build_fused(T):
    import contextlib
    NT = T // 128
    NSL = 64
    F = 2816
    E8 = 8
    W = 16
    S = T // BPC
    nc = new_nc()
    E = Em(nc)
    din = lambda n, shp, dt=F32: nc.dram_tensor(n, shp, dt, kind="ExternalInput").ap()
    mid = lambda n, shp, dt: nc.dram_tensor(n, shp, dt, kind="Internal").ap()

    x_in = din("x", [T, D])
    ident = din("ident", [128, 128])
    consts = {"perm": din("perm", [128, 128]), "selb": din("selb", [1, 256]),
              "dmask": din("dmask", [128, 128]),
              "rope_cos": din("rope_cos", [128, S]),
              "rope_ssin": din("rope_ssin", [128, S]),
              "bdmask": din("bdmask", [128, 128]),
              "bandm": din("bandm", [128, 128]), "topm": din("topm", [15, 128]),
              "mask0": din("mask0", [128, W * 128])}
    l0aw = {n: din("l0" + n, [D, D]) for n in ("WqT", "WkT", "WvT", "WoT")}
    l1aw = {n: din("l1" + n, [D, D]) for n in ("WqT", "WkT", "WvT", "WkpT", "WoT")}
    l0f = {"slot_init": din("slot_init", [NSL, D]), "w1": din("w1", [2 * D, 512]),
           "b1": din("b1", [512]), "w2": din("w2", [512, NSL]), "b2": din("b2", [NSL]),
           "tau": din("tau", [1]), "WihT": din("WihT", [D, 3 * D]),
           "WhhT": din("WhhT", [D, 3 * D]), "bih": din("bih", [3 * D]),
           "bhh": din("bhh", [3 * D]), "wqs": din("wqs", [D, 256]),
           "bqs": din("bqs", [256]), "wks": din("wks", [D, 256]),
           "bks": din("bks", [256]), "wvs": din("wvs", [D, D]), "bvs": din("bvs", [D]),
           "lng": din("lng", [D]), "lnb": din("lnb", [D]), "wvp": din("wvp", [D, D]),
           "bvp": din("bvp", [D]), "wo": din("wo", [D, D]), "bo": din("bo", [D])}
    l1f = {"router": din("router", [D, E8]), "w1e": din("w1e", [E8, D, F]),
           "w3e": din("w3e", [E8, D, F]), "w2e": din("w2e", [E8, F, D]),
           "ws1": din("ws1", [D, F]), "ws3": din("ws3", [D, F]),
           "ws2": din("ws2", [F, D])}
    gamma = din("gamma", [D]); beta = din("beta", [D])
    out_ap = nc.dram_tensor("out", [T, D], F32, kind="ExternalOutput").ap()

    xn = mid("i_xn", [T, D], BF16)
    xnfm = mid("i_xnfm", [D, T], BF16)
    stats = mid("i_stats", [128, 2 * NT], F32)
    tmp1 = mid("i_tmp1", [T, D], BF16)
    x1 = mid("i_x1", [T, D], BF16)
    xn1 = mid("i_xn1", [T, D], BF16)
    xn1fm = mid("i_xn1fm", [D, T], BF16)
    stats1 = mid("i_stats1", [128, 2 * NT], F32)
    attn1 = mid("i_attn1", [T, D], BF16)

    def phase(ns, fn):
        with contextlib.ExitStack() as st:
            salloc = lambda n, shp, dt=F32: st.enter_context(
                nc.sbuf_tensor(n, shp, dt)).ap()
            palloc = lambda n, shp, dt=F32: st.enter_context(
                nc.psum_tensor(n, shp, dt)).ap()
            fn(salloc, palloc)
        E.barrier()

    phase("p0", lambda sa, pa: emit_ln(nc, E, T,
          {"x": x_in, "ident": ident, "xn": xn, "xn_fm": xnfm, "stats": stats},
          ns="p0", salloc=sa, palloc=pa))
    e1 = {"xn_fm": xnfm, "x": x_in, "tmp1": tmp1, "perm": consts["perm"],
          "selb": consts["selb"], "dmask": consts["dmask"],
          "rope_cos": consts["rope_cos"], "rope_ssin": consts["rope_ssin"]}
    e1.update({"l0a_" + n: l0aw[n] for n in l0aw})
    phase("p1", lambda sa, pa: emit_l0_attn(nc, E, T, e1, ns="p1", salloc=sa, palloc=pa))
    e2 = {"xraw": x_in, "stats": stats, "xn": xn, "tmp1": tmp1, "x1": x1,
          "ident": ident, "bdmask": consts["bdmask"]}
    e2.update(l0f)
    phase("p2", lambda sa, pa: emit_l0_ffn(nc, E, T, e2, ns="p2", salloc=sa, palloc=pa))
    phase("p3", lambda sa, pa: emit_ln(nc, E, T,
          {"x": x1, "ident": ident, "xn": xn1, "xn_fm": xn1fm, "stats": stats1},
          ns="p3", salloc=sa, palloc=pa))
    e4 = {"xn_fm": xn1fm, "xn": xn1, "gamma": gamma, "beta": beta,
          "bandm": consts["bandm"], "topm": consts["topm"],
          "mask0": consts["mask0"], "attn1": attn1}
    e4.update({"l1a_" + n: l1aw[n] for n in l1aw})
    phase("p4", lambda sa, pa: emit_l1_attn(nc, E, T, e4, ns="p4", salloc=sa, palloc=pa))
    e5 = {"xn_fm": xn1fm, "x1": x1, "attn1": attn1, "ident": ident, "out": out_ap}
    e5.update(l1f)
    phase("p5", lambda sa, pa: emit_l1_ffn(nc, E, T, e5, ns="p5", salloc=sa, palloc=pa))
    E.finalize()
    return nc


def kernel(**inputs):
    T = BPC * SEQ
    C = lambda a: np.ascontiguousarray(np.asarray(a, dtype=np.float32))
    CT = lambda a: np.ascontiguousarray(np.asarray(a, dtype=np.float32).T)
    consts = host_consts()
    x_full = C(inputs["L0_attn_x"]).reshape(16, SEQ, D)
    xs = [np.ascontiguousarray(x_full[2 * i:2 * i + 2].reshape(T, D))
          for i in range(NCORES)]

    shared = {
        "ident": consts["ident"], "perm": consts["perm"], "selb": consts["selb"],
        "dmask": consts["dmask"], "rope_cos": consts["rope_cos"],
        "rope_ssin": consts["rope_ssin"], "bdmask": consts["bdmask"],
        "bandm": consts["bandm"], "topm": consts["topm"], "mask0": consts["mask0"],
        "l0WqT": CT(inputs["L0_attn_Wq"]), "l0WkT": CT(inputs["L0_attn_Wk"]),
        "l0WvT": CT(inputs["L0_attn_Wv"]), "l0WoT": CT(inputs["L0_attn_Wo"]),
        "l1WqT": CT(inputs["L1_attn_Wq"]), "l1WkT": CT(inputs["L1_attn_Wk"]),
        "l1WvT": CT(inputs["L1_attn_Wv"]), "l1WkpT": CT(inputs["L1_attn_Wkp"]),
        "l1WoT": CT(inputs["L1_attn_Wo"]),
        "slot_init": C(inputs["L0_ffn_slot_init"]), "w1": C(inputs["L0_ffn_w1"]),
        "b1": C(inputs["L0_ffn_b1"]), "w2": C(inputs["L0_ffn_w2"]),
        "b2": C(inputs["L0_ffn_b2"]), "tau": C(inputs["L0_ffn_tau"]),
        "WihT": CT(inputs["L0_ffn_Wih"]), "WhhT": CT(inputs["L0_ffn_Whh"]),
        "bih": C(inputs["L0_ffn_bih"]), "bhh": C(inputs["L0_ffn_bhh"]),
        "wqs": C(inputs["L0_ffn_wq"]), "bqs": C(inputs["L0_ffn_bq"]),
        "wks": C(inputs["L0_ffn_wk"]), "bks": C(inputs["L0_ffn_bk"]),
        "wvs": C(inputs["L0_ffn_wv"]), "bvs": C(inputs["L0_ffn_bv"]),
        "lng": C(inputs["L0_ffn_ln_g"]), "lnb": C(inputs["L0_ffn_ln_b"]),
        "wvp": C(inputs["L0_ffn_wvp"]), "bvp": C(inputs["L0_ffn_bvp"]),
        "wo": C(inputs["L0_ffn_wo"]), "bo": C(inputs["L0_ffn_bo"]),
        "gamma": C(inputs["L1_attn_gamma"]), "beta": C(inputs["L1_attn_beta"]),
        "router": C(inputs["L1_ffn_router_DE"]), "w1e": C(inputs["L1_ffn_w1_eDF"]),
        "w3e": C(inputs["L1_ffn_w3_eDF"]), "w2e": C(inputs["L1_ffn_w2_eFD"]),
        "ws1": C(inputs["L1_ffn_ws1_DF"]), "ws3": C(inputs["L1_ffn_ws3_DF"]),
        "ws2": C(inputs["L1_ffn_ws2_FD"]),
    }
    nc = build_fused(T)
    maps = [dict(x=xs[i], **shared) for i in range(NCORES)]
    res = run_bass_kernel_spmd(nc, maps, core_ids=list(range(NCORES))).results
    out = np.empty((16, SEQ, D), dtype=np.float32)
    for i in range(NCORES):
        out[2 * i:2 * i + 2] = np.asarray(res[i]["out"], np.float32).reshape(2, SEQ, D)
    return out
